# version 10
# baseline (speedup 1.0000x reference)
import contextlib
import numpy as np
import ml_dtypes
import concourse.bass as bass
import concourse.mybir as mybir
from concourse.bass_utils import run_bass_kernel_spmd

F32 = mybir.dt.float32
BF16 = mybir.dt.bfloat16
AF = mybir.ActivationFunctionType
ALU = mybir.AluOpType
AXL = mybir.AxisListType

D_MODEL = 1024
D_INNER = 2048
D_HALF = 1024
D_STATE = 16
DT_RANK = 64
D_CONV = 4
SEQ = 8192
BATCH = 4
NCORES = 8
TOK_PC = 4096

GELU_TANH_S = 0.7978845608028654
GELU_TANH_C = 0.044715
LN_EPS = 1e-5


def phase_pre(nc, st, d, sm, tag, TC=256):
    assert TC == 256
    NCH = SEQ // TC
    s_ld, s_ve, s_mm, s_ac, s_st = sm["ld"], sm["ve"], sm["mm"], sm["ac"], sm["st"]
    xT = d["xT"]
    w_in_x = d["w_in_x"]
    w_in_z = d["w_in_z"]
    wcv_h = d["wcv_d"]
    bcv_h = d["bcv_d"]
    wxp_h = d["wxp_d"]
    wdt_h = d["wdt_d"]
    bdt_h = d["bdt_d"]
    cst_h = d["cst_d"]
    xs_o = d["xs_d"]
    dt_o = d["dt_d"]
    bc_o = d["bc_d"]
    zs_o = d["zs_d"]
    wix = st.enter_context(nc.sbuf_tensor(tag + "wix", [128, 8 * D_INNER], BF16))
    wiz = st.enter_context(nc.sbuf_tensor(tag + "wiz", [128, 8 * D_HALF], BF16))
    wxp = st.enter_context(nc.sbuf_tensor(tag + "wxp", [128, 16 * 96], F32))
    wdt = st.enter_context(nc.sbuf_tensor(tag + "wdt", [64, D_HALF], BF16))
    wcv = st.enter_context(nc.sbuf_tensor(tag + "wcv", [128, 16 * 4], F32))
    bcv = st.enter_context(nc.sbuf_tensor(tag + "bcv", [128, 16], F32))
    bdt = st.enter_context(nc.sbuf_tensor(tag + "bdt", [128, 8], F32))
    cst = st.enter_context(nc.sbuf_tensor(tag + "cst", [128, 8], F32))
    xb16 = st.enter_context(nc.sbuf_tensor(tag + "xb16", [128, 2 * 8 * TC], BF16))
    xpad = st.enter_context(nc.sbuf_tensor(tag + "xpad", [128, 16 * (TC + 3)], F32))
    xcv = st.enter_context(nc.sbuf_tensor(tag + "xcv", [128, 16 * TC], F32))
    xsF = st.enter_context(nc.sbuf_tensor(tag + "xsF", [128, 16 * TC], F32))
    zs = st.enter_context(nc.sbuf_tensor(tag + "zs", [128, 8 * TC], F32))
    dbcf = st.enter_context(nc.sbuf_tensor(tag + "dbcf", [128, TC], F32))
    dtr16 = st.enter_context(nc.sbuf_tensor(tag + "dtr16", [64, TC], BF16))
    dtv = st.enter_context(nc.sbuf_tensor(tag + "dtv", [128, 8 * TC], F32))
    dtt = st.enter_context(nc.sbuf_tensor(tag + "dtt", [128, 8 * TC], F32))
    dts = st.enter_context(nc.sbuf_tensor(tag + "dts", [128, 8 * TC], F32))
    ps = [st.enter_context(nc.psum_tensor(f"{tag}ps{i}", [128, TC], F32)) for i in range(4)]
    psb = st.enter_context(nc.psum_tensor(tag + "psb", [128, TC], F32))
    psd = st.enter_context(nc.psum_tensor(tag + "psd", [128, TC], F32))

    xpad_v = xpad[:].rearrange("p (r t) -> p r t", r=16)
    xcv_v = xcv[:].rearrange("p (r t) -> p r t", r=16)
    xsF_v = xsF[:].rearrange("p (r t) -> p r t", r=16)
    xb16_v = xb16[:].rearrange("p (d k t) -> p d k t", d=2, k=8)
    wix_v = wix[:].rearrange("p (k m) -> p k m", k=8)
    wiz_v = wiz[:].rearrange("p (k m) -> p k m", k=8)
    wxp_v = wxp[:].rearrange("p (k m) -> p k m", k=16)
    dtv_v = dtv[:].rearrange("p (r t) -> p r t", r=8)
    zs_v = zs[:].rearrange("p (r t) -> p r t", r=8)

    MMC = 24 + 1 + 8
    ACC = 24 + 1 + 1 + 8 + 1
    VEC = 5
    NST = 4

    def em_g(g):
        g.dma_start(out=wix[:], in_=w_in_x).then_inc(s_ld, 16)
        g.dma_start(out=wiz[:], in_=w_in_z).then_inc(s_ld, 16)
        g.dma_start(out=wxp[:], in_=wxp_h).then_inc(s_ld, 16)
        g.dma_start(out=wdt[:], in_=wdt_h).then_inc(s_ld, 16)
        g.dma_start(out=wcv[:], in_=wcv_h).then_inc(s_ld, 16)
        g.dma_start(out=bcv[:], in_=bcv_h).then_inc(s_ld, 16)
        g.dma_start(out=bdt[:], in_=bdt_h).then_inc(s_ld, 16)
        g.dma_start(out=cst[:], in_=cst_h).then_inc(s_ld, 16)
        for c in range(NCH):
            if c >= 2:
                g.wait_ge(s_mm, MMC * (c - 2) + 24)
            g.dma_start(out=xb16[:, (c % 2) * 8 * TC:(c % 2 + 1) * 8 * TC],
                        in_=xT[c]).then_inc(s_ld, 16)

    def em_sy(sy):
        for c in range(NCH):
            sy.wait_ge(s_ac, ACC * c + 25)
            sy.dma_start(out=xs_o[:, c * TC:(c + 1) * TC].rearrange("(r p) t -> p r t", p=128),
                         in_=xsF_v[:, 0:8]).then_inc(s_st, 16)
            sy.wait_ge(s_ac, ACC * c + 24)
            sy.dma_start(out=zs_o[:, c * TC:(c + 1) * TC].rearrange("(r p) t -> p r t", p=128),
                         in_=zs_v).then_inc(s_st, 16)
            sy.wait_ge(s_ac, ACC * c + 26)
            sy.dma_start(out=bc_o[:, c * TC:(c + 1) * TC], in_=dbcf[0:96, :]).then_inc(s_st, 16)
            sy.wait_ge(s_ve, VEC * c + 5)
            sy.dma_start(out=dt_o[:, c * TC:(c + 1) * TC].rearrange("(r p) t -> p r t", p=128),
                         in_=dtv_v).then_inc(s_st, 16)
        sy.wait_ge(s_st, 16 * NST * NCH)

    def em_t(t):
        t.wait_ge(s_ld, 128)
        for c in range(NCH):
            t.wait_ge(s_ld, 128 + 16 * (c + 1))
            for m in range(24):
                tg = 24 * c + m
                if tg >= 4:
                    cp, mp = divmod(tg - 4, 24)
                    t.wait_ge(s_ac, ACC * cp + mp + 1)
                bank = ps[m % 4]
                for k in range(8):
                    lhs = wix_v[:, k, m * 128:(m + 1) * 128] if m < 16 else \
                        wiz_v[:, k, (m - 16) * 128:(m - 15) * 128]
                    mm = t.matmul(bank[:], lhs, xb16_v[:, c % 2, k], start=(k == 0), stop=(k == 7))
                mm.then_inc(s_mm, 1)
            t.wait_ge(s_ve, VEC * c + 3)
            if c >= 1:
                t.wait_ge(s_ac, ACC * (c - 1) + 26)
            for k in range(16):
                mm = t.matmul(psb[0:96, :], wxp_v[:, k, :], xsF_v[:, k], start=(k == 0), stop=(k == 15))
            mm.then_inc(s_mm, 1)
            t.wait_ge(s_ve, VEC * c + 4)
            for m in range(8):
                if c >= 1 or m >= 1:
                    t.wait_ge(s_ac, ACC * c + 26 + m if m >= 1 else ACC * (c - 1) + 34)
                t.matmul(psd[:], wdt[:, m * 128:(m + 1) * 128], dtr16[:],
                         start=True, stop=True).then_inc(s_mm, 1)

    def em_sc(sc):
        sc.wait_ge(s_ld, 128)
        for c in range(NCH):
            for m in range(24):
                sc.wait_ge(s_mm, MMC * c + m + 1)
                bank = ps[m % 4]
                if m < 16:
                    sc.wait_ge(s_ve, VEC * c + 1)
                    sc.copy(xpad_v[:, m, 3:3 + TC], bank[:]).then_inc(s_ac, 1)
                else:
                    if c >= 1 and m == 16:
                        sc.wait_ge(s_st, 16 * (NST * (c - 1) + 2))
                    sc.activation(zs_v[:, m - 16], bank[:], AF.Silu).then_inc(s_ac, 1)
            sc.wait_ge(s_ve, VEC * c + 2)
            if c >= 1:
                sc.wait_ge(s_st, 16 * (NST * (c - 1) + 1))
            sc.activation(xsF[:], xcv[:], AF.Silu).then_inc(s_ac, 1)
            sc.wait_ge(s_mm, MMC * c + 25)
            if c >= 1:
                sc.wait_ge(s_st, 16 * (NST * (c - 1) + 3))
            sc.copy(dbcf[0:96, :], psb[0:96, :]).then_inc(s_ac, 1)
            if c >= 1:
                sc.wait_ge(s_st, 16 * (NST * (c - 1) + 4))
            for m in range(8):
                sc.wait_ge(s_mm, MMC * c + 26 + m)
                sc.activation(dtv_v[:, m], psd[:], AF.Identity, bias=bdt[:, m:m + 1]).then_inc(s_ac, 1)
            sc.activation(dtt[:], dtv[:], AF.Abs)
            sc.activation(dts[:], dtt[:], AF.Exp, scale=cst[:, 2:3])
            sc.activation(dtt[:], dts[:], AF.Ln, bias=cst[:, 3:4]).then_inc(s_ac, 1)

    def em_v(v):
        v.wait_ge(s_ld, 128)
        for c in range(NCH):
            if c == 0:
                v.tensor_scalar(xpad_v[:, :, 0:3], cst[:, 0:1].broadcast_to([128, 16, 3]),
                                0.0, None, ALU.mult).then_inc(s_ve, 1)
            else:
                v.tensor_copy(xpad_v[:, :, 0:3], xpad_v[:, :, TC:TC + 3]).then_inc(s_ve, 1)
            v.wait_ge(s_ac, ACC * c + 16)
            last = None
            for r in range(16):
                v.scalar_tensor_tensor(xcv_v[:, r], xpad_v[:, r, 0:TC], wcv[:, r * 4:r * 4 + 1],
                                       bcv[:, r:r + 1].broadcast_to([128, TC]), ALU.mult, ALU.add)
                for k in range(1, 4):
                    last = v.scalar_tensor_tensor(xcv_v[:, r], xpad_v[:, r, k:k + TC],
                                                  wcv[:, r * 4 + k:r * 4 + k + 1],
                                                  xcv_v[:, r], ALU.mult, ALU.add)
            last.then_inc(s_ve, 1)
            v.wait_ge(s_ac, ACC * c + 25)
            v.nop().then_inc(s_ve, 1)
            v.wait_ge(s_ac, ACC * c + 26)
            v.tensor_copy(dtr16[:], dbcf[0:64, :]).then_inc(s_ve, 1)
            v.wait_ge(s_ac, ACC * c + 35)
            v.scalar_tensor_tensor(dtv[:], dtv[:], 0.0, dtt[:], ALU.max, ALU.add).then_inc(s_ve, 1)

    finals = {"ld": 16 * (8 + NCH), "mm": MMC * NCH, "ac": ACC * NCH,
              "ve": VEC * NCH, "st": 16 * NST * NCH}
    return {"g": em_g, "sy": em_sy, "t": em_t, "sc": em_sc, "v": em_v}, finals


def pack_kpm(W, k):
    W = np.asarray(W, np.float32)
    return np.ascontiguousarray(W.reshape(k, 128, -1).transpose(1, 0, 2).reshape(128, -1))


def host_pre_inputs(inputs, consts):
    x = np.asarray(inputs["L0_attn_x"])
    W_in = np.asarray(inputs["L0_attn_W_in"])
    W_conv = np.asarray(inputs["L0_attn_W_conv"])
    b_conv = np.asarray(inputs["L0_attn_b_conv"])
    W_xproj = np.asarray(inputs["L0_attn_W_xproj"])
    W_dt = np.asarray(inputs["L0_attn_W_dt"])
    b_dt = np.asarray(inputs["L0_attn_b_dt"])
    NCHA = SEQ // 256
    maps = []
    for core in range(NCORES):
        b, h = divmod(core, 2)
        own = np.arange(h * D_HALF, (h + 1) * D_HALF)
        oth = np.arange((1 - h) * D_HALF, (2 - h) * D_HALF)
        perm = np.concatenate([own, oth])
        wcv = np.empty((128, 16 * 4), np.float32)
        bcv = np.empty((128, 16), np.float32)
        for r in range(16):
            idx = perm[r * 128:(r + 1) * 128]
            for k in range(4):
                wcv[:, r * 4 + k] = W_conv[k, idx]
            bcv[:, r] = b_conv[idx]
        bdt = np.ascontiguousarray(b_dt[own].reshape(8, 128).T)
        xT = x[b].T
        xTP = np.ascontiguousarray(
            xT.reshape(8, 128, NCHA, 256).transpose(2, 1, 0, 3).reshape(NCHA, 128, 8 * 256))
        maps.append({
            "xT": xTP,
            "w_in_x": pack_kpm(W_in[:, perm], 8),
            "w_in_z": pack_kpm(W_in[:, D_INNER + own], 8),
            "wcv_d": wcv, "bcv_d": bcv,
            "wxp_d": pack_kpm(W_xproj[perm, :], 16),
            "wdt_d": np.ascontiguousarray(W_dt[:, own]),
            "bdt_d": bdt, "cst_d": consts,
        })
    return maps


def _consts_host():
    c = np.zeros((128, 8), np.float32)
    c[:, 0] = 10.0
    c[:, 1] = -10.0
    c[:, 2] = -1.0
    c[:, 3] = 1.0
    c[:, 4] = 0.5
    c[:, 5] = GELU_TANH_S
    c[:, 6] = LN_EPS
    c[:, 7] = 1.0 / D_MODEL
    return c


def phase_scan(nc, st, d, sm, tag, K=32, conv=()):
    NCH = SEQ // K
    s_ld, s_ve, s_ac, s_st = sm["ld"], sm["ve"], sm["ac"], sm["st"]
    s_xa, s_xb = sm["xa"], sm["xb"]
    s_pp = sm["mm"]
    cper = -(-len(conv) // NCH) if conv else 0
    xs_h = d["xs_d"]
    dt_h = d["dt_d"]
    bc_h = d["bc_d"]
    alog_h = d["alog"]
    dd_h = d["dd"]
    zs_h = d["zs_d"]
    cst_h = d["cst_d"]
    yz_o = d["yz_d"]

    apos = st.enter_context(nc.sbuf_tensor(tag + "apos", [128, 8 * 16], F32))
    alog = st.enter_context(nc.sbuf_tensor(tag + "alog_s", [128, 8 * 16], F32))
    dd = st.enter_context(nc.sbuf_tensor(tag + "dd_s", [128, 8], F32))
    cst = st.enter_context(nc.sbuf_tensor(tag + "cst", [128, 8], F32))
    dtc = st.enter_context(nc.sbuf_tensor(tag + "dtc", [128, 2 * 8 * K], F32))
    xsc = st.enter_context(nc.sbuf_tensor(tag + "xsc", [128, 2 * 8 * K], F32))
    zsc = st.enter_context(nc.sbuf_tensor(tag + "zsc", [128, 2 * 8 * K], F32))
    bbc = st.enter_context(nc.sbuf_tensor(tag + "bbc", [128, 2 * 16 * K], F32))
    cbc = st.enter_context(nc.sbuf_tensor(tag + "cbc", [128, 2 * 16 * K], F32))
    bbc2 = st.enter_context(nc.sbuf_tensor(tag + "bbc2", [128, 2 * 16 * K], F32))
    cbc2 = st.enter_context(nc.sbuf_tensor(tag + "cbc2", [128, 2 * 16 * K], F32))
    buf1 = st.enter_context(nc.sbuf_tensor(tag + "buf1", [128, 2 * K * 128], F32))
    buf2 = st.enter_context(nc.sbuf_tensor(tag + "buf2", [128, K * 128], F32))
    buf3 = st.enter_context(nc.sbuf_tensor(tag + "buf3", [128, 2 * K * 128], F32))
    straj = st.enter_context(nc.sbuf_tensor(tag + "straj", [128, 2 * K * 128], F32))
    tyb = st.enter_context(nc.sbuf_tensor(tag + "tyb", [128, 2 * K * 128], F32))
    sS = st.enter_context(nc.sbuf_tensor(tag + "sS", [128, 128], F32))
    zz = st.enter_context(nc.sbuf_tensor(tag + "zz", [128, 128], F32))
    xdtb = st.enter_context(nc.sbuf_tensor(tag + "xdtb", [128, 8 * K], F32))
    yb = st.enter_context(nc.sbuf_tensor(tag + "yb", [128, 8 * K], F32))
    dxb = st.enter_context(nc.sbuf_tensor(tag + "dxb", [128, 8 * K], F32))
    yzst = st.enter_context(nc.sbuf_tensor(tag + "yzst", [128, 2 * 8 * K], F32))
    cb = st.enter_context(nc.sbuf_tensor(tag + "cb", [128, 2 * 4096], BF16))

    dtc_v = dtc[:].rearrange("p (d g t) -> p d g t", d=2, g=8)
    xsc_v = xsc[:].rearrange("p (d g t) -> p d g t", d=2, g=8)
    zsc_v = zsc[:].rearrange("p (d g t) -> p d g t", d=2, g=8)
    bbc_v = bbc[:].rearrange("p (d n t) -> p d n t", d=2, n=16)
    cbc_v = cbc[:].rearrange("p (d n t) -> p d n t", d=2, n=16)
    bbc2_v = bbc2[:].rearrange("p (d t n) -> p d t n", d=2, t=K)
    cbc2_v = cbc2[:].rearrange("p (d t n) -> p d t n", d=2, t=K)
    apos_v = apos[:].rearrange("p (g n) -> p g n", g=8)
    yzst_v = yzst[:].rearrange("p (d g t) -> p d g t", d=2, g=8)
    NLD = 5
    SZ = K * 128

    mq, mu = {}, {}
    pp = [0]
    for it in range(NCH + 1):
        if 0 <= it - 1 < NCH:
            pp[0] += 1
            mq[it - 1] = pp[0]
            pp[0] += 1
            mu[it - 1] = pp[0]
    msc = {c: 2 * c + 1 for c in range(NCH)}
    mtl = {c: 2 * c + 2 for c in range(NCH)}
    vv = [2 * NCH]

    def em_g(g):
        g.dma_start(out=alog[:], in_=alog_h).then_inc(s_ld, 16)
        g.dma_start(out=dd[:], in_=dd_h).then_inc(s_ld, 16)
        g.dma_start(out=cst[:], in_=cst_h).then_inc(s_ld, 16)
        for it in range(NCH + 1):
            cq = it - 1
            if 0 <= cq < NCH:
                par = cq % 2
                g.wait_ge(s_ld, 48 + 16 * NLD * (cq + 1))
                if cq >= 2:
                    g.wait_ge(s_ve, msc[cq - 2])
                q_v = buf1[:, par * SZ:(par + 1) * SZ].rearrange(
                    "p (t g n) -> p t g n", t=K, g=8)
                dt_op = dtc_v[:, par].transpose([0, 2, 1]).unsqueeze(3) \
                    .broadcast_to([128, K, 8, 16])
                ap_op = apos_v.unsqueeze(1).broadcast_to([128, K, 8, 16])
                g.tensor_tensor(q_v, dt_op, ap_op, ALU.mult).then_inc(s_pp, 1)
                g.tensor_tensor(xdtb[:].rearrange("p (g t) -> p g t", g=8),
                                xsc_v[:, par], dtc_v[:, par], ALU.mult)
                g.wait_ge(s_ac, 2 * cq + 2)
                u_v = buf3[:, par * SZ:(par + 1) * SZ].rearrange(
                    "p (t g n) -> p t g n", t=K, g=8)
                xdt_op = xdtb[:].rearrange("p (g t) -> p g t", g=8).transpose([0, 2, 1]) \
                    .unsqueeze(3).broadcast_to([128, K, 8, 16])
                b_op = bbc2_v[:, par].unsqueeze(2).broadcast_to([128, K, 8, 16])
                g.tensor_tensor(u_v, xdt_op, b_op, ALU.mult).then_inc(s_pp, 1)
            if it < NCH:
                par = it % 2
                if it >= 2:
                    g.wait_ge(s_ve, mtl[it - 2])
                sl = slice(it * K, (it + 1) * K)
                g.dma_start(out=dtc_v[:, par],
                            in_=dt_h[:, sl].rearrange("(g p) t -> p g t", p=128)
                            ).then_inc(s_ld, 16)
                g.dma_start(out=xsc_v[:, par],
                            in_=xs_h[:, sl].rearrange("(g p) t -> p g t", p=128)
                            ).then_inc(s_ld, 16)
                g.dma_start(out=zsc_v[:, par],
                            in_=zs_h[:, sl].rearrange("(g p) t -> p g t", p=128)
                            ).then_inc(s_ld, 16)
                g.dma_start(out=bbc_v[:, par],
                            in_=bc_h[64:80, sl].unsqueeze(0).broadcast_to([128, 16, K])
                            ).then_inc(s_ld, 16)
                g.dma_start(out=cbc_v[:, par],
                            in_=bc_h[80:96, sl].unsqueeze(0).broadcast_to([128, 16, K])
                            ).then_inc(s_ld, 16)
                for i in range(cper * it, min(cper * (it + 1), len(conv))):
                    if i >= 2:
                        g.wait_ge(s_xb, 16 * (i - 1))
                    g.dma_start(out=cb[:, (i % 2) * 4096:(i % 2 + 1) * 4096],
                                in_=conv[i][0]).then_inc(s_xa, 16)

    def em_sc(sc):
        sc.wait_ge(s_ld, 48)
        sc.activation(apos[:], alog[:], AF.Exp).then_inc(s_ac, 1)
        for c in range(NCH):
            par = c % 2
            for i in range(cper * c, min(cper * (c + 1), len(conv))):
                sc.wait_ge(s_xa, 16 * (i + 1))
                sc.dma_start(out=conv[i][1],
                             in_=cb[:, (i % 2) * 4096:(i % 2 + 1) * 4096]
                             ).then_inc(s_xb, 16)
            sc.wait_ge(s_ld, 48 + 16 * NLD * (c + 1))
            if c >= 2:
                sc.wait_ge(s_pp, mu[c - 2])
                sc.wait_ge(s_ve, mtl[c - 2])
            sc.copy(bbc2_v[:, par], bbc_v[:, par].transpose([0, 2, 1]))
            sc.copy(cbc2_v[:, par], cbc_v[:, par].transpose([0, 2, 1])).then_inc(s_ac, 1)
            sc.wait_ge(s_pp, mq[c])
            sc.activation(buf2[:], buf1[:, par * SZ:(par + 1) * SZ], AF.Relu,
                          scale=cst[:, 2:3], bias=cst[:, 0:1])
            sc.activation(buf1[:, par * SZ:(par + 1) * SZ], buf2[:], AF.Exp,
                          bias=cst[:, 1:2]).then_inc(s_ac, 1)

    def em_v(v):
        v.wait_ge(s_ld, 48)
        v.tensor_scalar(zz[:], cst[:, 0:1].broadcast_to([128, 128]), 0.0, None, ALU.mult)

        for c in range(NCH):
            par = c % 2
            opar = (c - 1) % 2
            v.wait_ge(s_ac, 2 * c + 3)
            v.wait_ge(s_pp, mu[c])
            E = buf1[:, par * SZ:(par + 1) * SZ]
            U = buf3[:, par * SZ:(par + 1) * SZ]
            S = straj[:, par * SZ:(par + 1) * SZ]
            Sp = straj[:, opar * SZ:(opar + 1) * SZ]
            for t in range(K):
                prev = zz[:] if (c == 0 and t == 0) else \
                    Sp[:, (K - 1) * 128:] if t == 0 else S[:, (t - 1) * 128:t * 128]
                v.tensor_tensor(sS[:], prev, E[:, t * 128:(t + 1) * 128], ALU.mult)
                v.tensor_tensor(sS[:], sS[:], U[:, t * 128:(t + 1) * 128], ALU.add)
                last = v.tensor_scalar(S[:, t * 128:(t + 1) * 128], sS[:], 10.0, -10.0,
                                       ALU.min, ALU.max)
            last.then_inc(s_ve, 1)
            ty_v = tyb[:, par * SZ:(par + 1) * SZ].rearrange(
                "p (t g n) -> p t g n", t=K, g=8)
            s_op = straj[:, par * SZ:(par + 1) * SZ].rearrange(
                "p (t g n) -> p t g n", t=K, g=8)
            c_op = cbc2_v[:, par].unsqueeze(2).broadcast_to([128, K, 8, 16])
            v.tensor_tensor(ty_v, s_op, c_op, ALU.mult)
            yb_v = yb[:].rearrange("p (t g) -> p t g", t=K)
            v.tensor_reduce(yb_v, ty_v, AXL.X, ALU.add)
            dx_v = dxb[:].rearrange("p (t g) -> p t g", t=K)
            v.tensor_tensor(dx_v, xsc_v[:, par].transpose([0, 2, 1]),
                            dd[:].unsqueeze(1).broadcast_to([128, K, 8]), ALU.mult)
            v.tensor_tensor(yb[:], yb[:], dxb[:], ALU.add)
            if c >= 2:
                v.wait_ge(s_st, 16 * (c - 1))
            v.tensor_tensor(yzst_v[:, par], yb_v.transpose([0, 2, 1]),
                            zsc_v[:, par], ALU.mult).then_inc(s_ve, 1)

    def em_sy(sy):
        for c in range(NCH):
            sy.wait_ge(s_ve, mtl[c])
            sy.dma_start(out=yz_o[:, c * K:(c + 1) * K].rearrange("(g p) t -> p g t", p=128),
                         in_=yzst_v[:, c % 2]).then_inc(s_st, 16)
        sy.wait_ge(s_st, 16 * NCH)
        sy.wait_ge(s_xb, 16 * len(conv))

    finals = {"ld": 16 * (3 + NLD * NCH), "ac": 1 + 2 * NCH, "ve": vv[0], "mm": pp[0],
              "st": 16 * NCH, "xa": 16 * len(conv), "xb": 16 * len(conv)}
    return {"g": em_g, "sc": em_sc, "v": em_v, "sy": em_sy}, finals


def host_scan_inputs(inputs):
    A_log = np.asarray(inputs["L0_attn_A_log"])
    D = np.asarray(inputs["L0_attn_D"])
    out = []
    for core in range(NCORES):
        b, h = divmod(core, 2)
        own = np.arange(h * D_HALF, (h + 1) * D_HALF)
        al = A_log[own].reshape(8, 128, 16)
        alog = np.ascontiguousarray(al.transpose(1, 0, 2).reshape(128, 8 * 16))
        dd = np.ascontiguousarray(D[own].reshape(8, 128).T)
        out.append((alog, dd))
    return out


def _emit_ln(v, sc, t, ps_m, ps_q, sx_v, sq, sq_v, ones, cst, mb, rb, stat, out_view,
             waits, incs, nk=8, gamma=None, beta=None):
    raise NotImplementedError


def phase_outln(nc, st, d, sm, tag, TC=512):
    TOK = TOK_PC
    NCH = TOK // TC
    s_ld, s_ve, s_mm, s_ac, s_st = sm["ld"], sm["ve"], sm["mm"], sm["ac"], sm["st"]
    yz_h = d["yzT_d"]
    wo_h = d["w_out"]
    xT_h = d["xT_c"]
    cst_h = d["cst_d"]
    x1_o = d["x1_d"]

    wo = st.enter_context(nc.sbuf_tensor(tag + "wo", [128, 16 * D_MODEL], BF16))
    cst = st.enter_context(nc.sbuf_tensor(tag + "cst", [128, 8], F32))
    ones = st.enter_context(nc.sbuf_tensor(tag + "ones", [128, 2], F32))
    yzc = st.enter_context(nc.sbuf_tensor(tag + "yzc", [128, 2 * 16 * TC], BF16))
    xc = st.enter_context(nc.sbuf_tensor(tag + "xc", [128, 2 * 8 * TC], F32))
    sx = st.enter_context(nc.sbuf_tensor(tag + "sx", [128, 8 * TC], F32))
    sq = st.enter_context(nc.sbuf_tensor(tag + "sq", [128, 8 * TC], F32))
    stat = st.enter_context(nc.sbuf_tensor(tag + "stat", [1, 6 * TC], F32))
    mb = st.enter_context(nc.sbuf_tensor(tag + "mb", [128, TC], F32))
    rb = st.enter_context(nc.sbuf_tensor(tag + "rb", [128, TC], F32))
    ost = st.enter_context(nc.sbuf_tensor(tag + "ost", [128, 2 * 8 * TC], F32))
    psA = st.enter_context(nc.psum_tensor(tag + "psA", [128, TC], F32))
    psB = st.enter_context(nc.psum_tensor(tag + "psB", [128, TC], F32))
    psm = st.enter_context(nc.psum_tensor(tag + "psm", [1, TC], F32))
    psq = st.enter_context(nc.psum_tensor(tag + "psq", [1, TC], F32))
    psbc = st.enter_context(nc.psum_tensor(tag + "psbc", [128, 2 * TC], F32))

    wo_v = wo[:].rearrange("p (k m) -> p k m", k=16)
    yzc_v = yzc[:].rearrange("p (d k t) -> p d k t", d=2, k=16)
    xc_v = xc[:].rearrange("p (d r t) -> p d r t", d=2, r=8)
    sx_v = sx[:].rearrange("p (r t) -> p r t", r=8)
    sq_v = sq[:].rearrange("p (r t) -> p r t", r=8)
    ost_v = ost[:].rearrange("p (d r t) -> p d r t", d=2, r=8)
    MMC = 12
    ACC = 12
    VEC = 4

    def em_g(g):
        g.dma_start(out=wo[:], in_=wo_h).then_inc(s_ld, 16)
        g.dma_start(out=cst[:], in_=cst_h).then_inc(s_ld, 16)
        for c in range(NCH):
            d = c % 2
            if c >= 2:
                g.wait_ge(s_ve, VEC * (c - 2) + 4)
            sl = slice(c * TC, (c + 1) * TC)
            g.dma_start(out=yzc_v[:, d], in_=yz_h[:, sl].rearrange("(k p) t -> p k t", p=128)
                        ).then_inc(s_ld, 16)
            g.dma_start(out=xc_v[:, d], in_=xT_h[:, sl].rearrange("(r p) t -> p r t", p=128)
                        ).then_inc(s_ld, 16)

    def em_t(t):
        t.wait_ge(s_ld, 32)
        for c in range(NCH):
            d = c % 2
            t.wait_ge(s_ld, 32 + 16 * (2 * c + 1))
            for m in range(8):
                bank = psA if m % 2 == 0 else psB
                if 8 * c + m >= 2:
                    cp, mp = divmod(8 * c + m - 2, 8)
                    t.wait_ge(s_ac, 1 + ACC * cp + mp + 1)
                for k in range(16):
                    mm = t.matmul(bank[:], wo_v[:, k, m * 128:(m + 1) * 128],
                                  yzc_v[:, d, k], start=(k == 0), stop=(k == 15))
                mm.then_inc(s_mm, 1)
            t.wait_ge(s_ve, VEC * c + 1)
            for k in range(8):
                mm = t.matmul(psm[:], ones[:, 0:1], sx_v[:, k], start=(k == 0), stop=(k == 7))
            mm.then_inc(s_mm, 1)
            t.wait_ge(s_ac, 1 + ACC * c + 9)
            for k in range(8):
                mm = t.matmul(psq[:], ones[:, 0:1], sq_v[:, k], start=(k == 0), stop=(k == 7))
            mm.then_inc(s_mm, 1)
            t.wait_ge(s_ve, VEC * c + 3)
            t.matmul(psbc[:, 0:TC], ones[0:1, 0:1].broadcast_to([1, 128]),
                     stat[0:1, 0:TC], start=True, stop=True).then_inc(s_mm, 1)
            t.matmul(psbc[:, TC:2 * TC], ones[0:1, 0:1].broadcast_to([1, 128]),
                     stat[0:1, 3 * TC:4 * TC], start=True, stop=True).then_inc(s_mm, 1)

    def em_sc(sc):
        sc.wait_ge(s_ld, 32)
        sc.copy(ones[:], cst[:, 3:4].broadcast_to([128, 2])).then_inc(s_ac, 1)
        for c in range(NCH):
            ACB = 1 + ACC * c
            if c >= 1:
                sc.wait_ge(s_ve, VEC * (c - 1) + 4)
            for m in range(8):
                sc.wait_ge(s_mm, MMC * c + m + 1)
                bank = psA if m % 2 == 0 else psB
                sc.copy(sx_v[:, m], bank[:]).then_inc(s_ac, 1)
            sc.wait_ge(s_ve, VEC * c + 1)
            sc.activation(sq[:], sx[:], AF.Square).then_inc(s_ac, 1)
            sc.wait_ge(s_mm, MMC * c + 10)
            sc.activation(stat[0:1, 0:TC], psm[0:1, :], AF.Identity, scale=cst[0:1, 7:8])
            sc.activation(stat[0:1, TC:2 * TC], psq[0:1, :], AF.Identity,
                          scale=cst[0:1, 7:8]).then_inc(s_ac, 1)
            sc.wait_ge(s_ve, VEC * c + 2)
            sc.activation(stat[0:1, 5 * TC:6 * TC], stat[0:1, 4 * TC:5 * TC], AF.Sqrt,
                          bias=cst[0:1, 6:7]).then_inc(s_ac, 1)
            sc.wait_ge(s_mm, MMC * c + 12)
            sc.copy(mb[:], psbc[:, 0:TC])
            sc.copy(rb[:], psbc[:, TC:2 * TC]).then_inc(s_ac, 1)

    def em_v(v):
        for c in range(NCH):
            d = c % 2
            ACB = 1 + ACC * c
            v.wait_ge(s_ac, ACB + 8)
            v.wait_ge(s_ld, 32 + 16 * (2 * c + 2))
            v.tensor_tensor(sx_v[:], sx_v[:], xc_v[:, d], ALU.add).then_inc(s_ve, 1)
            v.wait_ge(s_ac, ACB + 10)
            v.tensor_tensor(stat[0:1, 4 * TC:5 * TC], stat[0:1, 0:TC], stat[0:1, 0:TC],
                            ALU.mult)
            v.scalar_tensor_tensor(stat[0:1, 4 * TC:5 * TC], stat[0:1, TC:2 * TC], 1.0,
                                   stat[0:1, 4 * TC:5 * TC], ALU.mult,
                                   ALU.subtract).then_inc(s_ve, 1)
            v.wait_ge(s_ac, ACB + 11)
            v.reciprocal(stat[0:1, 3 * TC:4 * TC],
                         stat[0:1, 5 * TC:6 * TC]).then_inc(s_ve, 1)
            v.wait_ge(s_ac, ACB + 12)
            if c >= 2:
                v.wait_ge(s_st, 16 * (c - 1))
            for k in range(8):
                v.tensor_tensor(ost_v[:, d, k], sx_v[:, k], mb[:], ALU.subtract)
                last = v.tensor_tensor(ost_v[:, d, k], ost_v[:, d, k], rb[:], ALU.mult)
            last.then_inc(s_ve, 1)

    def em_sy(sy):
        for c in range(NCH):
            sy.wait_ge(s_ve, VEC * c + 4)
            sy.dma_start(out=x1_o[:, c * TC:(c + 1) * TC].rearrange("(r p) t -> p r t", p=128),
                         in_=ost_v[:, c % 2]).then_inc(s_st, 16)
        sy.wait_ge(s_st, 16 * NCH)

    finals = {"ld": 16 * (2 + 2 * NCH), "mm": MMC * NCH, "ac": 1 + ACC * NCH,
              "ve": VEC * NCH, "st": 16 * NCH}
    return {"g": em_g, "t": em_t, "sc": em_sc, "v": em_v, "sy": em_sy}, finals


def phase_ffn0(nc, st, d, sm, tag, TC=256):
    TOK = TOK_PC
    NCH = TOK // TC
    s_ld, s_ve, s_mm, s_ac, s_st = sm["ld"], sm["ve"], sm["mm"], sm["ac"], sm["st"]
    x1_h = d["x1_d"]
    w1_h = d["w1b"]
    w2_h = d["w2b"]
    h1_h = d["h1b"]
    h2_h = d["h2b"]
    w3_h = d["w3b"]
    s1_h = d["s1"]
    s2_h = d["s2"]
    sh1_h = d["sh1"]
    sh2_h = d["sh2"]
    s3_h = d["s3"]
    cst_h = d["cst_d"]
    x2_o = d["x2_d"]

    ws = st.enter_context(nc.sbuf_tensor(tag + "ws", [128, 32 * 1024], BF16))
    t1a = st.enter_context(nc.sbuf_tensor(tag + "t1a", [128, 32 * TC], F32))
    t1 = st.enter_context(nc.sbuf_tensor(tag + "t1", [128, 32 * TC], BF16))
    t2 = st.enter_context(nc.sbuf_tensor(tag + "t2", [128, 32 * TC], BF16))
    tmp2 = st.enter_context(nc.sbuf_tensor(tag + "tmp2", [128, 2 * TC], F32))
    x1c = st.enter_context(nc.sbuf_tensor(tag + "x1c", [128, 8 * TC], F32))
    x16 = st.enter_context(nc.sbuf_tensor(tag + "x16", [128, 8 * TC], BF16))
    sx = st.enter_context(nc.sbuf_tensor(tag + "sx", [128, 8 * TC], F32))
    sq = st.enter_context(nc.sbuf_tensor(tag + "sq", [128, 8 * TC], F32))
    stat = st.enter_context(nc.sbuf_tensor(tag + "stat", [1, 6 * TC], F32))
    mb = st.enter_context(nc.sbuf_tensor(tag + "mb", [128, TC], F32))
    rb = st.enter_context(nc.sbuf_tensor(tag + "rb", [128, TC], F32))
    ost = st.enter_context(nc.sbuf_tensor(tag + "ost", [128, 8 * TC], F32))
    s1s = st.enter_context(nc.sbuf_tensor(tag + "s1s", [128, 32], F32))
    s2s = st.enter_context(nc.sbuf_tensor(tag + "s2s", [128, 32], F32))
    sh1s = st.enter_context(nc.sbuf_tensor(tag + "sh1s", [128, 32], F32))
    sh2s = st.enter_context(nc.sbuf_tensor(tag + "sh2s", [128, 32], F32))
    s3s = st.enter_context(nc.sbuf_tensor(tag + "s3s", [128, 8], F32))
    cst = st.enter_context(nc.sbuf_tensor(tag + "cst", [128, 8], F32))
    ones = st.enter_context(nc.sbuf_tensor(tag + "ones", [128, 2], F32))
    ps = [st.enter_context(nc.psum_tensor(f"{tag}ps{i}", [128, TC], F32)) for i in range(4)]
    psm = st.enter_context(nc.psum_tensor(tag + "psm", [1, TC], F32))
    psq = st.enter_context(nc.psum_tensor(tag + "psq", [1, TC], F32))
    psbc = st.enter_context(nc.psum_tensor(tag + "psbc", [128, 2 * TC], F32))

    ws8 = ws[:].rearrange("p (k m) -> p k m", k=8)
    ws32 = ws[:].rearrange("p (k m) -> p k m", k=32)
    t1a_v = t1a[:].rearrange("p (m t) -> p m t", m=32)
    t1_v = t1[:].rearrange("p (m t) -> p m t", m=32)
    t2_v = t2[:].rearrange("p (m t) -> p m t", m=32)
    tmp2_v = tmp2[:].rearrange("p (d t) -> p d t", d=2)
    x1c_v = x1c[:].rearrange("p (r t) -> p r t", r=8)
    x16_v = x16[:].rearrange("p (r t) -> p r t", r=8)
    sx_v = sx[:].rearrange("p (r t) -> p r t", r=8)
    sq_v = sq[:].rearrange("p (r t) -> p r t", r=8)
    ost_v = ost[:].rearrange("p (r t) -> p r t", r=8)
    LDC = 23
    MMC = 140
    ACC = 140
    VEC = 37
    STG = ([(w1_h[s], 8, 16, 1 + 16 * s) for s in range(2)] +
           [(w2_h[s], 8, 16, 33 + 16 * s) for s in range(2)] +
           [(h1_h[s], 32, 4, 65 + 4 * s) for s in range(8)] +
           [(h2_h[s], 32, 4, 97 + 4 * s) for s in range(8)] +
           [(w3_h[s], 32, 4, 129 + 4 * s) for s in range(2)])
    TILE_END = [src[3] + src[2] - 1 for src in STG]
    HW = 16 * 1024

    def em_g(g):
        for w, dst in [(cst_h, cst[:]), (s1_h, s1s[:]), (s2_h, s2s[:]),
                       (sh1_h, sh1s[:]), (sh2_h, sh2s[:]), (s3_h, s3s[:])]:
            g.dma_start(out=dst, in_=w).then_inc(s_ld, 16)
        for c in range(NCH):
            sl = slice(c * TC, (c + 1) * TC)
            if c >= 1:
                g.wait_ge(s_ve, VEC * (c - 1) + 34)
            g.dma_start(out=x1c_v, in_=x1_h[:, sl].rearrange("(r p) t -> p r t", p=128)
                        ).then_inc(s_ld, 16)
            for s, (src, kk, nt, j0) in enumerate(STG):
                half = s % 2
                if s >= 2:
                    g.wait_ge(s_mm, MMC * c + TILE_END[s - 2])
                elif c >= 1:
                    g.wait_ge(s_mm, MMC * (c - 1) + TILE_END[s + 20])
                g.dma_start(out=ws[:, half * HW:(half + 1) * HW], in_=src
                            ).then_inc(s_ld, 16)

    def bank_wait(t, c, j):
        if j > 4:
            t.wait_ge(s_ac, 1 + ACC * c + j - 4)
        elif c >= 1:
            t.wait_ge(s_ac, 1 + ACC * (c - 1) + 132 + j)

    def em_t(t):
        t.wait_ge(s_ld, 96)
        for c in range(NCH):
            t.wait_ge(s_ve, VEC * c + 1)
            for s, (src, kk, nt, j0) in enumerate(STG):
                half = s % 2
                wsv = ws[:, half * HW:(half + 1) * HW].rearrange(
                    "p (k m) -> p k m", k=kk)
                if j0 == 65:
                    t.wait_ge(s_ve, VEC * c + 33)
                elif j0 == 97:
                    t.wait_ge(s_ac, 1 + ACC * c + 96)
                elif j0 == 129:
                    t.wait_ge(s_ac, 1 + ACC * c + 128)
                t.wait_ge(s_ld, 96 + 16 * (LDC * c + 2 + s))
                if j0 < 65:
                    rhs = x16_v
                elif j0 < 97:
                    rhs = t1_v
                elif j0 < 129:
                    rhs = t2_v
                else:
                    rhs = t1_v
                for lt in range(nt):
                    j = j0 + lt
                    bank_wait(t, c, j)
                    for k in range(kk):
                        mm = t.matmul(ps[(j - 1) % 4][:],
                                      wsv[:, k, lt * 128:(lt + 1) * 128],
                                      rhs[:, k], start=(k == 0), stop=(k == kk - 1))
                    mm.then_inc(s_mm, 1)
            t.wait_ge(s_ve, VEC * c + 34)
            for k in range(8):
                mm = t.matmul(psm[:], ones[:, 0:1], sx_v[:, k], start=(k == 0), stop=(k == 7))
            mm.then_inc(s_mm, 1)
            t.wait_ge(s_ac, 1 + ACC * c + 137)
            for k in range(8):
                mm = t.matmul(psq[:], ones[:, 0:1], sq_v[:, k], start=(k == 0), stop=(k == 7))
            mm.then_inc(s_mm, 1)
            t.wait_ge(s_ve, VEC * c + 36)
            t.matmul(psbc[:, 0:TC], ones[0:1, 0:1].broadcast_to([1, 128]),
                     stat[0:1, 0:TC], start=True, stop=True).then_inc(s_mm, 1)
            t.matmul(psbc[:, TC:2 * TC], ones[0:1, 0:1].broadcast_to([1, 128]),
                     stat[0:1, 3 * TC:4 * TC], start=True, stop=True).then_inc(s_mm, 1)

    def em_sc(sc):
        sc.wait_ge(s_ld, 96)
        sc.copy(ones[:], cst[:, 3:4].broadcast_to([128, 2])).then_inc(s_ac, 1)
        for c in range(NCH):
            ACB = 1 + ACC * c
            for m in range(32):
                sc.wait_ge(s_mm, MMC * c + m + 1)
                sc.activation(t1a_v[:, m], ps[m % 4][:], AF.Silu,
                              scale=s1s[:, m:m + 1]).then_inc(s_ac, 1)
            for m in range(32):
                j = 32 + m + 1
                sc.wait_ge(s_mm, MMC * c + j)
                if m >= 2:
                    sc.wait_ge(s_ve, VEC * c + 1 + m - 1)
                sc.activation(tmp2_v[:, m % 2], ps[(j - 1) % 4][:], AF.Identity,
                              scale=s2s[:, m:m + 1]).then_inc(s_ac, 1)
            for m in range(32):
                j = 64 + m + 1
                sc.wait_ge(s_mm, MMC * c + j)
                sc.activation(t2_v[:, m], ps[(j - 1) % 4][:], AF.Gelu,
                              scale=sh1s[:, m:m + 1]).then_inc(s_ac, 1)
            for m in range(32):
                j = 96 + m + 1
                sc.wait_ge(s_mm, MMC * c + j)
                sc.activation(t1_v[:, m], ps[(j - 1) % 4][:], AF.Gelu,
                              scale=sh2s[:, m:m + 1]).then_inc(s_ac, 1)
            for m in range(8):
                j = 128 + m + 1
                sc.wait_ge(s_mm, MMC * c + j)
                sc.activation(sx_v[:, m], ps[(j - 1) % 4][:], AF.Identity,
                              scale=s3s[:, m:m + 1]).then_inc(s_ac, 1)
            sc.wait_ge(s_ve, VEC * c + 34)
            sc.activation(sq[:], sx[:], AF.Square).then_inc(s_ac, 1)
            sc.wait_ge(s_mm, MMC * c + 138)
            sc.activation(stat[0:1, 0:TC], psm[0:1, :], AF.Identity, scale=cst[0:1, 7:8])
            sc.activation(stat[0:1, TC:2 * TC], psq[0:1, :], AF.Identity,
                          scale=cst[0:1, 7:8]).then_inc(s_ac, 1)
            sc.wait_ge(s_ve, VEC * c + 35)
            sc.activation(stat[0:1, 5 * TC:6 * TC], stat[0:1, 4 * TC:5 * TC], AF.Sqrt,
                          bias=cst[0:1, 6:7]).then_inc(s_ac, 1)
            sc.wait_ge(s_mm, MMC * c + 140)
            sc.copy(mb[:], psbc[:, 0:TC])
            sc.copy(rb[:], psbc[:, TC:2 * TC]).then_inc(s_ac, 1)

    def em_v(v):
        for c in range(NCH):
            ACB = 1 + ACC * c
            v.wait_ge(s_ld, 96 + 16 * (LDC * c + 1))
            v.tensor_copy(x16[:], x1c[:]).then_inc(s_ve, 1)
            for m in range(32):
                v.wait_ge(s_ac, ACB + 32 + m + 1)
                v.tensor_tensor(t1_v[:, m], t1a_v[:, m], tmp2_v[:, m % 2],
                                ALU.mult).then_inc(s_ve, 1)
            v.wait_ge(s_ac, ACB + 136)
            v.tensor_tensor(sx_v[:], sx_v[:], x1c_v[:], ALU.add).then_inc(s_ve, 1)
            v.wait_ge(s_ac, ACB + 138)
            v.tensor_tensor(stat[0:1, 4 * TC:5 * TC], stat[0:1, 0:TC], stat[0:1, 0:TC],
                            ALU.mult)
            v.scalar_tensor_tensor(stat[0:1, 4 * TC:5 * TC], stat[0:1, TC:2 * TC], 1.0,
                                   stat[0:1, 4 * TC:5 * TC], ALU.mult,
                                   ALU.subtract).then_inc(s_ve, 1)
            v.wait_ge(s_ac, ACB + 139)
            v.reciprocal(stat[0:1, 3 * TC:4 * TC],
                         stat[0:1, 5 * TC:6 * TC]).then_inc(s_ve, 1)
            v.wait_ge(s_ac, ACB + 140)
            if c >= 1:
                v.wait_ge(s_st, 16 * c)
            for k in range(8):
                v.tensor_tensor(ost_v[:, k], sx_v[:, k], mb[:], ALU.subtract)
                last = v.tensor_tensor(ost_v[:, k], ost_v[:, k], rb[:], ALU.mult)
            last.then_inc(s_ve, 1)

    def em_sy(sy):
        for c in range(NCH):
            sy.wait_ge(s_ve, VEC * c + 37)
            sy.dma_start(out=x2_o[:, c * TC:(c + 1) * TC].rearrange("(r p) t -> p r t", p=128),
                         in_=ost_v).then_inc(s_st, 16)
        sy.wait_ge(s_st, 16 * NCH)

    finals = {"ld": 16 * (6 + LDC * NCH), "mm": MMC * NCH, "ac": 1 + ACC * NCH,
              "ve": VEC * NCH, "st": 16 * NCH}
    return {"g": em_g, "t": em_t, "sc": em_sc, "v": em_v, "sy": em_sy}, finals


def host_ffn0_weights(inputs, consts):
    w1 = np.asarray(inputs["L0_ffn_w1_w"]); s1 = np.asarray(inputs["L0_ffn_w1_s"])
    w2 = np.asarray(inputs["L0_ffn_w2_w"]); s2 = np.asarray(inputs["L0_ffn_w2_s"])
    h1 = np.asarray(inputs["L0_ffn_h1_w"]); sh1 = np.asarray(inputs["L0_ffn_h1_s"])
    h2 = np.asarray(inputs["L0_ffn_h2_w"]); sh2 = np.asarray(inputs["L0_ffn_h2_s"])
    w3 = np.asarray(inputs["L0_ffn_w3_w"]); s3 = np.asarray(inputs["L0_ffn_w3_s"])
    arr = lambda s, m: np.ascontiguousarray(s.reshape(m, 128).T)

    def pack_s(wT, k, ns):
        M = wT.shape[1]
        w = M // ns
        return np.stack([pack_kpm(wT[:, i * w:(i + 1) * w], k) for i in range(ns)])
    return {
        "w1T": pack_s(w1.T, 8, 2), "w2T": pack_s(w2.T, 8, 2),
        "h1T": pack_s(h1.T, 32, 8), "h2T": pack_s(h2.T, 32, 8),
        "w3T": pack_s(w3.T, 32, 2),
        "s1": arr(s1, 32), "s2": arr(s2, 32), "sh1": arr(sh1, 32),
        "sh2": arr(sh2, 32), "s3": arr(s3, 8), "cst_d": consts,
    }


def phase_sig(nc, st, d, sm, tag, TC=512):
    TOK = TOK_PC
    NCH = TOK // TC
    s_ld, s_mm, s_ac, s_st = sm["ld"], sm["mm"], sm["ac"], sm["st"]
    x2_h = d["x2_d"]
    wr_h = d["w_ret"]
    br_h = d["b_ret"]
    sig_o = d["sig_d"]

    wr = st.enter_context(nc.sbuf_tensor(tag + "wr", [128, 8 * D_MODEL], BF16))
    br = st.enter_context(nc.sbuf_tensor(tag + "br", [128, 8], F32))
    xc = st.enter_context(nc.sbuf_tensor(tag + "xc", [128, 2 * 8 * TC], BF16))
    ob = st.enter_context(nc.sbuf_tensor(tag + "ob", [128, 2 * 8 * TC], F32))
    psA = st.enter_context(nc.psum_tensor(tag + "psA", [128, TC], F32))
    psB = st.enter_context(nc.psum_tensor(tag + "psB", [128, TC], F32))

    wr_v = wr[:].rearrange("p (k m) -> p k m", k=8)
    xc_v = xc[:].rearrange("p (d k t) -> p d k t", d=2, k=8)
    ob_v = ob[:].rearrange("p (d r t) -> p d r t", d=2, r=8)

    def em_g(g):
        g.dma_start(out=wr[:], in_=wr_h).then_inc(s_ld, 16)
        g.dma_start(out=br[:], in_=br_h).then_inc(s_ld, 16)
        for c in range(NCH):
            if c >= 2:
                g.wait_ge(s_mm, 8 * (c - 1))
            g.dma_start(out=xc_v[:, c % 2],
                        in_=x2_h[:, c * TC:(c + 1) * TC].rearrange("(k p) t -> p k t", p=128)
                        ).then_inc(s_ld, 16)

    def em_t(t):
        t.wait_ge(s_ld, 32)
        for c in range(NCH):
            t.wait_ge(s_ld, 32 + 16 * (c + 1))
            for m in range(8):
                bank = psA if m % 2 == 0 else psB
                if 8 * c + m >= 2:
                    t.wait_ge(s_ac, 8 * c + m - 1)
                for k in range(8):
                    mm = t.matmul(bank[:], wr_v[:, k, m * 128:(m + 1) * 128],
                                  xc_v[:, c % 2, k], start=(k == 0), stop=(k == 7))
                mm.then_inc(s_mm, 1)

    def em_sc(sc):
        for c in range(NCH):
            if c >= 2:
                sc.wait_ge(s_st, 16 * (c - 1))
            for m in range(8):
                sc.wait_ge(s_mm, 8 * c + m + 1)
                bank = psA if m % 2 == 0 else psB
                sc.activation(ob_v[:, c % 2, m], bank[:], AF.Sigmoid,
                              bias=br[:, m:m + 1]).then_inc(s_ac, 1)

    def em_sy(sy):
        for c in range(NCH):
            sy.wait_ge(s_ac, 8 * (c + 1))
            sy.dma_start(out=sig_o[:, c * TC:(c + 1) * TC].rearrange("(r p) t -> p r t", p=128),
                         in_=ob_v[:, c % 2]).then_inc(s_st, 16)
        sy.wait_ge(s_st, 16 * NCH)

    finals = {"ld": 16 * (2 + NCH), "mm": 8 * NCH, "ac": 8 * NCH, "st": 16 * NCH}
    return {"g": em_g, "t": em_t, "sc": em_sc, "sy": em_sy}, finals


def phase_ema(nc, st, d, sm, tag):
    s_ld, s_ve, s_st = sm["ld"], sm["ve"], sm["st"]
    sig_h = d["sigb_d"]
    cst_h = d["cst_d"]
    ret_o = d["ret_d"]

    sg = st.enter_context(nc.sbuf_tensor(tag + "sg", [128, 4 * SEQ], F32))
    rt = st.enter_context(nc.sbuf_tensor(tag + "rt", [128, SEQ], F32))
    cst = st.enter_context(nc.sbuf_tensor(tag + "cst", [128, 8], F32))
    sg_v = sg[:].rearrange("p (r t) -> p r t", r=4)

    def em_g(g):
        g.dma_start(out=cst[:], in_=cst_h).then_inc(s_ld, 16)
        g.dma_start(out=sg_v, in_=sig_h.rearrange("(r p) t -> p r t", p=128)).then_inc(s_ld, 16)

    def em_v(v):
        v.wait_ge(s_ld, 32)
        half = cst[:, 4:5]
        for r in range(4):
            if r >= 1:
                v.wait_ge(s_st, 16 * r)
            v.tensor_scalar(sg_v[:, r], sg_v[:, r], 0.5, None, ALU.mult)
            v.tensor_tensor_scan(rt[:], half.broadcast_to([128, SEQ]), sg_v[:, r],
                                 0.0, ALU.mult, ALU.add).then_inc(s_ve, 1)

    def em_sy(sy):
        for r in range(4):
            sy.wait_ge(s_ve, r + 1)
            sy.dma_start(out=ret_o[r * 128:(r + 1) * 128, :], in_=rt[:]).then_inc(s_st, 16)
        sy.wait_ge(s_st, 64)

    finals = {"ld": 32, "ve": 4, "st": 64}
    return {"g": em_g, "v": em_v, "sy": em_sy}, finals


def phase_l1rest(nc, st, d, sm, tag, TC=256, TOK=None):
    TOK = TOK_PC if TOK is None else TOK
    NCH = TOK // TC
    s_ld, s_ve, s_mm, s_ac, s_st = sm["ld"], sm["ve"], sm["mm"], sm["ac"], sm["st"]
    x2_h = d["x2_d"]
    ret_h = d["retc_d"]
    w1_h = d["W1b"]
    w2_h = d["W2b"]
    b1_h = d["b1"]
    b2_h = d["b2"]
    g1_h = d["g1"]
    be1_h = d["be1"]
    g2_h = d["g2"]
    be2_h = d["be2"]
    cst_h = d["cst_d"]
    x3_o = d["x3_d"]

    ws = st.enter_context(nc.sbuf_tensor(tag + "ws", [128, 32 * 1024], BF16))
    x2c = st.enter_context(nc.sbuf_tensor(tag + "x2c", [128, 8 * TC], F32))
    rtc = st.enter_context(nc.sbuf_tensor(tag + "rtc", [128, 8 * TC], F32))
    hb = st.enter_context(nc.sbuf_tensor(tag + "hb", [128, 8 * TC], F32))
    h16 = st.enter_context(nc.sbuf_tensor(tag + "h16", [128, 8 * TC], BF16))
    gb = st.enter_context(nc.sbuf_tensor(tag + "gb", [128, 32 * TC], BF16))
    vb = st.enter_context(nc.sbuf_tensor(tag + "vb", [128, 2 * TC], F32))
    v2b = st.enter_context(nc.sbuf_tensor(tag + "v2b", [128, TC], F32))
    thb = st.enter_context(nc.sbuf_tensor(tag + "thb", [128, TC], F32))
    sx = st.enter_context(nc.sbuf_tensor(tag + "sx", [128, 8 * TC], F32))
    sq = st.enter_context(nc.sbuf_tensor(tag + "sq", [128, 8 * TC], F32))
    stat = st.enter_context(nc.sbuf_tensor(tag + "stat", [1, 6 * TC], F32))
    mb = st.enter_context(nc.sbuf_tensor(tag + "mb", [128, TC], F32))
    rb = st.enter_context(nc.sbuf_tensor(tag + "rb", [128, TC], F32))
    ost = st.enter_context(nc.sbuf_tensor(tag + "ost", [128, 8 * TC], F32))
    b1s = st.enter_context(nc.sbuf_tensor(tag + "b1s", [128, 32], F32))
    b2s = st.enter_context(nc.sbuf_tensor(tag + "b2s", [128, 8], F32))
    g1s = st.enter_context(nc.sbuf_tensor(tag + "g1s", [128, 8], F32))
    be1s = st.enter_context(nc.sbuf_tensor(tag + "be1s", [128, 8], F32))
    g2s = st.enter_context(nc.sbuf_tensor(tag + "g2s", [128, 8], F32))
    be2s = st.enter_context(nc.sbuf_tensor(tag + "be2s", [128, 8], F32))
    cst = st.enter_context(nc.sbuf_tensor(tag + "cst", [128, 8], F32))
    ones = st.enter_context(nc.sbuf_tensor(tag + "ones", [128, 2], F32))
    ps = [st.enter_context(nc.psum_tensor(f"{tag}ps{i}", [128, TC], F32)) for i in range(4)]
    psm = st.enter_context(nc.psum_tensor(tag + "psm", [1, TC], F32))
    psq = st.enter_context(nc.psum_tensor(tag + "psq", [1, TC], F32))
    psbc = st.enter_context(nc.psum_tensor(tag + "psbc", [128, 2 * TC], F32))

    ws8 = ws[:].rearrange("p (k m) -> p k m", k=8)
    ws32 = ws[:].rearrange("p (k m) -> p k m", k=32)
    x2c_v = x2c[:].rearrange("p (r t) -> p r t", r=8)
    rtc_v = rtc[:].rearrange("p (r t) -> p r t", r=8)
    hb_v = hb[:].rearrange("p (r t) -> p r t", r=8)
    h16_v = h16[:].rearrange("p (r t) -> p r t", r=8)
    gb_v = gb[:].rearrange("p (m t) -> p m t", m=32)
    vb_v = vb[:].rearrange("p (d t) -> p d t", d=2)
    sx_v = sx[:].rearrange("p (r t) -> p r t", r=8)
    sq_v = sq[:].rearrange("p (r t) -> p r t", r=8)
    ost_v = ost[:].rearrange("p (r t) -> p r t", r=8)
    LDC = 4
    MMC = 32 + 8 + 3 * 4
    ACC = 32 + 8 + 32 * 1 + 3 * 4 + 3 + 1
    VEC = None

    mmn = [0]; acn = [1]; ven = [0]

    MMCc = 52
    ACCc = 86
    VECc = 46

    def em_g(g):
        for w, dst in [(cst_h, cst[:]), (b1_h, b1s[:]), (b2_h, b2s[:]),
                       (g1_h, g1s[:]), (be1_h, be1s[:]), (g2_h, g2s[:]), (be2_h, be2s[:])]:
            g.dma_start(out=dst, in_=w).then_inc(s_ld, 16)
        for c in range(NCH):
            sl = slice(c * TC, (c + 1) * TC)
            if c >= 1:
                g.wait_ge(s_ve, VECc * (c - 1) + 43)
            g.dma_start(out=x2c_v, in_=x2_h[:, sl].rearrange("(r p) t -> p r t", p=128)
                        ).then_inc(s_ld, 16)
            if c >= 1:
                g.wait_ge(s_ve, VECc * (c - 1) + 1)
            g.dma_start(out=rtc_v, in_=ret_h[:, sl].rearrange("(r p) t -> p r t", p=128)
                        ).then_inc(s_ld, 16)
            if c >= 1:
                g.wait_ge(s_mm, MMCc * (c - 1) + 44)
            g.dma_start(out=ws[:], in_=w1_h).then_inc(s_ld, 16)
            g.wait_ge(s_mm, MMCc * c + 36)
            g.dma_start(out=ws[:], in_=w2_h).then_inc(s_ld, 16)

    def emit_ln_tensor(t, c, src_v, base_mm, base_ac_sq, ve_src, ve_rstd):
        t.wait_ge(s_ve, ve_src)
        for k in range(8):
            mm = t.matmul(psm[:], ones[:, 0:1], src_v[:, k], start=(k == 0), stop=(k == 7))
        mm.then_inc(s_mm, 1)
        t.wait_ge(s_ac, base_ac_sq)
        for k in range(8):
            mm = t.matmul(psq[:], ones[:, 0:1], sq_v[:, k], start=(k == 0), stop=(k == 7))
        mm.then_inc(s_mm, 1)
        t.wait_ge(s_ve, ve_rstd)
        t.matmul(psbc[:, 0:TC], ones[0:1, 0:1].broadcast_to([1, 128]), stat[0:1, 0:TC],
                 start=True, stop=True).then_inc(s_mm, 1)
        t.matmul(psbc[:, TC:2 * TC], ones[0:1, 0:1].broadcast_to([1, 128]),
                 stat[0:1, 3 * TC:4 * TC], start=True, stop=True).then_inc(s_mm, 1)

    def em_t(t):
        t.wait_ge(s_ld, 112)
        for c in range(NCH):
            MB = MMCc * c
            AB = 1 + ACCc * c
            emit_ln_tensor(t, c, sx_v, MB, AB + 1, VECc * c + 1, VECc * c + 3)
            t.wait_ge(s_ld, 112 + 16 * (LDC * c + 3))
            t.wait_ge(s_ve, VECc * c + 5)
            for m in range(32):
                j = m + 1
                if j > 4:
                    t.wait_ge(s_ac, 1 + ACCc * c + 2 * j - 4)
                elif c >= 1:
                    t.wait_ge(s_ac, 1 + ACCc * (c - 1) + 73 + j)
                for k in range(8):
                    mm = t.matmul(ps[(j - 1) % 4][:], ws8[:, k, m * 128:(m + 1) * 128],
                                  h16_v[:, k], start=(k == 0), stop=(k == 7))
                mm.then_inc(s_mm, 1)
            t.wait_ge(s_ld, 112 + 16 * (LDC * c + 4))
            t.wait_ge(s_ve, VECc * c + 38)
            for m in range(8):
                j = m + 1
                AB = 1 + ACCc * c
                t.wait_ge(s_ac, AB + 66 + m if m >= 4 else AB + 62 + 2 * m)
                for k in range(32):
                    mm = t.matmul(ps[(j - 1) % 4][:], ws32[:, k, m * 128:(m + 1) * 128],
                                  gb_v[:, k], start=(k == 0), stop=(k == 31))
                mm.then_inc(s_mm, 1)
            emit_ln_tensor(t, c, sx_v, MB + 44, AB + 78, VECc * c + 39, VECc * c + 41)
            emit_ln_tensor(t, c, sx_v, MB + 48, AB + 83, VECc * c + 43, VECc * c + 45)

    def em_sc(sc):
        sc.wait_ge(s_ld, 112)
        sc.copy(ones[:], cst[:, 3:4].broadcast_to([128, 2])).then_inc(s_ac, 1)
        for c in range(NCH):
            AB = 1 + ACCc * c
            MB = MMCc * c
            sc.wait_ge(s_ve, VECc * c + 1)
            sc.activation(sq[:], sx[:], AF.Square).then_inc(s_ac, 1)
            sc.wait_ge(s_mm, MB + 2)
            sc.activation(stat[0:1, 0:TC], psm[0:1, :], AF.Identity, scale=cst[0:1, 7:8])
            sc.activation(stat[0:1, TC:2 * TC], psq[0:1, :], AF.Identity,
                          scale=cst[0:1, 7:8]).then_inc(s_ac, 1)
            sc.wait_ge(s_ve, VECc * c + 2)
            sc.activation(stat[0:1, 5 * TC:6 * TC], stat[0:1, 4 * TC:5 * TC], AF.Sqrt,
                          bias=cst[0:1, 6:7]).then_inc(s_ac, 1)
            sc.wait_ge(s_mm, MB + 4)
            sc.copy(mb[:], psbc[:, 0:TC])
            sc.copy(rb[:], psbc[:, TC:2 * TC]).then_inc(s_ac, 1)
            sc.wait_ge(s_ve, VECc * c + 4)
            for r in range(8):
                a = sc.activation(hb_v[:, r], hb_v[:, r], AF.Identity,
                                  scale=g1s[:, r:r + 1], bias=be1s[:, r:r + 1])
            a.then_inc(s_ac, 1)
            for m in range(32):
                sc.wait_ge(s_mm, MB + 4 + m + 1)
                if m >= 2:
                    sc.wait_ge(s_ve, VECc * c + 5 + m)
                sc.activation(vb_v[:, m % 2], ps[m % 4][:], AF.Identity,
                              bias=b1s[:, m:m + 1]).then_inc(s_ac, 1)
                sc.wait_ge(s_ve, VECc * c + 6 + m)
                sc.activation(thb[:], v2b[:], AF.Tanh,
                              scale=cst[:, 5:6]).then_inc(s_ac, 1)
            for m in range(8):
                sc.wait_ge(s_mm, MB + 36 + m + 1)
                sc.activation(sx_v[:, m], ps[m % 4][:], AF.Identity,
                              bias=b2s[:, m:m + 1]).then_inc(s_ac, 1)
            sc.wait_ge(s_ve, VECc * c + 39)
            sc.activation(sq[:], sx[:], AF.Square).then_inc(s_ac, 1)
            sc.wait_ge(s_mm, MB + 46)
            sc.activation(stat[0:1, 0:TC], psm[0:1, :], AF.Identity, scale=cst[0:1, 7:8])
            sc.activation(stat[0:1, TC:2 * TC], psq[0:1, :], AF.Identity,
                          scale=cst[0:1, 7:8]).then_inc(s_ac, 1)
            sc.wait_ge(s_ve, VECc * c + 40)
            sc.activation(stat[0:1, 5 * TC:6 * TC], stat[0:1, 4 * TC:5 * TC], AF.Sqrt,
                          bias=cst[0:1, 6:7]).then_inc(s_ac, 1)
            sc.wait_ge(s_mm, MB + 48)
            sc.copy(mb[:], psbc[:, 0:TC])
            sc.copy(rb[:], psbc[:, TC:2 * TC]).then_inc(s_ac, 1)
            sc.wait_ge(s_ve, VECc * c + 42)
            for r in range(8):
                a = sc.activation(sx_v[:, r], sx_v[:, r], AF.Identity,
                                  scale=g2s[:, r:r + 1], bias=be2s[:, r:r + 1])
            a.then_inc(s_ac, 1)
            sc.wait_ge(s_ve, VECc * c + 43)
            sc.activation(sq[:], sx[:], AF.Square).then_inc(s_ac, 1)
            sc.wait_ge(s_mm, MB + 50)
            sc.activation(stat[0:1, 0:TC], psm[0:1, :], AF.Identity, scale=cst[0:1, 7:8])
            sc.activation(stat[0:1, TC:2 * TC], psq[0:1, :], AF.Identity,
                          scale=cst[0:1, 7:8]).then_inc(s_ac, 1)
            sc.wait_ge(s_ve, VECc * c + 44)
            sc.activation(stat[0:1, 5 * TC:6 * TC], stat[0:1, 4 * TC:5 * TC], AF.Sqrt,
                          bias=cst[0:1, 6:7]).then_inc(s_ac, 1)
            sc.wait_ge(s_mm, MB + 52)
            sc.copy(mb[:], psbc[:, 0:TC])
            sc.copy(rb[:], psbc[:, TC:2 * TC]).then_inc(s_ac, 1)

    def em_v(v):
        for c in range(NCH):
            AB = 1 + ACCc * c
            v.wait_ge(s_ld, 112 + 16 * (LDC * c + 2))
            if c >= 1:
                v.wait_ge(s_ac, 1 + ACCc * (c - 1) + 86)
            v.tensor_tensor(sx_v[:], x2c_v[:], rtc_v[:], ALU.add).then_inc(s_ve, 1)
            v.wait_ge(s_ac, AB + 2)
            v.tensor_tensor(stat[0:1, 4 * TC:5 * TC], stat[0:1, 0:TC], stat[0:1, 0:TC], ALU.mult)
            v.scalar_tensor_tensor(stat[0:1, 4 * TC:5 * TC], stat[0:1, TC:2 * TC], 1.0,
                                   stat[0:1, 4 * TC:5 * TC], ALU.mult, ALU.subtract
                                   ).then_inc(s_ve, 1)
            v.wait_ge(s_ac, AB + 3)
            v.reciprocal(stat[0:1, 3 * TC:4 * TC], stat[0:1, 5 * TC:6 * TC]).then_inc(s_ve, 1)
            v.wait_ge(s_ac, AB + 4)
            for k in range(8):
                v.tensor_tensor(hb_v[:, k], sx_v[:, k], mb[:], ALU.subtract)
                last = v.tensor_tensor(hb_v[:, k], hb_v[:, k], rb[:], ALU.mult)
            last.then_inc(s_ve, 1)
            v.wait_ge(s_ac, AB + 5)
            v.tensor_copy(h16[:], hb[:]).then_inc(s_ve, 1)
            for m in range(32):
                v.wait_ge(s_ac, AB + 6 + 2 * m)
                vv = vb_v[:, m % 2]
                v.tensor_tensor(v2b[:], vv, vv, ALU.mult)
                v.tensor_tensor(v2b[:], v2b[:], vv, ALU.mult)
                v.scalar_tensor_tensor(v2b[:], v2b[:], GELU_TANH_C, vv, ALU.mult,
                                       ALU.add).then_inc(s_ve, 1)
                v.wait_ge(s_ac, AB + 7 + 2 * m)
                v.tensor_scalar(thb[:], thb[:], 1.0, 0.5, ALU.add, ALU.mult)
                gtt = v.tensor_tensor(gb_v[:, m], vv, thb[:], ALU.mult)
            gtt.then_inc(s_ve, 1)
            v.wait_ge(s_ac, AB + 77)
            v.tensor_tensor(sx_v[:], hb_v[:], sx_v[:], ALU.add).then_inc(s_ve, 1)
            v.wait_ge(s_ac, AB + 79)
            v.tensor_tensor(stat[0:1, 4 * TC:5 * TC], stat[0:1, 0:TC], stat[0:1, 0:TC], ALU.mult)
            v.scalar_tensor_tensor(stat[0:1, 4 * TC:5 * TC], stat[0:1, TC:2 * TC], 1.0,
                                   stat[0:1, 4 * TC:5 * TC], ALU.mult, ALU.subtract
                                   ).then_inc(s_ve, 1)
            v.wait_ge(s_ac, AB + 80)
            v.reciprocal(stat[0:1, 3 * TC:4 * TC], stat[0:1, 5 * TC:6 * TC]).then_inc(s_ve, 1)
            v.wait_ge(s_ac, AB + 81)
            for k in range(8):
                v.tensor_tensor(sx_v[:, k], sx_v[:, k], mb[:], ALU.subtract)
                last = v.tensor_tensor(sx_v[:, k], sx_v[:, k], rb[:], ALU.mult)
            last.then_inc(s_ve, 1)
            v.wait_ge(s_ac, AB + 82)
            v.tensor_tensor(sx_v[:], x2c_v[:], sx_v[:], ALU.add).then_inc(s_ve, 1)
            v.wait_ge(s_ac, AB + 84)
            v.tensor_tensor(stat[0:1, 4 * TC:5 * TC], stat[0:1, 0:TC], stat[0:1, 0:TC], ALU.mult)
            v.scalar_tensor_tensor(stat[0:1, 4 * TC:5 * TC], stat[0:1, TC:2 * TC], 1.0,
                                   stat[0:1, 4 * TC:5 * TC], ALU.mult, ALU.subtract
                                   ).then_inc(s_ve, 1)
            v.wait_ge(s_ac, AB + 85)
            v.reciprocal(stat[0:1, 3 * TC:4 * TC], stat[0:1, 5 * TC:6 * TC]).then_inc(s_ve, 1)
            v.wait_ge(s_ac, AB + 86)
            if c >= 1:
                v.wait_ge(s_st, 16 * c)
            for k in range(8):
                v.tensor_tensor(ost_v[:, k], sx_v[:, k], mb[:], ALU.subtract)
                last = v.tensor_tensor(ost_v[:, k], ost_v[:, k], rb[:], ALU.mult)
            last.then_inc(s_ve, 1)

    def em_sy(sy):
        for c in range(NCH):
            sy.wait_ge(s_ve, VECc * c + 46)
            sy.dma_start(out=x3_o[:, c * TC:(c + 1) * TC].rearrange("(r p) t -> p r t", p=128),
                         in_=ost_v).then_inc(s_st, 16)
        sy.wait_ge(s_st, 16 * NCH)

    finals = {"ld": 16 * (7 + LDC * NCH), "mm": MMCc * NCH, "ac": 1 + ACCc * NCH,
              "ve": VECc * NCH, "st": 16 * NCH}
    return {"g": em_g, "t": em_t, "sc": em_sc, "v": em_v, "sy": em_sy}, finals


class KSched:
    def __init__(self):
        self.steps = []
        self.cnt = {}
        self.marks = {}

    def add(self, eng, fn, waits=(), mark=None, sem=None, step=1):
        if mark is not None:
            assert sem is not None
            v = self.cnt.get(sem, 0) + step
            self.cnt[sem] = v
            assert mark not in self.marks, mark
            self.marks[mark] = (sem, v)
        for w in waits:
            assert w in self.marks, f"wait on undefined mark {w}"
        self.steps.append((eng, tuple(waits), fn, mark, sem, step))

    def emit_for(self, name, engobj, sems):
        for eng, waits, fn, mark, sem, step in self.steps:
            if eng != name:
                continue
            for w in waits:
                ws, wv = self.marks[w]
                engobj.wait_ge(sems[ws], wv)
            inst = fn(engobj)
            if mark is not None:
                assert inst is not None, f"step {mark} returned no instruction"
                inst.then_inc(sems[sem], step)

    def final(self, sem):
        return self.cnt.get(sem, 0)


def phase_moe(nc, st, d, sm, tag, TC=256, TOK=None):
    TOK = TOK_PC if TOK is None else TOK
    NCH = TOK // TC
    NSB = TC // 128
    x3_h = d["x3_d"]
    gw_h = d["gate_w_d"]
    gb_h = d["gate_biasR"]
    wg_h = d["wgb"]
    wu_h = d["wub"]
    wd_h = d["wdb"]
    sg_h = d["sw_gate"]
    su_h = d["sw_up"]
    sd_h = d["sw_down"]
    ml4_h = d["ml4_d"]
    ml8_h = d["ml8_d"]
    idf_h = d["idf_d"]
    cst_h = d["cst_d"]
    x4_o = d["x4"]
    w8d = d["w8d"]

    swg = st.enter_context(nc.sbuf_tensor(tag + "swg", [128, 8 * 1024], BF16))
    swu = st.enter_context(nc.sbuf_tensor(tag + "swu", [128, 8 * 1024], BF16))
    swd = st.enter_context(nc.sbuf_tensor(tag + "swd", [128, 8 * 1024], BF16))
    gw = st.enter_context(nc.sbuf_tensor(tag + "gw", [128, 8 * 8], F32))
    gbias = st.enter_context(nc.sbuf_tensor(tag + "gbias", [128, 8], F32))
    ml4 = st.enter_context(nc.sbuf_tensor(tag + "ml4", [128, 16], F32))
    ml8 = st.enter_context(nc.sbuf_tensor(tag + "ml8", [128, 64], F32))
    idf = st.enter_context(nc.sbuf_tensor(tag + "idf", [128, 128], F32))
    cst = st.enter_context(nc.sbuf_tensor(tag + "cst", [128, 8], F32))
    ones = st.enter_context(nc.sbuf_tensor(tag + "ones", [128, 2], F32))
    x3c = st.enter_context(nc.sbuf_tensor(tag + "x3c", [128, 8 * TC], F32))
    x16 = st.enter_context(nc.sbuf_tensor(tag + "x16", [128, 8 * TC], BF16))
    web = st.enter_context(nc.sbuf_tensor(tag + "web", [128, 2 * 12 * 1024], BF16))
    gt = st.enter_context(nc.sbuf_tensor(tag + "gt", [128, 512], F32))
    w8s = st.enter_context(nc.sbuf_tensor(tag + "w8s", [128, 8 * NSB], F32))
    w8T = st.enter_context(nc.sbuf_tensor(tag + "w8T", [8, TC], F32))
    w8bc = st.enter_context(nc.sbuf_tensor(tag + "w8bc", [128, 8 * TC], F32))
    shg = st.enter_context(nc.sbuf_tensor(tag + "shg", [128, 8 * TC], F32))
    sh16 = st.enter_context(nc.sbuf_tensor(tag + "sh16", [128, 8 * TC], BF16))
    ge32 = st.enter_context(nc.sbuf_tensor(tag + "ge32", [128, 4 * TC], F32))
    ysb = st.enter_context(nc.sbuf_tensor(tag + "ysb", [128, 8 * TC], F32))
    gu16 = st.enter_context(nc.sbuf_tensor(tag + "gu16", [128, 2 * 4 * TC], BF16))
    sx = st.enter_context(nc.sbuf_tensor(tag + "sx", [128, 8 * TC], F32))
    sq = st.enter_context(nc.sbuf_tensor(tag + "sq", [128, 8 * TC], F32))
    stat = st.enter_context(nc.sbuf_tensor(tag + "stat", [1, 6 * TC], F32))
    mbx = st.enter_context(nc.sbuf_tensor(tag + "mbx", [128, TC], F32))
    rbx = st.enter_context(nc.sbuf_tensor(tag + "rbx", [128, TC], F32))
    ost = st.enter_context(nc.sbuf_tensor(tag + "ost", [128, 8 * TC], F32))
    psA = st.enter_context(nc.psum_tensor(tag + "psA", [128, 512], F32))
    psB = st.enter_context(nc.psum_tensor(tag + "psB", [128, 512], F32))
    psbc = st.enter_context(nc.psum_tensor(tag + "psbc", [128, 2 * TC], F32))
    psgT_t = st.enter_context(nc.psum_tensor(tag + "psgT", [128, 136], F32))
    psg = psgT_t[:, 0:8]
    psT = psgT_t[0:8, 8:136]
    psmq_t = st.enter_context(nc.psum_tensor(tag + "psmq", [1, 2 * TC], F32))
    psm = psmq_t[0:1, 0:TC]
    psq = psmq_t[0:1, TC:2 * TC]
    sems = {"s_ld": sm["ld"], "s_ve": sm["ve"], "s_mm": sm["mm"],
            "s_ac": sm["ac"], "s_st": sm["st"]}

    swg_v = swg[:].rearrange("p (k m) -> p k m", k=8)
    swu_v = swu[:].rearrange("p (k m) -> p k m", k=8)
    swd_v = swd[:].rearrange("p (k m) -> p k m", k=8)
    gw_v = gw[:].rearrange("p (k e) -> p k e", k=8)
    x3c_v = x3c[:].rearrange("p (r t) -> p r t", r=8)
    x16_v = x16[:].rearrange("p (r t) -> p r t", r=8)
    web_v = web[:].rearrange("p (d c) -> p d c", d=2)
    shg_v = shg[:].rearrange("p (m t) -> p m t", m=8)
    sh16_v = sh16[:].rearrange("p (m t) -> p m t", m=8)
    ge_v = ge32[:].rearrange("p (m t) -> p m t", m=4)
    gu16_v = gu16[:].rearrange("p (d m t) -> p d m t", d=2, m=4)
    w8bc_v = w8bc[:].rearrange("p (e t) -> p e t", e=8)
    w8s_v = w8s[:].rearrange("p (s e) -> p s e", s=NSB)
    ysb_v = ysb[:].rearrange("p (m t) -> p m t", m=8)
    sx_v = sx[:].rearrange("p (r t) -> p r t", r=8)
    sq_v = sq[:].rearrange("p (r t) -> p r t", r=8)
    ost_v = ost[:].rearrange("p (r t) -> p r t", r=8)

    S = KSched()
    A = S.add
    for i, (w, dst) in enumerate([(cst_h, cst), (gb_h, gbias), (ml4_h, ml4),
                                  (ml8_h, ml8), (idf_h, idf)]):
        A("g", (lambda w=w, dst=dst: (lambda g: g.dma_start(out=dst[:], in_=w)))(),
          mark=f"up{i}", sem="s_ld", step=16)
    A("g", lambda g: g.dma_start(out=gw[:], in_=gw_h),
      mark="up5", sem="s_ld", step=16)
    A("g", lambda g: g.dma_start(out=swg[:], in_=sg_h),
      mark="up6", sem="s_ld", step=16)
    A("g", lambda g: g.dma_start(out=swu[:], in_=su_h),
      mark="up7", sem="s_ld", step=16)
    A("g", lambda g: g.dma_start(out=swd[:], in_=sd_h),
      mark="up8", sem="s_ld", step=16)
    A("sc", lambda sc: sc.copy(ones[:], cst[:, 3:4].broadcast_to([128, 2])),
      waits=["up8"], mark="ones", sem="s_ac")

    for c in range(NCH):
        P = f"c{c}_"
        sl = slice(c * TC, (c + 1) * TC)
        A("g", (lambda sl=sl: (lambda g: g.dma_start(
            out=x3c_v, in_=x3_h[:, sl].rearrange("(r p) t -> p r t", p=128))))(),
          waits=([f"c{c-1}_xhat"] if c else []), mark=P + "ldx", sem="s_ld", step=16)
        A("v", lambda v: v.tensor_copy(x16[:], x3c[:]),
          waits=[P + "ldx"] + ([f"c{c-1}_resid"] if c else []),
          mark=P + "cast", sem="s_ve")
        for s in range(NSB):
            Q = P + f"g{s}_"
            def gmm(s=s):
                def f(t):
                    for k in range(8):
                        mm = t.matmul(psg, x3c_v[:, k, s * 128:(s + 1) * 128], gw_v[:, k],
                                      start=(k == 0), stop=(k == 7))
                    return mm
                return f
            A("t", gmm(), waits=[P + "ldx", "up5"] +
              ([P + f"g{s-1}_sig"] if s else ([f"c{c-1}_g{NSB-1}_w8T"] if c else [])),
              mark=Q + "mm", sem="s_mm")
            A("sc", (lambda s=s: (lambda sc: sc.activation(
                w8s_v[:, s], psg, AF.Sigmoid)))(),
              waits=[Q + "mm"], mark=Q + "sig", sem="s_ac")
        def gchain2():
            def f(v):
                G = [gt[:, 256 * s:256 * (s + 1)] for s in range(NSB)]
                last = None
                def both(fn):
                    nonlocal last
                    for s in range(NSB):
                        last = fn(G[s], s)
                both(lambda g, s: v.tensor_tensor(g[:, 0:8], w8s_v[:, s], gbias[:], ALU.add))
                both(lambda g, s: v.tensor_tensor(
                    g[:, 8:12], g[:, 0:8].rearrange("p (j i) -> p j i", j=4)[:, :, 0],
                    g[:, 0:8].rearrange("p (j i) -> p j i", j=4)[:, :, 1], ALU.add))
                def cmp4(g, s):
                    gs = g[:, 8:12]
                    v.tensor_tensor(g[:, 16:32].rearrange("p (j jp) -> p j jp", j=4),
                                    gs.unsqueeze(1).broadcast_to([128, 4, 4]),
                                    gs.unsqueeze(2).broadcast_to([128, 4, 4]), ALU.is_gt)
                    return v.tensor_tensor(
                        g[:, 32:48].rearrange("p (j jp) -> p j jp", j=4),
                        gs.unsqueeze(1).broadcast_to([128, 4, 4]),
                        gs.unsqueeze(2).broadcast_to([128, 4, 4]), ALU.is_equal)
                both(cmp4)
                both(lambda g, s: v.tensor_tensor(
                    g[:, 32:48], g[:, 32:48], ml4[:], ALU.mult))
                both(lambda g, s: v.tensor_tensor(g[:, 16:32], g[:, 16:32], g[:, 32:48],
                                                  ALU.add))
                both(lambda g, s: v.tensor_reduce(
                    g[:, 48:52], g[:, 16:32].rearrange("p (j jp) -> p j jp", j=4),
                    AXL.X, ALU.add))
                both(lambda g, s: v.tensor_scalar(g[:, 52:56], g[:, 48:52], 2.0, None,
                                                  ALU.is_lt))
                both(lambda g, s: v.tensor_tensor(
                    g[:, 56:64].rearrange("p (j i) -> p j i", j=4),
                    g[:, 0:8].rearrange("p (j i) -> p j i", j=4),
                    g[:, 52:56].unsqueeze(2).broadcast_to([128, 4, 2]), ALU.mult))
                def cmp8(g, s):
                    tm = g[:, 56:64]
                    v.tensor_tensor(g[:, 64:128].rearrange("p (e ep) -> p e ep", e=8),
                                    tm.unsqueeze(1).broadcast_to([128, 8, 8]),
                                    tm.unsqueeze(2).broadcast_to([128, 8, 8]), ALU.is_gt)
                    return v.tensor_tensor(
                        g[:, 128:192].rearrange("p (e ep) -> p e ep", e=8),
                        tm.unsqueeze(1).broadcast_to([128, 8, 8]),
                        tm.unsqueeze(2).broadcast_to([128, 8, 8]), ALU.is_equal)
                both(cmp8)
                both(lambda g, s: v.tensor_tensor(g[:, 128:192], g[:, 128:192], ml8[:],
                                                  ALU.mult))
                both(lambda g, s: v.tensor_tensor(g[:, 64:128], g[:, 64:128],
                                                  g[:, 128:192], ALU.add))
                both(lambda g, s: v.tensor_reduce(
                    g[:, 192:200], g[:, 64:128].rearrange("p (e ep) -> p e ep", e=8),
                    AXL.X, ALU.add))
                both(lambda g, s: v.tensor_scalar(g[:, 200:208], g[:, 192:200], 2.0, None,
                                                  ALU.is_lt))
                both(lambda g, s: v.tensor_tensor(g[:, 208:216], g[:, 200:208],
                                                  g[:, 0:8], ALU.mult))
                both(lambda g, s: v.tensor_reduce(g[:, 216:217],
                                                  g[:, 208:216].unsqueeze(1), AXL.X,
                                                  ALU.add))
                both(lambda g, s: v.tensor_scalar(g[:, 216:217], g[:, 216:217], 1e-20,
                                                  None, ALU.add))
                both(lambda g, s: v.reciprocal(g[:, 217:218], g[:, 216:217]))
                both(lambda g, s: v.tensor_scalar(w8s_v[:, s], g[:, 208:216],
                                                  g[:, 217:218], 2.5, ALU.mult, ALU.mult))
                return last
            return f
        A("v", gchain2(), waits=[P + f"g{NSB-1}_sig", "up8"], mark=P + "w8all", sem="s_ve")
        for s in range(NSB):
            Q = P + f"g{s}_"
            A("t", (lambda s=s: (lambda t: t.transpose(psT, w8s_v[:, s], idf[:])))(),
              waits=[P + "w8all"] + ([P + f"g{s-1}_w8T"] if s else ([f"c{c-1}_g{NSB-1}_w8T"] if c else [])),
              mark=Q + "tr", sem="s_mm")
            A("sc", (lambda s=s: (lambda sc: sc.copy(w8T[:, s * 128:(s + 1) * 128], psT)))(),
              waits=[Q + "tr"], mark=Q + "w8T", sem="s_ac")
        A("sy", lambda sy: sy.dma_start(out=w8d, in_=w8T[:]),
          waits=[P + f"g{NSB-1}_w8T"] + ([f"c{c-1}_ldw8"] if c else []),
          mark=P + "stw8", sem="s_st", step=16)
        A("g", lambda g: g.dma_start(out=w8bc_v, in_=w8d.unsqueeze(0).broadcast_to([128, 8, TC])),
          waits=[P + "stw8"] + ([f"c{c-1}_e7_guTT"] if c else []),
          mark=P + "ldw8", sem="s_ld", step=16)
        bankq = [None, None]

        def bwait(par, extra=()):
            w = list(extra)
            if bankq[par] is not None:
                w.append(bankq[par])
            return w

        def kmm(bank, wv, kk, rhs_v, mslice):
            def f(t):
                for k in range(kk):
                    mm = t.matmul(bank[:, 0:TC], wv[:, k, mslice], rhs_v[:, k],
                                  start=(k == 0), stop=(k == kk - 1))
                return mm
            return f
        ti = 0
        for m in range(8):
            Q = P + f"sh{m}_"
            pg, pu = ti % 2, (ti + 1) % 2
            A("t", kmm([psA, psB][pg], swg_v, 8, x16_v, slice(m * 128, (m + 1) * 128)),
              waits=bwait(pg, [P + "cast", "up8"]), mark=Q + "mmg", sem="s_mm")
            A("sc", (lambda m=m, pg=pg: (lambda sc: sc.activation(
                shg_v[:, m], [psA, psB][pg][:, 0:TC], AF.Silu)))(),
              waits=[Q + "mmg"], mark=Q + "evg", sem="s_ac")
            bankq[pg] = Q + "evg"
            A("t", kmm([psA, psB][pu], swu_v, 8, x16_v, slice(m * 128, (m + 1) * 128)),
              waits=bwait(pu), mark=Q + "mmu", sem="s_mm")
            A("v", (lambda m=m, pu=pu: (lambda v: v.tensor_tensor(
                sh16_v[:, m], shg_v[:, m], [psA, psB][pu][:, 0:TC], ALU.mult)))(),
              waits=[Q + "mmu", Q + "evg"], mark=Q + "sh16", sem="s_ve")
            bankq[pu] = Q + "sh16"
            ti += 2
        for m in range(8):
            Q = P + f"sd{m}_"
            pp = ti % 2
            A("t", kmm([psA, psB][pp], swd_v, 8, sh16_v, slice(m * 128, (m + 1) * 128)),
              waits=bwait(pp, [P + "sh7_sh16"]), mark=Q + "mm", sem="s_mm")
            A("sc", (lambda m=m, pp=pp: (lambda sc: sc.copy(
                ysb_v[:, m], [psA, psB][pp][:, 0:TC])))(),
              waits=[Q + "mm"] + ([f"c{c-1}_resid"] if c and m == 0 else []),
              mark=Q + "ev", sem="s_ac")
            bankq[pp] = Q + "ev"
            ti += 1
        for e in range(8):
            Q = P + f"e{e}_"
            d = e % 2
            ei = c * 8 + e
            wts = []
            if ei >= 2:
                wts.append(f"c{(ei-2)//8}_e{(ei-2)%8}_dnlast")
            A("g", (lambda e=e, d=d: (lambda g: g.dma_start(
                out=web[:, d * 12288:d * 12288 + 4096], in_=wg_h[e])))(),
              waits=wts, mark=Q + "lg", sem="s_ld", step=16)
            A("g", (lambda e=e, d=d: (lambda g: g.dma_start(
                out=web[:, d * 12288 + 4096:d * 12288 + 8192], in_=wu_h[e])))(),
              mark=Q + "lu", sem="s_ld", step=16)
            A("g", (lambda e=e, d=d: (lambda g: g.dma_start(
                out=web[:, d * 12288 + 8192:d * 12288 + 12288], in_=wd_h[e])))(),
              mark=Q + "ld", sem="s_ld", step=16)
            wgv = web_v[:, d, 0:8 * 512].rearrange("p (k m) -> p k m", k=8)
            wuv = web_v[:, d, 8 * 512:16 * 512].rearrange("p (k m) -> p k m", k=8)
            wdv = web_v[:, d, 16 * 512:16 * 512 + 4 * 1024].rearrange("p (k m) -> p k m", k=4)
            for m in range(4):
                R = Q + f"m{m}_"
                pg, pu = ti % 2, (ti + 1) % 2
                A("t", kmm([psA, psB][pg], wgv, 8, x16_v, slice(m * 128, (m + 1) * 128)),
                  waits=bwait(pg, [Q + "lg"]), mark=R + "mmg", sem="s_mm")
                gw_wts = [R + "mmg"]
                if m == 0 and e >= 1:
                    gw_wts.append(P + f"e{e-1}_guTT")
                elif m == 0 and c >= 1:
                    gw_wts.append(f"c{c-1}_e7_guTT")
                A("sc", (lambda m=m, pg=pg: (lambda sc: sc.activation(
                    ge_v[:, m], [psA, psB][pg][:, 0:TC], AF.Silu)))(),
                  waits=gw_wts, mark=R + "evg", sem="s_ac")
                bankq[pg] = R + "evg"
                A("t", kmm([psA, psB][pu], wuv, 8, x16_v, slice(m * 128, (m + 1) * 128)),
                  waits=bwait(pu, [Q + "lu"]), mark=R + "mmu", sem="s_mm")
                A("v", (lambda m=m, pu=pu: (lambda v: v.tensor_tensor(
                    ge_v[:, m], ge_v[:, m], [psA, psB][pu][:, 0:TC], ALU.mult)))(),
                  waits=[R + "mmu", R + "evg"], mark=R + "gu", sem="s_ve")
                bankq[pu] = R + "gu"
                ti += 2
            gu_wts = [Q + "m3_gu", P + "ldw8"]
            ei2 = c * 8 + e - 2
            if ei2 >= 0:
                gu_wts.append(f"c{ei2//8}_e{ei2%8}_dnlast")
            A("v", (lambda e=e, d=d: (lambda v: v.tensor_tensor(
                gu16_v[:, d], ge_v,
                w8bc_v[:, e].unsqueeze(1).broadcast_to([128, 4, TC]), ALU.mult)))(),
              waits=gu_wts, mark=Q + "guTT", sem="s_ve")
            for m in range(8):
                R = Q + f"d{m}_"
                pp = ti % 2
                A("t", kmm([psA, psB][pp], wdv, 4, gu16_v[:, d], slice(m * 128, (m + 1) * 128)),
                  waits=bwait(pp, [Q + "guTT", Q + "ld"] +
                              ([P + f"sd{m}_ev"] if e == 0 else
                               [P + f"e{e-1}_d{m}_add"])),
                  mark=R + "mm", sem="s_mm")
                A("v", (lambda m=m, pp=pp: (lambda v: v.tensor_tensor(
                    ysb_v[:, m], ysb_v[:, m], [psA, psB][pp][:, 0:TC], ALU.add)))(),
                  waits=[R + "mm"], mark=R + "add", sem="s_ve")
                bankq[pp] = R + "add"
                ti += 1
            A("v", lambda v: v.nop(), waits=[Q + "d7_add"],
              mark=Q + "dnlast", sem="s_ve")
        A("v", lambda v: v.tensor_tensor(sx_v[:], ysb_v[:], x3c_v[:], ALU.add),
          waits=[P + "e7_dnlast"] + ([f"c{c-1}_sqr", f"c{c-1}_psmm"] if c else []),
          mark=P + "resid", sem="s_ve")
        A("sc", lambda sc: sc.activation(sq[:], sx[:], AF.Square),
          waits=[P + "resid"], mark=P + "sqr", sem="s_ac")
        def stats_mm(which, src_v):
            def f(t):
                tgt = psm if which == 'm' else psq
                for k in range(8):
                    mm = t.matmul(tgt, ones[:, 0:1], src_v[:, k], start=(k == 0),
                                  stop=(k == 7))
                return mm
            return f
        A("t", stats_mm("m", sx_v), waits=[P + "resid", "ones"] +
          ([f"c{c-1}_stats"] if c else []), mark=P + "psmm", sem="s_mm")
        A("t", stats_mm("q", sq_v), waits=[P + "sqr"], mark=P + "psmq", sem="s_mm")
        A("sc", lambda sc: (sc.activation(stat[0:1, 0:TC], psm, AF.Identity,
                                          scale=cst[0:1, 7:8]),
                            sc.activation(stat[0:1, TC:2 * TC], psq, AF.Identity,
                                          scale=cst[0:1, 7:8]))[-1],
          waits=[P + "psmm", P + "psmq"], mark=P + "stats", sem="s_ac")
        A("v", lambda v: (v.tensor_tensor(stat[0:1, 4 * TC:5 * TC], stat[0:1, 0:TC],
                                          stat[0:1, 0:TC], ALU.mult),
                          v.scalar_tensor_tensor(stat[0:1, 4 * TC:5 * TC],
                                                 stat[0:1, TC:2 * TC], 1.0,
                                                 stat[0:1, 4 * TC:5 * TC], ALU.mult,
                                                 ALU.subtract))[-1],
          waits=[P + "stats"], mark=P + "var", sem="s_ve")
        A("sc", lambda sc: sc.activation(stat[0:1, 5 * TC:6 * TC], stat[0:1, 4 * TC:5 * TC],
                                         AF.Sqrt, bias=cst[0:1, 6:7]),
          waits=[P + "var"], mark=P + "sqrt", sem="s_ac")
        A("v", lambda v: v.reciprocal(stat[0:1, 3 * TC:4 * TC], stat[0:1, 5 * TC:6 * TC]),
          waits=[P + "sqrt"], mark=P + "rstd", sem="s_ve")
        def bcast_mm(t):
            t.matmul(psbc[:, 0:TC], ones[0:1, 0:1].broadcast_to([1, 128]), stat[0:1, 0:TC],
                     start=True, stop=True)
            return t.matmul(psbc[:, TC:2 * TC], ones[0:1, 0:1].broadcast_to([1, 128]),
                            stat[0:1, 3 * TC:4 * TC], start=True, stop=True)
        A("t", bcast_mm, waits=[P + "rstd"] + ([f"c{c-1}_bcev"] if c else []),
          mark=P + "bc", sem="s_mm")
        A("sc", lambda sc: (sc.copy(mbx[:], psbc[:, 0:TC]),
                            sc.copy(rbx[:], psbc[:, TC:2 * TC]))[-1],
          waits=[P + "bc"], mark=P + "bcev", sem="s_ac")
        def xhat(v):
            for k in range(8):
                v.tensor_tensor(ost_v[:, k], sx_v[:, k], mbx[:], ALU.subtract)
                last = v.tensor_tensor(ost_v[:, k], ost_v[:, k], rbx[:], ALU.mult)
            return last
        A("v", xhat, waits=[P + "bcev"] + ([f"c{c-1}_st"] if c else []),
          mark=P + "xhat", sem="s_ve")
        A("sy", (lambda sl=sl: (lambda sy: sy.dma_start(
            out=x4_o[:, sl].rearrange("(r p) t -> p r t", p=128), in_=ost_v)))(),
          waits=[P + "xhat"], mark=P + "st", sem="s_st", step=16)

    def em_g(g):
        S.emit_for("g", g, sems)

    def em_t(t):
        S.emit_for("t", t, sems)

    def em_sc(sc):
        S.emit_for("sc", sc, sems)

    def em_v(v):
        S.emit_for("v", v, sems)

    def em_sy(sy):
        S.emit_for("sy", sy, sems)
        sy.wait_ge(sems["s_st"], S.final("s_st"))

    finals = {"ld": S.final("s_ld"), "mm": S.final("s_mm"), "ac": S.final("s_ac"),
              "ve": S.final("s_ve"), "st": S.final("s_st")}
    return {"g": em_g, "t": em_t, "sc": em_sc, "v": em_v, "sy": em_sy}, finals


def phase_compact(nc, st, d, sm, tag, triples=()):
    s_ld, s_ve, s_ac, s_st = sm["ld"], sm["ve"], sm["ac"], sm["st"]
    CT = 2048
    msk = st.enter_context(nc.sbuf_tensor(tag + "msk", [128, 2], F32))
    sbA = st.enter_context(nc.sbuf_tensor(tag + "sbA", [128, 2 * CT], F32))
    sbB = st.enter_context(nc.sbuf_tensor(tag + "sbB", [128, 2 * CT], F32))
    sbO = st.enter_context(nc.sbuf_tensor(tag + "sbO", [128, 2 * CT], F32))
    msk_h = d["msk_d"]
    tiles = []
    for (Aap, Bap, Oap) in triples:
        R, C = Oap.shape
        ct = min(CT, C)
        assert R % 128 == 0 and C % ct == 0, (R, C)
        for r in range(R // 128):
            for cc in range(C // ct):
                rs = slice(r * 128, (r + 1) * 128)
                csl = slice(cc * ct, cc * ct + ct)
                tiles.append((Aap[rs, csl], Bap[rs, csl], Oap[rs, csl], ct))
    N = len(tiles)

    def em_g(g):
        g.dma_start(out=msk[:], in_=msk_h).then_inc(s_ld, 16)
        for i, (a, b, o, w) in enumerate(tiles):
            db = i % 2
            if i >= 2:
                g.wait_ge(s_ve, i - 1)
            g.dma_start(out=sbA[:, db * CT:db * CT + w], in_=a).then_inc(s_ld, 16)
            g.dma_start(out=sbB[:, db * CT:db * CT + w], in_=b).then_inc(s_ld, 16)

    def em_sc(sc):
        for i, (a, b, o, w) in enumerate(tiles):
            db = i % 2
            sc.wait_ge(s_ld, 16 * (2 + 2 * i))
            if i >= 2:
                sc.wait_ge(s_st, 16 * (i - 1))
            sc.activation(sbO[:, db * CT:db * CT + w], sbA[:, db * CT:db * CT + w],
                          AF.Identity, scale=msk[:, 0:1]).then_inc(s_ac, 1)

    def em_v(v):
        for i, (a, b, o, w) in enumerate(tiles):
            db = i % 2
            v.wait_ge(s_ac, i + 1)
            v.wait_ge(s_ld, 16 * (3 + 2 * i))
            v.scalar_tensor_tensor(sbO[:, db * CT:db * CT + w],
                                   sbB[:, db * CT:db * CT + w], msk[:, 1:2],
                                   sbO[:, db * CT:db * CT + w],
                                   ALU.mult, ALU.add).then_inc(s_ve, 1)

    def em_sy(sy):
        for i, (a, b, o, w) in enumerate(tiles):
            sy.wait_ge(s_ve, i + 1)
            sy.dma_start(out=o, in_=sbO[:, (i % 2) * CT:(i % 2) * CT + w]
                         ).then_inc(s_st, 16)
        sy.wait_ge(s_st, 16 * N)

    finals = {"ld": 16 * (1 + 2 * N), "ac": N, "ve": N, "st": 16 * N}
    return {"g": em_g, "sc": em_sc, "v": em_v, "sy": em_sy}, finals


PAIRS = [[0, 1], [2, 3], [4, 5], [6, 7]]


def build_fused():
    nc = bass.Bass("TRN2", target_bir_lowering=False)
    dh = {}
    d = {}

    def decl(name, shape, kind=None, dt=F32):
        h = (nc.dram_tensor(name, shape, dt, kind=kind) if kind
             else nc.dram_tensor(name, shape, dt))
        dh[name] = h
        d[name] = h.ap()

    for name, shape in [
        ("xT", [SEQ // 256, 128, 8 * 256]), ("w_in_x", [128, 8 * D_INNER]),
        ("w_in_z", [128, 8 * D_HALF]), ("wcv_d", [128, 64]), ("bcv_d", [128, 16]),
        ("wxp_d", [128, 16 * 96]), ("wdt_d", [DT_RANK, D_HALF]), ("bdt_d", [128, 8]),
        ("cst_d", [128, 8]), ("msk_d", [128, 2]),
        ("alog", [128, 128]), ("dd", [128, 8]),
        ("w_out", [128, 16 * 1024]), ("xT_c", [D_MODEL, TOK_PC]),
        ("w1T", [2, 128, 8 * 2048]), ("w2T", [2, 128, 8 * 2048]),
        ("h1T", [8, 128, 32 * 512]), ("h2T", [8, 128, 32 * 512]),
        ("w3T", [2, 128, 32 * 512]),
        ("s1", [128, 32]), ("s2", [128, 32]), ("sh1", [128, 32]),
        ("sh2", [128, 32]), ("s3", [128, 8]),
        ("w_ret", [128, 8 * 1024]), ("b_ret", [128, 8]),
        ("W1", [128, 8 * 4096]), ("W2", [128, 32 * 1024]), ("b1", [128, 32]),
        ("b2", [128, 8]), ("g1", [128, 8]), ("be1", [128, 8]),
        ("g2", [128, 8]), ("be2", [128, 8]),
        ("gate_w_d", [128, 64]), ("gate_biasR", [128, 8]),
        ("w_gate", [8, 128, 8 * 512]), ("w_up", [8, 128, 8 * 512]),
        ("w_down", [8, 128, 4 * 1024]),
        ("sw_gate", [128, 8 * 1024]), ("sw_up", [128, 8 * 1024]),
        ("sw_down", [128, 8 * 1024]),
        ("ml4_d", [128, 16]), ("ml8_d", [128, 64]), ("idf_d", [128, 128]),
    ]:
        decl(name, shape, "ExternalInput")

    for name, shape in [
        ("xs_d", [D_HALF, SEQ]), ("dt_d", [D_HALF, SEQ]), ("bc_d", [96, SEQ]),
        ("zs_d", [D_HALF, SEQ]), ("yz_d", [D_HALF, SEQ]),
        ("yzag_d", [D_INNER, SEQ]), ("yzT_d", [D_INNER, TOK_PC]),
        ("x1_d", [D_MODEL, TOK_PC]), ("x2_d", [D_MODEL, TOK_PC]),
        ("sig_d", [D_MODEL, TOK_PC]), ("sigag_d", [2 * D_MODEL, TOK_PC]),
        ("sigb_d", [512, SEQ]), ("ret_d", [512, SEQ]),
        ("retag_d", [D_MODEL, SEQ]), ("retc_d", [D_MODEL, TOK_PC]),
        ("x3_d", [D_MODEL, TOK_PC]), ("w8d", [8, 256]),
    ]:
        decl(name, shape)

    for name, shape in [
        ("w1b", [2, 128, 8 * 2048]), ("w2b", [2, 128, 8 * 2048]),
        ("h1b", [8, 128, 32 * 512]), ("h2b", [8, 128, 32 * 512]),
        ("w3b", [2, 128, 32 * 512]),
        ("W1b", [128, 8 * 4096]), ("W2b", [128, 32 * 1024]),
        ("wgb", [8, 128, 8 * 512]), ("wub", [8, 128, 8 * 512]),
        ("wdb", [8, 128, 4 * 1024]),
    ]:
        decl(name, shape, dt=BF16)

    decl("x4", [D_MODEL, TOK_PC], "ExternalOutput")

    conv = []
    for s, t in [("W1", "W1b"), ("W2", "W2b")]:
        n = d[s].shape[-1] // 4096
        for i in range(n):
            conv.append((d[s][:, i * 4096:(i + 1) * 4096],
                         d[t][:, i * 4096:(i + 1) * 4096]))
    for s, t in [("w1T", "w1b"), ("w2T", "w2b"), ("w3T", "w3b"),
                 ("h1T", "h1b"), ("h2T", "h2b")]:
        ns = d[s].shape[0]
        n = d[s].shape[-1] // 4096
        for q in range(ns):
            for i in range(n):
                conv.append((d[s][q][:, i * 4096:(i + 1) * 4096],
                             d[t][q][:, i * 4096:(i + 1) * 4096]))
    for s, t in [("w_gate", "wgb"), ("w_up", "wub"), ("w_down", "wdb")]:
        for e in range(8):
            conv.append((d[s][e], d[t][e]))

    st = contextlib.ExitStack()
    s_cc = st.enter_context(nc.semaphore("s_cc"))
    recs = []

    def add_phase(tag, fn, **kw):
        sm = {n: st.enter_context(nc.semaphore(f"{tag}_{n}"))
              for n in ("ld", "ve", "mm", "ac", "st", "xa", "xb")}
        pst = contextlib.ExitStack()
        em, fin = fn(nc, pst, d, sm, tag + "_", **kw)
        pst.close()
        recs.append((em, {sm[k]: v for k, v in fin.items()}))

    ccn = [0]

    def add_cc(in_name, out_name):
        ccn[0] += 1
        n = ccn[0]
        in_h, out_h = dh[in_name], dh[out_name]

        def em_g(g, in_h=in_h, out_h=out_h, n=n):
            g.collective_compute(
                "AllGather", mybir.AluOpType.bypass, replica_groups=PAIRS,
                ins=[in_h.ap().opt()], outs=[out_h.ap().opt()]).then_inc(s_cc, 1)
            g.wait_ge(s_cc, n)
        recs.append(({"g": em_g}, {s_cc: n}))

    add_phase("A", phase_pre)
    add_phase("B", phase_scan, conv=conv)
    add_cc("yz_d", "yzag_d")
    add_phase("CPy", phase_compact, triples=[
        (d["yzag_d"][:, 0:TOK_PC], d["yzag_d"][:, TOK_PC:SEQ], d["yzT_d"])])
    add_phase("C", phase_outln)
    add_phase("D", phase_ffn0)
    add_phase("E", phase_sig)
    add_cc("sig_d", "sigag_d")
    add_phase("CPs", phase_compact, triples=[
        (d["sigag_d"][0:512, :], d["sigag_d"][512:1024, :], d["sigb_d"][:, 0:TOK_PC]),
        (d["sigag_d"][1024:1536, :], d["sigag_d"][1536:2048, :],
         d["sigb_d"][:, TOK_PC:SEQ])])
    add_phase("F", phase_ema)
    add_cc("ret_d", "retag_d")
    add_phase("CPr", phase_compact, triples=[
        (d["retag_d"][:, 0:TOK_PC], d["retag_d"][:, TOK_PC:SEQ], d["retc_d"])])
    add_phase("G", phase_l1rest)
    add_phase("H", phase_moe)

    block = st.enter_context(nc.Block())

    def mk(name):
        def run(eng):
            prev = None
            for em, fin in recs:
                if prev:
                    for sem, v in prev.items():
                        if v:
                            eng.wait_ge(sem, v)
                f = em.get(name)
                if f:
                    f(eng)
                prev = fin
        return run

    block.tensor(mk("t"))
    block.scalar(mk("sc"))
    block.vector(mk("v"))
    block.gpsimd(mk("g"))
    block.sync(mk("sy"))
    st.close()
    return nc


def _arr(s, m):
    return np.ascontiguousarray(np.asarray(s, np.float32).reshape(m, 128).T)


def host_maps(inputs):
    inputs = {k: np.asarray(v) for k, v in inputs.items()}
    consts = _consts_host()
    cores = list(range(NCORES))

    preM = host_pre_inputs(inputs, consts)
    wD = host_ffn0_weights(inputs, consts)
    scanM = host_scan_inputs(inputs)
    x = inputs["L0_attn_x"]
    ml4 = np.zeros((4, 4), np.float32)
    for j in range(4):
        ml4[j, :j] = 1.0
    ml8 = np.zeros((8, 8), np.float32)
    for e in range(8):
        ml8[e, :e] = 1.0
    shared = {
        "w_out": pack_kpm(inputs["L0_attn_W_out"], 16),
        "w_ret": pack_kpm(inputs["L1_attn_W_ret"], 8),
        "b_ret": _arr(inputs["L1_attn_b_ret"], 8),
        "W1": pack_kpm(inputs["L1_attn_W1"], 8),
        "W2": pack_kpm(inputs["L1_attn_W2"], 32),
        "b1": _arr(inputs["L1_attn_b1"], 32), "b2": _arr(inputs["L1_attn_b2"], 8),
        "g1": _arr(inputs["L1_attn_ln1_scale"], 8),
        "be1": _arr(inputs["L1_attn_ln1_bias"], 8),
        "g2": _arr(inputs["L1_attn_ln2_scale"], 8),
        "be2": _arr(inputs["L1_attn_ln2_bias"], 8),
        "gate_w_d": pack_kpm(inputs["L1_ffn_gate_w"], 8),
        "gate_biasR": np.ascontiguousarray(
            np.broadcast_to(np.asarray(inputs["L1_ffn_gate_bias"], np.float32), (128, 8))),
        "w_gate": np.stack([pack_kpm(w, 8) for w in np.asarray(inputs["L1_ffn_w_gate"])]),
        "w_up": np.stack([pack_kpm(w, 8) for w in np.asarray(inputs["L1_ffn_w_up"])]),
        "w_down": np.stack([pack_kpm(w, 4) for w in np.asarray(inputs["L1_ffn_w_down"])]),
        "sw_gate": pack_kpm(inputs["L1_ffn_sw_gate"], 8),
        "sw_up": pack_kpm(inputs["L1_ffn_sw_up"], 8),
        "sw_down": pack_kpm(inputs["L1_ffn_sw_down"], 8),
        "ml4_d": np.ascontiguousarray(np.broadcast_to(ml4.reshape(-1), (128, 16))),
        "ml8_d": np.ascontiguousarray(np.broadcast_to(ml8.reshape(-1), (128, 64))),
        "idf_d": np.eye(128, dtype=np.float32),
    }
    maps = []
    for core in cores:
        b, h = divmod(core, 2)
        sl = slice(h * TOK_PC, (h + 1) * TOK_PC)
        msk = np.zeros((128, 2), np.float32)
        msk[:, 0] = 1.0 - h
        msk[:, 1] = float(h)
        m = dict(preM[core])
        m.update(wD)
        m.update(shared)
        m.update({
            "alog": scanM[core][0], "dd": scanM[core][1],
            "xT_c": np.ascontiguousarray(x[b].T[:, sl]),
            "msk_d": msk,
        })
        maps.append(m)
    return maps


def kernel(**inputs):
    inputs = {k: np.asarray(v) for k, v in inputs.items()}
    cores = list(range(NCORES))
    nc = build_fused()
    maps = host_maps(inputs)
    res = run_bass_kernel_spmd(nc, maps, core_ids=cores).results

    out = np.empty((BATCH, SEQ, D_MODEL), np.float32)
    for core in cores:
        b, h = divmod(core, 2)
        out[b, h * TOK_PC:(h + 1) * TOK_PC, :] = res[core]["x4"].T
    return out


# version 12 (final)
# speedup vs baseline: 1.0146x; 1.0146x over previous
import contextlib
import numpy as np
import ml_dtypes
import concourse.bass as bass
import concourse.mybir as mybir
from concourse.bass_utils import run_bass_kernel_spmd

F32 = mybir.dt.float32
BF16 = mybir.dt.bfloat16
AF = mybir.ActivationFunctionType
ALU = mybir.AluOpType
AXL = mybir.AxisListType

D_MODEL = 1024
D_INNER = 2048
D_HALF = 1024
D_STATE = 16
DT_RANK = 64
D_CONV = 4
SEQ = 8192
BATCH = 4
NCORES = 8
TOK_PC = 4096

GELU_TANH_S = 0.7978845608028654
GELU_TANH_C = 0.044715
LN_EPS = 1e-5


def phase_pre(nc, st, d, sm, tag, TC=256):
    assert TC == 256
    NCH = SEQ // TC
    s_ld, s_ve, s_mm, s_ac, s_st = sm["ld"], sm["ve"], sm["mm"], sm["ac"], sm["st"]
    xT = d["xT"]
    w_in_x = d["w_in_x"]
    w_in_z = d["w_in_z"]
    wcv_h = d["wcv_d"]
    bcv_h = d["bcv_d"]
    wxp_h = d["wxp_d"]
    wdt_h = d["wdt_d"]
    bdt_h = d["bdt_d"]
    cst_h = d["cst_d"]
    xs_o = d["xs_d"]
    dt_o = d["dt_d"]
    bc_o = d["bc_d"]
    zs_o = d["zs_d"]
    wix = st.enter_context(nc.sbuf_tensor(tag + "wix", [128, 8 * D_INNER], BF16))
    wiz = st.enter_context(nc.sbuf_tensor(tag + "wiz", [128, 8 * D_HALF], BF16))
    wxp = st.enter_context(nc.sbuf_tensor(tag + "wxp", [128, 16 * 96], F32))
    wdt = st.enter_context(nc.sbuf_tensor(tag + "wdt", [64, D_HALF], BF16))
    wcv = st.enter_context(nc.sbuf_tensor(tag + "wcv", [128, 16 * 4], F32))
    bcv = st.enter_context(nc.sbuf_tensor(tag + "bcv", [128, 16], F32))
    bdt = st.enter_context(nc.sbuf_tensor(tag + "bdt", [128, 8], F32))
    cst = st.enter_context(nc.sbuf_tensor(tag + "cst", [128, 8], F32))
    xb16 = st.enter_context(nc.sbuf_tensor(tag + "xb16", [128, 2 * 8 * TC], BF16))
    xpad = st.enter_context(nc.sbuf_tensor(tag + "xpad", [128, 16 * (TC + 3)], F32))
    xcv = st.enter_context(nc.sbuf_tensor(tag + "xcv", [128, 16 * TC], F32))
    xsF = st.enter_context(nc.sbuf_tensor(tag + "xsF", [128, 16 * TC], F32))
    zs = st.enter_context(nc.sbuf_tensor(tag + "zs", [128, 8 * TC], F32))
    dbcf = st.enter_context(nc.sbuf_tensor(tag + "dbcf", [128, TC], F32))
    dtr16 = st.enter_context(nc.sbuf_tensor(tag + "dtr16", [64, TC], BF16))
    dtv = st.enter_context(nc.sbuf_tensor(tag + "dtv", [128, 8 * TC], F32))
    dtt = st.enter_context(nc.sbuf_tensor(tag + "dtt", [128, 8 * TC], F32))
    dts = st.enter_context(nc.sbuf_tensor(tag + "dts", [128, 8 * TC], F32))
    ps = [st.enter_context(nc.psum_tensor(f"{tag}ps{i}", [128, TC], F32)) for i in range(4)]
    psb = st.enter_context(nc.psum_tensor(tag + "psb", [128, TC], F32))
    psd = st.enter_context(nc.psum_tensor(tag + "psd", [128, TC], F32))

    xpad_v = xpad[:].rearrange("p (r t) -> p r t", r=16)
    xcv_v = xcv[:].rearrange("p (r t) -> p r t", r=16)
    xsF_v = xsF[:].rearrange("p (r t) -> p r t", r=16)
    xb16_v = xb16[:].rearrange("p (d k t) -> p d k t", d=2, k=8)
    wix_v = wix[:].rearrange("p (k m) -> p k m", k=8)
    wiz_v = wiz[:].rearrange("p (k m) -> p k m", k=8)
    wxp_v = wxp[:].rearrange("p (k m) -> p k m", k=16)
    dtv_v = dtv[:].rearrange("p (r t) -> p r t", r=8)
    zs_v = zs[:].rearrange("p (r t) -> p r t", r=8)

    MMC = 24 + 1 + 8
    ACC = 24 + 1 + 1 + 8 + 1
    VEC = 5
    NST = 4

    def em_g(g):
        g.dma_start(out=wix[:], in_=w_in_x).then_inc(s_ld, 16)
        g.dma_start(out=wiz[:], in_=w_in_z).then_inc(s_ld, 16)
        g.dma_start(out=wxp[:], in_=wxp_h).then_inc(s_ld, 16)
        g.dma_start(out=wdt[:], in_=wdt_h).then_inc(s_ld, 16)
        g.dma_start(out=wcv[:], in_=wcv_h).then_inc(s_ld, 16)
        g.dma_start(out=bcv[:], in_=bcv_h).then_inc(s_ld, 16)
        g.dma_start(out=bdt[:], in_=bdt_h).then_inc(s_ld, 16)
        g.dma_start(out=cst[:], in_=cst_h).then_inc(s_ld, 16)
        g.wait_ge(s_ld, 128)
        for c in range(NCH):
            if c >= 2:
                g.wait_ge(s_mm, MMC * (c - 2) + 24)
            g.dma_start(out=xb16[:, (c % 2) * 8 * TC:(c % 2 + 1) * 8 * TC],
                        in_=xT[c]).then_inc(s_ld, 16)

    def em_sy(sy):
        for c in range(NCH):
            sy.wait_ge(s_ac, ACC * c + 25)
            sy.dma_start(out=xs_o[:, c * TC:(c + 1) * TC].rearrange("(r p) t -> p r t", p=128),
                         in_=xsF_v[:, 0:8]).then_inc(s_st, 16)
            sy.wait_ge(s_ac, ACC * c + 24)
            sy.dma_start(out=zs_o[:, c * TC:(c + 1) * TC].rearrange("(r p) t -> p r t", p=128),
                         in_=zs_v).then_inc(s_st, 16)
            sy.wait_ge(s_ac, ACC * c + 26)
            sy.dma_start(out=bc_o[:, c * TC:(c + 1) * TC], in_=dbcf[0:96, :]).then_inc(s_st, 16)
            sy.wait_ge(s_ve, VEC * c + 5)
            sy.dma_start(out=dt_o[:, c * TC:(c + 1) * TC].rearrange("(r p) t -> p r t", p=128),
                         in_=dtv_v).then_inc(s_st, 16)
        sy.wait_ge(s_st, 16 * NST * NCH)

    def em_t(t):
        t.wait_ge(s_ld, 128)
        for c in range(NCH):
            t.wait_ge(s_ld, 128 + 16 * (c + 1))
            for m in range(24):
                tg = 24 * c + m
                if tg >= 4:
                    cp, mp = divmod(tg - 4, 24)
                    t.wait_ge(s_ac, ACC * cp + mp + 1)
                bank = ps[m % 4]
                for k in range(8):
                    lhs = wix_v[:, k, m * 128:(m + 1) * 128] if m < 16 else \
                        wiz_v[:, k, (m - 16) * 128:(m - 15) * 128]
                    mm = t.matmul(bank[:], lhs, xb16_v[:, c % 2, k], start=(k == 0), stop=(k == 7))
                mm.then_inc(s_mm, 1)
            t.wait_ge(s_ve, VEC * c + 3)
            if c >= 1:
                t.wait_ge(s_ac, ACC * (c - 1) + 26)
            for k in range(16):
                mm = t.matmul(psb[0:96, :], wxp_v[:, k, :], xsF_v[:, k], start=(k == 0), stop=(k == 15))
            mm.then_inc(s_mm, 1)
            t.wait_ge(s_ve, VEC * c + 4)
            for m in range(8):
                if c >= 1 or m >= 1:
                    t.wait_ge(s_ac, ACC * c + 26 + m if m >= 1 else ACC * (c - 1) + 34)
                t.matmul(psd[:], wdt[:, m * 128:(m + 1) * 128], dtr16[:],
                         start=True, stop=True).then_inc(s_mm, 1)

    def em_sc(sc):
        sc.wait_ge(s_ld, 128)
        for c in range(NCH):
            for m in range(24):
                sc.wait_ge(s_mm, MMC * c + m + 1)
                bank = ps[m % 4]
                if m < 16:
                    sc.wait_ge(s_ve, VEC * c + 1)
                    sc.copy(xpad_v[:, m, 3:3 + TC], bank[:]).then_inc(s_ac, 1)
                else:
                    if c >= 1 and m == 16:
                        sc.wait_ge(s_st, 16 * (NST * (c - 1) + 2))
                    sc.activation(zs_v[:, m - 16], bank[:], AF.Silu).then_inc(s_ac, 1)
            sc.wait_ge(s_ve, VEC * c + 2)
            if c >= 1:
                sc.wait_ge(s_st, 16 * (NST * (c - 1) + 1))
            sc.activation(xsF[:], xcv[:], AF.Silu).then_inc(s_ac, 1)
            sc.wait_ge(s_mm, MMC * c + 25)
            if c >= 1:
                sc.wait_ge(s_st, 16 * (NST * (c - 1) + 3))
            sc.copy(dbcf[0:96, :], psb[0:96, :]).then_inc(s_ac, 1)
            if c >= 1:
                sc.wait_ge(s_st, 16 * (NST * (c - 1) + 4))
            for m in range(8):
                sc.wait_ge(s_mm, MMC * c + 26 + m)
                sc.activation(dtv_v[:, m], psd[:], AF.Identity, bias=bdt[:, m:m + 1]).then_inc(s_ac, 1)
            sc.activation(dtt[:], dtv[:], AF.Abs)
            sc.activation(dts[:], dtt[:], AF.Exp, scale=cst[:, 2:3])
            sc.activation(dtt[:], dts[:], AF.Ln, bias=cst[:, 3:4]).then_inc(s_ac, 1)

    def em_v(v):
        v.wait_ge(s_ld, 128)
        for c in range(NCH):
            if c == 0:
                v.tensor_scalar(xpad_v[:, :, 0:3], cst[:, 0:1].broadcast_to([128, 16, 3]),
                                0.0, None, ALU.mult).then_inc(s_ve, 1)
            else:
                v.tensor_copy(xpad_v[:, :, 0:3], xpad_v[:, :, TC:TC + 3]).then_inc(s_ve, 1)
            v.wait_ge(s_ac, ACC * c + 16)
            last = None
            for r in range(16):
                v.scalar_tensor_tensor(xcv_v[:, r], xpad_v[:, r, 0:TC], wcv[:, r * 4:r * 4 + 1],
                                       bcv[:, r:r + 1].broadcast_to([128, TC]), ALU.mult, ALU.add)
                for k in range(1, 4):
                    last = v.scalar_tensor_tensor(xcv_v[:, r], xpad_v[:, r, k:k + TC],
                                                  wcv[:, r * 4 + k:r * 4 + k + 1],
                                                  xcv_v[:, r], ALU.mult, ALU.add)
            last.then_inc(s_ve, 1)
            v.wait_ge(s_ac, ACC * c + 25)
            v.nop().then_inc(s_ve, 1)
            v.wait_ge(s_ac, ACC * c + 26)
            v.tensor_copy(dtr16[:], dbcf[0:64, :]).then_inc(s_ve, 1)
            v.wait_ge(s_ac, ACC * c + 35)
            v.scalar_tensor_tensor(dtv[:], dtv[:], 0.0, dtt[:], ALU.max, ALU.add).then_inc(s_ve, 1)

    finals = {"ld": 16 * (8 + NCH), "mm": MMC * NCH, "ac": ACC * NCH,
              "ve": VEC * NCH, "st": 16 * NST * NCH}
    return {"g": em_g, "sy": em_sy, "t": em_t, "sc": em_sc, "v": em_v}, finals


def pack_kpm(W, k):
    W = np.asarray(W, np.float32)
    return np.ascontiguousarray(W.reshape(k, 128, -1).transpose(1, 0, 2).reshape(128, -1))


def host_pre_inputs(inputs, consts):
    x = np.asarray(inputs["L0_attn_x"])
    W_in = np.asarray(inputs["L0_attn_W_in"])
    W_conv = np.asarray(inputs["L0_attn_W_conv"])
    b_conv = np.asarray(inputs["L0_attn_b_conv"])
    W_xproj = np.asarray(inputs["L0_attn_W_xproj"])
    W_dt = np.asarray(inputs["L0_attn_W_dt"])
    b_dt = np.asarray(inputs["L0_attn_b_dt"])
    NCHA = SEQ // 256
    maps = []
    for core in range(NCORES):
        b, h = divmod(core, 2)
        own = np.arange(h * D_HALF, (h + 1) * D_HALF)
        oth = np.arange((1 - h) * D_HALF, (2 - h) * D_HALF)
        perm = np.concatenate([own, oth])
        wcv = np.empty((128, 16 * 4), np.float32)
        bcv = np.empty((128, 16), np.float32)
        for r in range(16):
            idx = perm[r * 128:(r + 1) * 128]
            for k in range(4):
                wcv[:, r * 4 + k] = W_conv[k, idx]
            bcv[:, r] = b_conv[idx]
        bdt = np.ascontiguousarray(b_dt[own].reshape(8, 128).T)
        xT = x[b].T
        xTP = np.ascontiguousarray(
            xT.reshape(8, 128, NCHA, 256).transpose(2, 1, 0, 3).reshape(NCHA, 128, 8 * 256))
        maps.append({
            "xT": xTP,
            "w_in_x": pack_kpm(W_in[:, perm], 8),
            "w_in_z": pack_kpm(W_in[:, D_INNER + own], 8),
            "wcv_d": wcv, "bcv_d": bcv,
            "wxp_d": pack_kpm(W_xproj[perm, :], 16),
            "wdt_d": np.ascontiguousarray(W_dt[:, own]),
            "bdt_d": bdt, "cst_d": consts,
        })
    return maps


def _consts_host():
    c = np.zeros((128, 8), np.float32)
    c[:, 0] = 10.0
    c[:, 1] = -10.0
    c[:, 2] = -1.0
    c[:, 3] = 1.0
    c[:, 4] = 0.5
    c[:, 5] = GELU_TANH_S
    c[:, 6] = LN_EPS
    c[:, 7] = 1.0 / D_MODEL
    return c


def phase_scan(nc, st, d, sm, tag, K=32, conv=()):
    NCH = SEQ // K
    s_ld, s_ve, s_ac, s_st = sm["ld"], sm["ve"], sm["ac"], sm["st"]
    s_xa, s_xb = sm["xa"], sm["xb"]
    s_pp = sm["mm"]
    cper = -(-len(conv) // NCH) if conv else 0
    xs_h = d["xs_d"]
    dt_h = d["dt_d"]
    bc_h = d["bc_d"]
    alog_h = d["alog"]
    dd_h = d["dd"]
    zs_h = d["zs_d"]
    cst_h = d["cst_d"]
    yz_o = d["yz_d"]

    apos = st.enter_context(nc.sbuf_tensor(tag + "apos", [128, 8 * 16], F32))
    alog = st.enter_context(nc.sbuf_tensor(tag + "alog_s", [128, 8 * 16], F32))
    dd = st.enter_context(nc.sbuf_tensor(tag + "dd_s", [128, 8], F32))
    cst = st.enter_context(nc.sbuf_tensor(tag + "cst", [128, 8], F32))
    dtc = st.enter_context(nc.sbuf_tensor(tag + "dtc", [128, 2 * 8 * K], F32))
    xsc = st.enter_context(nc.sbuf_tensor(tag + "xsc", [128, 2 * 8 * K], F32))
    zsc = st.enter_context(nc.sbuf_tensor(tag + "zsc", [128, 2 * 8 * K], F32))
    bbc = st.enter_context(nc.sbuf_tensor(tag + "bbc", [128, 2 * 16 * K], F32))
    cbc = st.enter_context(nc.sbuf_tensor(tag + "cbc", [128, 2 * 16 * K], F32))
    bbc2 = st.enter_context(nc.sbuf_tensor(tag + "bbc2", [128, 2 * 16 * K], F32))
    cbc2 = st.enter_context(nc.sbuf_tensor(tag + "cbc2", [128, 2 * 16 * K], F32))
    buf1 = st.enter_context(nc.sbuf_tensor(tag + "buf1", [128, 2 * K * 128], F32))
    buf2 = st.enter_context(nc.sbuf_tensor(tag + "buf2", [128, K * 128], F32))
    buf3 = st.enter_context(nc.sbuf_tensor(tag + "buf3", [128, 2 * K * 128], F32))
    straj = st.enter_context(nc.sbuf_tensor(tag + "straj", [128, 2 * K * 128], F32))
    tyb = st.enter_context(nc.sbuf_tensor(tag + "tyb", [128, 2 * K * 128], F32))
    sS = st.enter_context(nc.sbuf_tensor(tag + "sS", [128, 128], F32))
    zz = st.enter_context(nc.sbuf_tensor(tag + "zz", [128, 128], F32))
    xdtb = st.enter_context(nc.sbuf_tensor(tag + "xdtb", [128, 8 * K], F32))
    yb = st.enter_context(nc.sbuf_tensor(tag + "yb", [128, 8 * K], F32))
    dxb = st.enter_context(nc.sbuf_tensor(tag + "dxb", [128, 8 * K], F32))
    yzst = st.enter_context(nc.sbuf_tensor(tag + "yzst", [128, 2 * 8 * K], F32))
    cb = st.enter_context(nc.sbuf_tensor(tag + "cb", [128, 2 * 4096], BF16))

    dtc_v = dtc[:].rearrange("p (d g t) -> p d g t", d=2, g=8)
    xsc_v = xsc[:].rearrange("p (d g t) -> p d g t", d=2, g=8)
    zsc_v = zsc[:].rearrange("p (d g t) -> p d g t", d=2, g=8)
    bbc_v = bbc[:].rearrange("p (d n t) -> p d n t", d=2, n=16)
    cbc_v = cbc[:].rearrange("p (d n t) -> p d n t", d=2, n=16)
    bbc2_v = bbc2[:].rearrange("p (d t n) -> p d t n", d=2, t=K)
    cbc2_v = cbc2[:].rearrange("p (d t n) -> p d t n", d=2, t=K)
    apos_v = apos[:].rearrange("p (g n) -> p g n", g=8)
    yzst_v = yzst[:].rearrange("p (d g t) -> p d g t", d=2, g=8)
    NLD = 5
    SZ = K * 128

    mq, mu = {}, {}
    pp = [0]
    for it in range(NCH + 1):
        if 0 <= it - 1 < NCH:
            pp[0] += 1
            mq[it - 1] = pp[0]
            pp[0] += 1
            mu[it - 1] = pp[0]
    msc = {c: 2 * c + 1 for c in range(NCH)}
    mtl = {c: 2 * c + 2 for c in range(NCH)}
    vv = [2 * NCH]

    def em_g(g):
        g.dma_start(out=alog[:], in_=alog_h).then_inc(s_ld, 16)
        g.dma_start(out=dd[:], in_=dd_h).then_inc(s_ld, 16)
        g.dma_start(out=cst[:], in_=cst_h).then_inc(s_ld, 16)
        g.wait_ge(s_ld, 48)
        for it in range(NCH + 1):
            cq = it - 1
            if 0 <= cq < NCH:
                par = cq % 2
                g.wait_ge(s_ld, 48 + 16 * NLD * (cq + 1))
                if cq >= 2:
                    g.wait_ge(s_ve, msc[cq - 2])
                q_v = buf1[:, par * SZ:(par + 1) * SZ].rearrange(
                    "p (t g n) -> p t g n", t=K, g=8)
                dt_op = dtc_v[:, par].transpose([0, 2, 1]).unsqueeze(3) \
                    .broadcast_to([128, K, 8, 16])
                ap_op = apos_v.unsqueeze(1).broadcast_to([128, K, 8, 16])
                g.tensor_tensor(q_v, dt_op, ap_op, ALU.mult).then_inc(s_pp, 1)
                g.tensor_tensor(xdtb[:].rearrange("p (g t) -> p g t", g=8),
                                xsc_v[:, par], dtc_v[:, par], ALU.mult)
                g.wait_ge(s_ac, 2 * cq + 2)
                u_v = buf3[:, par * SZ:(par + 1) * SZ].rearrange(
                    "p (t g n) -> p t g n", t=K, g=8)
                xdt_op = xdtb[:].rearrange("p (g t) -> p g t", g=8).transpose([0, 2, 1]) \
                    .unsqueeze(3).broadcast_to([128, K, 8, 16])
                b_op = bbc2_v[:, par].unsqueeze(2).broadcast_to([128, K, 8, 16])
                g.tensor_tensor(u_v, xdt_op, b_op, ALU.mult).then_inc(s_pp, 1)
            if it < NCH:
                par = it % 2
                if it >= 2:
                    g.wait_ge(s_ve, mtl[it - 2])
                sl = slice(it * K, (it + 1) * K)
                g.dma_start(out=dtc_v[:, par],
                            in_=dt_h[:, sl].rearrange("(g p) t -> p g t", p=128)
                            ).then_inc(s_ld, 16)
                g.dma_start(out=xsc_v[:, par],
                            in_=xs_h[:, sl].rearrange("(g p) t -> p g t", p=128)
                            ).then_inc(s_ld, 16)
                g.dma_start(out=zsc_v[:, par],
                            in_=zs_h[:, sl].rearrange("(g p) t -> p g t", p=128)
                            ).then_inc(s_ld, 16)
                g.dma_start(out=bbc_v[:, par],
                            in_=bc_h[64:80, sl].unsqueeze(0).broadcast_to([128, 16, K])
                            ).then_inc(s_ld, 16)
                g.dma_start(out=cbc_v[:, par],
                            in_=bc_h[80:96, sl].unsqueeze(0).broadcast_to([128, 16, K])
                            ).then_inc(s_ld, 16)
                for i in range(cper * it, min(cper * (it + 1), len(conv))):
                    if i >= 2:
                        g.wait_ge(s_xb, 16 * (i - 1))
                    g.dma_start(out=cb[:, (i % 2) * 4096:(i % 2 + 1) * 4096],
                                in_=conv[i][0]).then_inc(s_xa, 16)

    def em_sc(sc):
        sc.wait_ge(s_ld, 48)
        sc.activation(apos[:], alog[:], AF.Exp).then_inc(s_ac, 1)
        for c in range(NCH):
            par = c % 2
            for i in range(cper * c, min(cper * (c + 1), len(conv))):
                sc.wait_ge(s_xa, 16 * (i + 1))
                sc.dma_start(out=conv[i][1],
                             in_=cb[:, (i % 2) * 4096:(i % 2 + 1) * 4096]
                             ).then_inc(s_xb, 16)
            sc.wait_ge(s_ld, 48 + 16 * NLD * (c + 1))
            if c >= 2:
                sc.wait_ge(s_pp, mu[c - 2])
                sc.wait_ge(s_ve, mtl[c - 2])
            sc.copy(bbc2_v[:, par], bbc_v[:, par].transpose([0, 2, 1]))
            sc.copy(cbc2_v[:, par], cbc_v[:, par].transpose([0, 2, 1])).then_inc(s_ac, 1)
            sc.wait_ge(s_pp, mq[c])
            sc.activation(buf2[:], buf1[:, par * SZ:(par + 1) * SZ], AF.Relu,
                          scale=cst[:, 2:3], bias=cst[:, 0:1])
            sc.activation(buf1[:, par * SZ:(par + 1) * SZ], buf2[:], AF.Exp,
                          bias=cst[:, 1:2]).then_inc(s_ac, 1)

    def em_v(v):
        v.wait_ge(s_ld, 48)
        v.tensor_scalar(zz[:], cst[:, 0:1].broadcast_to([128, 128]), 0.0, None, ALU.mult)

        for c in range(NCH):
            par = c % 2
            opar = (c - 1) % 2
            v.wait_ge(s_ac, 2 * c + 3)
            v.wait_ge(s_pp, mu[c])
            E = buf1[:, par * SZ:(par + 1) * SZ]
            U = buf3[:, par * SZ:(par + 1) * SZ]
            S = straj[:, par * SZ:(par + 1) * SZ]
            Sp = straj[:, opar * SZ:(opar + 1) * SZ]
            for t in range(K):
                prev = zz[:] if (c == 0 and t == 0) else \
                    Sp[:, (K - 1) * 128:] if t == 0 else S[:, (t - 1) * 128:t * 128]
                v.tensor_tensor(sS[:], prev, E[:, t * 128:(t + 1) * 128], ALU.mult)
                v.tensor_tensor(sS[:], sS[:], U[:, t * 128:(t + 1) * 128], ALU.add)
                last = v.tensor_scalar(S[:, t * 128:(t + 1) * 128], sS[:], 10.0, -10.0,
                                       ALU.min, ALU.max)
            last.then_inc(s_ve, 1)
            ty_v = tyb[:, par * SZ:(par + 1) * SZ].rearrange(
                "p (t g n) -> p t g n", t=K, g=8)
            s_op = straj[:, par * SZ:(par + 1) * SZ].rearrange(
                "p (t g n) -> p t g n", t=K, g=8)
            c_op = cbc2_v[:, par].unsqueeze(2).broadcast_to([128, K, 8, 16])
            v.tensor_tensor(ty_v, s_op, c_op, ALU.mult)
            yb_v = yb[:].rearrange("p (t g) -> p t g", t=K)
            v.tensor_reduce(yb_v, ty_v, AXL.X, ALU.add)
            dx_v = dxb[:].rearrange("p (t g) -> p t g", t=K)
            v.tensor_tensor(dx_v, xsc_v[:, par].transpose([0, 2, 1]),
                            dd[:].unsqueeze(1).broadcast_to([128, K, 8]), ALU.mult)
            v.tensor_tensor(yb[:], yb[:], dxb[:], ALU.add)
            if c >= 2:
                v.wait_ge(s_st, 16 * (c - 1))
            v.tensor_tensor(yzst_v[:, par], yb_v.transpose([0, 2, 1]),
                            zsc_v[:, par], ALU.mult).then_inc(s_ve, 1)

    def em_sy(sy):
        for c in range(NCH):
            sy.wait_ge(s_ve, mtl[c])
            sy.dma_start(out=yz_o[:, c * K:(c + 1) * K].rearrange("(g p) t -> p g t", p=128),
                         in_=yzst_v[:, c % 2]).then_inc(s_st, 16)
        sy.wait_ge(s_st, 16 * NCH)
        sy.wait_ge(s_xb, 16 * len(conv))

    finals = {"ld": 16 * (3 + NLD * NCH), "ac": 1 + 2 * NCH, "ve": vv[0], "mm": pp[0],
              "st": 16 * NCH, "xa": 16 * len(conv), "xb": 16 * len(conv)}
    return {"g": em_g, "sc": em_sc, "v": em_v, "sy": em_sy}, finals


def host_scan_inputs(inputs):
    A_log = np.asarray(inputs["L0_attn_A_log"])
    D = np.asarray(inputs["L0_attn_D"])
    out = []
    for core in range(NCORES):
        b, h = divmod(core, 2)
        own = np.arange(h * D_HALF, (h + 1) * D_HALF)
        al = A_log[own].reshape(8, 128, 16)
        alog = np.ascontiguousarray(al.transpose(1, 0, 2).reshape(128, 8 * 16))
        dd = np.ascontiguousarray(D[own].reshape(8, 128).T)
        out.append((alog, dd))
    return out


def _emit_ln(v, sc, t, ps_m, ps_q, sx_v, sq, sq_v, ones, cst, mb, rb, stat, out_view,
             waits, incs, nk=8, gamma=None, beta=None):
    raise NotImplementedError


def phase_outln(nc, st, d, sm, tag, TC=512):
    TOK = TOK_PC
    NCH = TOK // TC
    s_ld, s_ve, s_mm, s_ac, s_st = sm["ld"], sm["ve"], sm["mm"], sm["ac"], sm["st"]
    yz_h = d["yzT_d"]
    wo_h = d["w_out"]
    xT_h = d["xT_c"]
    cst_h = d["cst_d"]
    x1_o = d["x1_d"]

    wo = st.enter_context(nc.sbuf_tensor(tag + "wo", [128, 16 * D_MODEL], BF16))
    cst = st.enter_context(nc.sbuf_tensor(tag + "cst", [128, 8], F32))
    ones = st.enter_context(nc.sbuf_tensor(tag + "ones", [128, 2], F32))
    yzc = st.enter_context(nc.sbuf_tensor(tag + "yzc", [128, 2 * 16 * TC], BF16))
    xc = st.enter_context(nc.sbuf_tensor(tag + "xc", [128, 2 * 8 * TC], F32))
    sx = st.enter_context(nc.sbuf_tensor(tag + "sx", [128, 8 * TC], F32))
    sq = st.enter_context(nc.sbuf_tensor(tag + "sq", [128, 8 * TC], F32))
    stat = st.enter_context(nc.sbuf_tensor(tag + "stat", [1, 6 * TC], F32))
    mb = st.enter_context(nc.sbuf_tensor(tag + "mb", [128, TC], F32))
    rb = st.enter_context(nc.sbuf_tensor(tag + "rb", [128, TC], F32))
    ost = st.enter_context(nc.sbuf_tensor(tag + "ost", [128, 2 * 8 * TC], F32))
    psA = st.enter_context(nc.psum_tensor(tag + "psA", [128, TC], F32))
    psB = st.enter_context(nc.psum_tensor(tag + "psB", [128, TC], F32))
    psm = st.enter_context(nc.psum_tensor(tag + "psm", [1, TC], F32))
    psq = st.enter_context(nc.psum_tensor(tag + "psq", [1, TC], F32))
    psbc = st.enter_context(nc.psum_tensor(tag + "psbc", [128, 2 * TC], F32))

    wo_v = wo[:].rearrange("p (k m) -> p k m", k=16)
    yzc_v = yzc[:].rearrange("p (d k t) -> p d k t", d=2, k=16)
    xc_v = xc[:].rearrange("p (d r t) -> p d r t", d=2, r=8)
    sx_v = sx[:].rearrange("p (r t) -> p r t", r=8)
    sq_v = sq[:].rearrange("p (r t) -> p r t", r=8)
    ost_v = ost[:].rearrange("p (d r t) -> p d r t", d=2, r=8)
    MMC = 12
    ACC = 12
    VEC = 4

    def em_g(g):
        g.dma_start(out=wo[:], in_=wo_h).then_inc(s_ld, 16)
        g.dma_start(out=cst[:], in_=cst_h).then_inc(s_ld, 16)
        g.wait_ge(s_ld, 32)
        for c in range(NCH):
            d = c % 2
            if c >= 2:
                g.wait_ge(s_ve, VEC * (c - 2) + 4)
            sl = slice(c * TC, (c + 1) * TC)
            g.dma_start(out=yzc_v[:, d], in_=yz_h[:, sl].rearrange("(k p) t -> p k t", p=128)
                        ).then_inc(s_ld, 16)
            g.dma_start(out=xc_v[:, d], in_=xT_h[:, sl].rearrange("(r p) t -> p r t", p=128)
                        ).then_inc(s_ld, 16)

    def em_t(t):
        t.wait_ge(s_ld, 32)
        for c in range(NCH):
            d = c % 2
            t.wait_ge(s_ld, 32 + 16 * (2 * c + 1))
            for m in range(8):
                bank = psA if m % 2 == 0 else psB
                if 8 * c + m >= 2:
                    cp, mp = divmod(8 * c + m - 2, 8)
                    t.wait_ge(s_ac, 1 + ACC * cp + mp + 1)
                for k in range(16):
                    mm = t.matmul(bank[:], wo_v[:, k, m * 128:(m + 1) * 128],
                                  yzc_v[:, d, k], start=(k == 0), stop=(k == 15))
                mm.then_inc(s_mm, 1)
            t.wait_ge(s_ve, VEC * c + 1)
            for k in range(8):
                mm = t.matmul(psm[:], ones[:, 0:1], sx_v[:, k], start=(k == 0), stop=(k == 7))
            mm.then_inc(s_mm, 1)
            t.wait_ge(s_ac, 1 + ACC * c + 9)
            for k in range(8):
                mm = t.matmul(psq[:], ones[:, 0:1], sq_v[:, k], start=(k == 0), stop=(k == 7))
            mm.then_inc(s_mm, 1)
            t.wait_ge(s_ve, VEC * c + 3)
            t.matmul(psbc[:, 0:TC], ones[0:1, 0:1].broadcast_to([1, 128]),
                     stat[0:1, 0:TC], start=True, stop=True).then_inc(s_mm, 1)
            t.matmul(psbc[:, TC:2 * TC], ones[0:1, 0:1].broadcast_to([1, 128]),
                     stat[0:1, 3 * TC:4 * TC], start=True, stop=True).then_inc(s_mm, 1)

    def em_sc(sc):
        sc.wait_ge(s_ld, 32)
        sc.copy(ones[:], cst[:, 3:4].broadcast_to([128, 2])).then_inc(s_ac, 1)
        for c in range(NCH):
            ACB = 1 + ACC * c
            if c >= 1:
                sc.wait_ge(s_ve, VEC * (c - 1) + 4)
            for m in range(8):
                sc.wait_ge(s_mm, MMC * c + m + 1)
                bank = psA if m % 2 == 0 else psB
                sc.copy(sx_v[:, m], bank[:]).then_inc(s_ac, 1)
            sc.wait_ge(s_ve, VEC * c + 1)
            sc.activation(sq[:], sx[:], AF.Square).then_inc(s_ac, 1)
            sc.wait_ge(s_mm, MMC * c + 10)
            sc.activation(stat[0:1, 0:TC], psm[0:1, :], AF.Identity, scale=cst[0:1, 7:8])
            sc.activation(stat[0:1, TC:2 * TC], psq[0:1, :], AF.Identity,
                          scale=cst[0:1, 7:8]).then_inc(s_ac, 1)
            sc.wait_ge(s_ve, VEC * c + 2)
            sc.activation(stat[0:1, 5 * TC:6 * TC], stat[0:1, 4 * TC:5 * TC], AF.Sqrt,
                          bias=cst[0:1, 6:7]).then_inc(s_ac, 1)
            sc.wait_ge(s_mm, MMC * c + 12)
            sc.copy(mb[:], psbc[:, 0:TC])
            sc.copy(rb[:], psbc[:, TC:2 * TC]).then_inc(s_ac, 1)

    def em_v(v):
        for c in range(NCH):
            d = c % 2
            ACB = 1 + ACC * c
            v.wait_ge(s_ac, ACB + 8)
            v.wait_ge(s_ld, 32 + 16 * (2 * c + 2))
            v.tensor_tensor(sx_v[:], sx_v[:], xc_v[:, d], ALU.add).then_inc(s_ve, 1)
            v.wait_ge(s_ac, ACB + 10)
            v.tensor_tensor(stat[0:1, 4 * TC:5 * TC], stat[0:1, 0:TC], stat[0:1, 0:TC],
                            ALU.mult)
            v.scalar_tensor_tensor(stat[0:1, 4 * TC:5 * TC], stat[0:1, TC:2 * TC], 1.0,
                                   stat[0:1, 4 * TC:5 * TC], ALU.mult,
                                   ALU.subtract).then_inc(s_ve, 1)
            v.wait_ge(s_ac, ACB + 11)
            v.reciprocal(stat[0:1, 3 * TC:4 * TC],
                         stat[0:1, 5 * TC:6 * TC]).then_inc(s_ve, 1)
            v.wait_ge(s_ac, ACB + 12)
            if c >= 2:
                v.wait_ge(s_st, 16 * (c - 1))
            for k in range(8):
                v.tensor_tensor(ost_v[:, d, k], sx_v[:, k], mb[:], ALU.subtract)
                last = v.tensor_tensor(ost_v[:, d, k], ost_v[:, d, k], rb[:], ALU.mult)
            last.then_inc(s_ve, 1)

    def em_sy(sy):
        for c in range(NCH):
            sy.wait_ge(s_ve, VEC * c + 4)
            sy.dma_start(out=x1_o[:, c * TC:(c + 1) * TC].rearrange("(r p) t -> p r t", p=128),
                         in_=ost_v[:, c % 2]).then_inc(s_st, 16)
        sy.wait_ge(s_st, 16 * NCH)

    finals = {"ld": 16 * (2 + 2 * NCH), "mm": MMC * NCH, "ac": 1 + ACC * NCH,
              "ve": VEC * NCH, "st": 16 * NCH}
    return {"g": em_g, "t": em_t, "sc": em_sc, "v": em_v, "sy": em_sy}, finals


def phase_ffn0(nc, st, d, sm, tag, TC=256):
    TOK = TOK_PC
    NCH = TOK // TC
    s_ld, s_ve, s_mm, s_ac, s_st = sm["ld"], sm["ve"], sm["mm"], sm["ac"], sm["st"]
    x1_h = d["x1_d"]
    w1_h = d["w1b"]
    w2_h = d["w2b"]
    h1_h = d["h1b"]
    h2_h = d["h2b"]
    w3_h = d["w3b"]
    s1_h = d["s1"]
    s2_h = d["s2"]
    sh1_h = d["sh1"]
    sh2_h = d["sh2"]
    s3_h = d["s3"]
    cst_h = d["cst_d"]
    x2_o = d["x2_d"]

    ws = st.enter_context(nc.sbuf_tensor(tag + "ws", [128, 32 * 1024], BF16))
    t1a = st.enter_context(nc.sbuf_tensor(tag + "t1a", [128, 32 * TC], F32))
    t1 = st.enter_context(nc.sbuf_tensor(tag + "t1", [128, 32 * TC], BF16))
    t2 = st.enter_context(nc.sbuf_tensor(tag + "t2", [128, 32 * TC], BF16))
    tmp2 = st.enter_context(nc.sbuf_tensor(tag + "tmp2", [128, 2 * TC], F32))
    x1c = st.enter_context(nc.sbuf_tensor(tag + "x1c", [128, 8 * TC], F32))
    x16 = st.enter_context(nc.sbuf_tensor(tag + "x16", [128, 8 * TC], BF16))
    sx = st.enter_context(nc.sbuf_tensor(tag + "sx", [128, 8 * TC], F32))
    sq = st.enter_context(nc.sbuf_tensor(tag + "sq", [128, 8 * TC], F32))
    stat = st.enter_context(nc.sbuf_tensor(tag + "stat", [1, 6 * TC], F32))
    mb = st.enter_context(nc.sbuf_tensor(tag + "mb", [128, TC], F32))
    rb = st.enter_context(nc.sbuf_tensor(tag + "rb", [128, TC], F32))
    ost = st.enter_context(nc.sbuf_tensor(tag + "ost", [128, 8 * TC], F32))
    s1s = st.enter_context(nc.sbuf_tensor(tag + "s1s", [128, 32], F32))
    s2s = st.enter_context(nc.sbuf_tensor(tag + "s2s", [128, 32], F32))
    sh1s = st.enter_context(nc.sbuf_tensor(tag + "sh1s", [128, 32], F32))
    sh2s = st.enter_context(nc.sbuf_tensor(tag + "sh2s", [128, 32], F32))
    s3s = st.enter_context(nc.sbuf_tensor(tag + "s3s", [128, 8], F32))
    cst = st.enter_context(nc.sbuf_tensor(tag + "cst", [128, 8], F32))
    ones = st.enter_context(nc.sbuf_tensor(tag + "ones", [128, 2], F32))
    ps = [st.enter_context(nc.psum_tensor(f"{tag}ps{i}", [128, TC], F32)) for i in range(4)]
    psm = st.enter_context(nc.psum_tensor(tag + "psm", [1, TC], F32))
    psq = st.enter_context(nc.psum_tensor(tag + "psq", [1, TC], F32))
    psbc = st.enter_context(nc.psum_tensor(tag + "psbc", [128, 2 * TC], F32))

    ws8 = ws[:].rearrange("p (k m) -> p k m", k=8)
    ws32 = ws[:].rearrange("p (k m) -> p k m", k=32)
    t1a_v = t1a[:].rearrange("p (m t) -> p m t", m=32)
    t1_v = t1[:].rearrange("p (m t) -> p m t", m=32)
    t2_v = t2[:].rearrange("p (m t) -> p m t", m=32)
    tmp2_v = tmp2[:].rearrange("p (d t) -> p d t", d=2)
    x1c_v = x1c[:].rearrange("p (r t) -> p r t", r=8)
    x16_v = x16[:].rearrange("p (r t) -> p r t", r=8)
    sx_v = sx[:].rearrange("p (r t) -> p r t", r=8)
    sq_v = sq[:].rearrange("p (r t) -> p r t", r=8)
    ost_v = ost[:].rearrange("p (r t) -> p r t", r=8)
    LDC = 23
    MMC = 140
    ACC = 140
    VEC = 37
    STG = ([(w1_h[s], 8, 16, 1 + 16 * s) for s in range(2)] +
           [(w2_h[s], 8, 16, 33 + 16 * s) for s in range(2)] +
           [(h1_h[s], 32, 4, 65 + 4 * s) for s in range(8)] +
           [(h2_h[s], 32, 4, 97 + 4 * s) for s in range(8)] +
           [(w3_h[s], 32, 4, 129 + 4 * s) for s in range(2)])
    TILE_END = [src[3] + src[2] - 1 for src in STG]
    HW = 16 * 1024

    def em_g(g):
        for w, dst in [(cst_h, cst[:]), (s1_h, s1s[:]), (s2_h, s2s[:]),
                       (sh1_h, sh1s[:]), (sh2_h, sh2s[:]), (s3_h, s3s[:])]:
            g.dma_start(out=dst, in_=w).then_inc(s_ld, 16)
        g.wait_ge(s_ld, 96)
        for c in range(NCH):
            sl = slice(c * TC, (c + 1) * TC)
            if c >= 1:
                g.wait_ge(s_ve, VEC * (c - 1) + 34)
            g.dma_start(out=x1c_v, in_=x1_h[:, sl].rearrange("(r p) t -> p r t", p=128)
                        ).then_inc(s_ld, 16)
            for s, (src, kk, nt, j0) in enumerate(STG):
                half = s % 2
                if s >= 2:
                    g.wait_ge(s_mm, MMC * c + TILE_END[s - 2])
                elif c >= 1:
                    g.wait_ge(s_mm, MMC * (c - 1) + TILE_END[s + 20])
                g.dma_start(out=ws[:, half * HW:(half + 1) * HW], in_=src
                            ).then_inc(s_ld, 16)

    def bank_wait(t, c, j):
        if j > 4:
            t.wait_ge(s_ac, 1 + ACC * c + j - 4)
        elif c >= 1:
            t.wait_ge(s_ac, 1 + ACC * (c - 1) + 132 + j)

    def em_t(t):
        t.wait_ge(s_ld, 96)
        for c in range(NCH):
            t.wait_ge(s_ve, VEC * c + 1)
            for s, (src, kk, nt, j0) in enumerate(STG):
                half = s % 2
                wsv = ws[:, half * HW:(half + 1) * HW].rearrange(
                    "p (k m) -> p k m", k=kk)
                if j0 == 65:
                    t.wait_ge(s_ve, VEC * c + 33)
                elif j0 == 97:
                    t.wait_ge(s_ac, 1 + ACC * c + 96)
                elif j0 == 129:
                    t.wait_ge(s_ac, 1 + ACC * c + 128)
                t.wait_ge(s_ld, 96 + 16 * (LDC * c + 2 + s))
                if j0 < 65:
                    rhs = x16_v
                elif j0 < 97:
                    rhs = t1_v
                elif j0 < 129:
                    rhs = t2_v
                else:
                    rhs = t1_v
                for lt in range(nt):
                    j = j0 + lt
                    bank_wait(t, c, j)
                    for k in range(kk):
                        mm = t.matmul(ps[(j - 1) % 4][:],
                                      wsv[:, k, lt * 128:(lt + 1) * 128],
                                      rhs[:, k], start=(k == 0), stop=(k == kk - 1))
                    mm.then_inc(s_mm, 1)
            t.wait_ge(s_ve, VEC * c + 34)
            for k in range(8):
                mm = t.matmul(psm[:], ones[:, 0:1], sx_v[:, k], start=(k == 0), stop=(k == 7))
            mm.then_inc(s_mm, 1)
            t.wait_ge(s_ac, 1 + ACC * c + 137)
            for k in range(8):
                mm = t.matmul(psq[:], ones[:, 0:1], sq_v[:, k], start=(k == 0), stop=(k == 7))
            mm.then_inc(s_mm, 1)
            t.wait_ge(s_ve, VEC * c + 36)
            t.matmul(psbc[:, 0:TC], ones[0:1, 0:1].broadcast_to([1, 128]),
                     stat[0:1, 0:TC], start=True, stop=True).then_inc(s_mm, 1)
            t.matmul(psbc[:, TC:2 * TC], ones[0:1, 0:1].broadcast_to([1, 128]),
                     stat[0:1, 3 * TC:4 * TC], start=True, stop=True).then_inc(s_mm, 1)

    def em_sc(sc):
        sc.wait_ge(s_ld, 96)
        sc.copy(ones[:], cst[:, 3:4].broadcast_to([128, 2])).then_inc(s_ac, 1)
        for c in range(NCH):
            ACB = 1 + ACC * c
            for m in range(32):
                sc.wait_ge(s_mm, MMC * c + m + 1)
                sc.activation(t1a_v[:, m], ps[m % 4][:], AF.Silu,
                              scale=s1s[:, m:m + 1]).then_inc(s_ac, 1)
            for m in range(32):
                j = 32 + m + 1
                sc.wait_ge(s_mm, MMC * c + j)
                if m >= 2:
                    sc.wait_ge(s_ve, VEC * c + 1 + m - 1)
                sc.activation(tmp2_v[:, m % 2], ps[(j - 1) % 4][:], AF.Identity,
                              scale=s2s[:, m:m + 1]).then_inc(s_ac, 1)
            for m in range(32):
                j = 64 + m + 1
                sc.wait_ge(s_mm, MMC * c + j)
                sc.activation(t2_v[:, m], ps[(j - 1) % 4][:], AF.Gelu,
                              scale=sh1s[:, m:m + 1]).then_inc(s_ac, 1)
            for m in range(32):
                j = 96 + m + 1
                sc.wait_ge(s_mm, MMC * c + j)
                sc.activation(t1_v[:, m], ps[(j - 1) % 4][:], AF.Gelu,
                              scale=sh2s[:, m:m + 1]).then_inc(s_ac, 1)
            for m in range(8):
                j = 128 + m + 1
                sc.wait_ge(s_mm, MMC * c + j)
                sc.activation(sx_v[:, m], ps[(j - 1) % 4][:], AF.Identity,
                              scale=s3s[:, m:m + 1]).then_inc(s_ac, 1)
            sc.wait_ge(s_ve, VEC * c + 34)
            sc.activation(sq[:], sx[:], AF.Square).then_inc(s_ac, 1)
            sc.wait_ge(s_mm, MMC * c + 138)
            sc.activation(stat[0:1, 0:TC], psm[0:1, :], AF.Identity, scale=cst[0:1, 7:8])
            sc.activation(stat[0:1, TC:2 * TC], psq[0:1, :], AF.Identity,
                          scale=cst[0:1, 7:8]).then_inc(s_ac, 1)
            sc.wait_ge(s_ve, VEC * c + 35)
            sc.activation(stat[0:1, 5 * TC:6 * TC], stat[0:1, 4 * TC:5 * TC], AF.Sqrt,
                          bias=cst[0:1, 6:7]).then_inc(s_ac, 1)
            sc.wait_ge(s_mm, MMC * c + 140)
            sc.copy(mb[:], psbc[:, 0:TC])
            sc.copy(rb[:], psbc[:, TC:2 * TC]).then_inc(s_ac, 1)

    def em_v(v):
        for c in range(NCH):
            ACB = 1 + ACC * c
            v.wait_ge(s_ld, 96 + 16 * (LDC * c + 1))
            v.tensor_copy(x16[:], x1c[:]).then_inc(s_ve, 1)
            for m in range(32):
                v.wait_ge(s_ac, ACB + 32 + m + 1)
                v.tensor_tensor(t1_v[:, m], t1a_v[:, m], tmp2_v[:, m % 2],
                                ALU.mult).then_inc(s_ve, 1)
            v.wait_ge(s_ac, ACB + 136)
            v.tensor_tensor(sx_v[:], sx_v[:], x1c_v[:], ALU.add).then_inc(s_ve, 1)
            v.wait_ge(s_ac, ACB + 138)
            v.tensor_tensor(stat[0:1, 4 * TC:5 * TC], stat[0:1, 0:TC], stat[0:1, 0:TC],
                            ALU.mult)
            v.scalar_tensor_tensor(stat[0:1, 4 * TC:5 * TC], stat[0:1, TC:2 * TC], 1.0,
                                   stat[0:1, 4 * TC:5 * TC], ALU.mult,
                                   ALU.subtract).then_inc(s_ve, 1)
            v.wait_ge(s_ac, ACB + 139)
            v.reciprocal(stat[0:1, 3 * TC:4 * TC],
                         stat[0:1, 5 * TC:6 * TC]).then_inc(s_ve, 1)
            v.wait_ge(s_ac, ACB + 140)
            if c >= 1:
                v.wait_ge(s_st, 16 * c)
            for k in range(8):
                v.tensor_tensor(ost_v[:, k], sx_v[:, k], mb[:], ALU.subtract)
                last = v.tensor_tensor(ost_v[:, k], ost_v[:, k], rb[:], ALU.mult)
            last.then_inc(s_ve, 1)

    def em_sy(sy):
        for c in range(NCH):
            sy.wait_ge(s_ve, VEC * c + 37)
            sy.dma_start(out=x2_o[:, c * TC:(c + 1) * TC].rearrange("(r p) t -> p r t", p=128),
                         in_=ost_v).then_inc(s_st, 16)
        sy.wait_ge(s_st, 16 * NCH)

    finals = {"ld": 16 * (6 + LDC * NCH), "mm": MMC * NCH, "ac": 1 + ACC * NCH,
              "ve": VEC * NCH, "st": 16 * NCH}
    return {"g": em_g, "t": em_t, "sc": em_sc, "v": em_v, "sy": em_sy}, finals


def host_ffn0_weights(inputs, consts):
    w1 = np.asarray(inputs["L0_ffn_w1_w"]); s1 = np.asarray(inputs["L0_ffn_w1_s"])
    w2 = np.asarray(inputs["L0_ffn_w2_w"]); s2 = np.asarray(inputs["L0_ffn_w2_s"])
    h1 = np.asarray(inputs["L0_ffn_h1_w"]); sh1 = np.asarray(inputs["L0_ffn_h1_s"])
    h2 = np.asarray(inputs["L0_ffn_h2_w"]); sh2 = np.asarray(inputs["L0_ffn_h2_s"])
    w3 = np.asarray(inputs["L0_ffn_w3_w"]); s3 = np.asarray(inputs["L0_ffn_w3_s"])
    arr = lambda s, m: np.ascontiguousarray(s.reshape(m, 128).T)

    def pack_s(wT, k, ns):
        M = wT.shape[1]
        w = M // ns
        return np.stack([pack_kpm(wT[:, i * w:(i + 1) * w], k) for i in range(ns)])
    return {
        "w1T": pack_s(w1.T, 8, 2), "w2T": pack_s(w2.T, 8, 2),
        "h1T": pack_s(h1.T, 32, 8), "h2T": pack_s(h2.T, 32, 8),
        "w3T": pack_s(w3.T, 32, 2),
        "s1": arr(s1, 32), "s2": arr(s2, 32), "sh1": arr(sh1, 32),
        "sh2": arr(sh2, 32), "s3": arr(s3, 8), "cst_d": consts,
    }


def phase_sig(nc, st, d, sm, tag, TC=512):
    TOK = TOK_PC
    NCH = TOK // TC
    s_ld, s_mm, s_ac, s_st = sm["ld"], sm["mm"], sm["ac"], sm["st"]
    x2_h = d["x2_d"]
    wr_h = d["w_ret"]
    br_h = d["b_ret"]
    sig_o = d["sig_d"]

    wr = st.enter_context(nc.sbuf_tensor(tag + "wr", [128, 8 * D_MODEL], BF16))
    br = st.enter_context(nc.sbuf_tensor(tag + "br", [128, 8], F32))
    xc = st.enter_context(nc.sbuf_tensor(tag + "xc", [128, 2 * 8 * TC], BF16))
    ob = st.enter_context(nc.sbuf_tensor(tag + "ob", [128, 2 * 8 * TC], F32))
    psA = st.enter_context(nc.psum_tensor(tag + "psA", [128, TC], F32))
    psB = st.enter_context(nc.psum_tensor(tag + "psB", [128, TC], F32))

    wr_v = wr[:].rearrange("p (k m) -> p k m", k=8)
    xc_v = xc[:].rearrange("p (d k t) -> p d k t", d=2, k=8)
    ob_v = ob[:].rearrange("p (d r t) -> p d r t", d=2, r=8)

    def em_g(g):
        g.dma_start(out=wr[:], in_=wr_h).then_inc(s_ld, 16)
        g.dma_start(out=br[:], in_=br_h).then_inc(s_ld, 16)
        g.wait_ge(s_ld, 32)
        for c in range(NCH):
            if c >= 2:
                g.wait_ge(s_mm, 8 * (c - 1))
            g.dma_start(out=xc_v[:, c % 2],
                        in_=x2_h[:, c * TC:(c + 1) * TC].rearrange("(k p) t -> p k t", p=128)
                        ).then_inc(s_ld, 16)

    def em_t(t):
        t.wait_ge(s_ld, 32)
        for c in range(NCH):
            t.wait_ge(s_ld, 32 + 16 * (c + 1))
            for m in range(8):
                bank = psA if m % 2 == 0 else psB
                if 8 * c + m >= 2:
                    t.wait_ge(s_ac, 8 * c + m - 1)
                for k in range(8):
                    mm = t.matmul(bank[:], wr_v[:, k, m * 128:(m + 1) * 128],
                                  xc_v[:, c % 2, k], start=(k == 0), stop=(k == 7))
                mm.then_inc(s_mm, 1)

    def em_sc(sc):
        for c in range(NCH):
            if c >= 2:
                sc.wait_ge(s_st, 16 * (c - 1))
            for m in range(8):
                sc.wait_ge(s_mm, 8 * c + m + 1)
                bank = psA if m % 2 == 0 else psB
                sc.activation(ob_v[:, c % 2, m], bank[:], AF.Sigmoid,
                              bias=br[:, m:m + 1]).then_inc(s_ac, 1)

    def em_sy(sy):
        for c in range(NCH):
            sy.wait_ge(s_ac, 8 * (c + 1))
            sy.dma_start(out=sig_o[:, c * TC:(c + 1) * TC].rearrange("(r p) t -> p r t", p=128),
                         in_=ob_v[:, c % 2]).then_inc(s_st, 16)
        sy.wait_ge(s_st, 16 * NCH)

    finals = {"ld": 16 * (2 + NCH), "mm": 8 * NCH, "ac": 8 * NCH, "st": 16 * NCH}
    return {"g": em_g, "t": em_t, "sc": em_sc, "sy": em_sy}, finals


def phase_ema(nc, st, d, sm, tag):
    s_ld, s_ve, s_st = sm["ld"], sm["ve"], sm["st"]
    sig_h = d["sigb_d"]
    cst_h = d["cst_d"]
    ret_o = d["ret_d"]

    sg = st.enter_context(nc.sbuf_tensor(tag + "sg", [128, 4 * SEQ], F32))
    rt = st.enter_context(nc.sbuf_tensor(tag + "rt", [128, SEQ], F32))
    cst = st.enter_context(nc.sbuf_tensor(tag + "cst", [128, 8], F32))
    sg_v = sg[:].rearrange("p (r t) -> p r t", r=4)

    def em_g(g):
        g.dma_start(out=cst[:], in_=cst_h).then_inc(s_ld, 16)
        g.dma_start(out=sg_v, in_=sig_h.rearrange("(r p) t -> p r t", p=128)).then_inc(s_ld, 16)

    def em_v(v):
        v.wait_ge(s_ld, 32)
        half = cst[:, 4:5]
        for r in range(4):
            if r >= 1:
                v.wait_ge(s_st, 16 * r)
            v.tensor_scalar(sg_v[:, r], sg_v[:, r], 0.5, None, ALU.mult)
            v.tensor_tensor_scan(rt[:], half.broadcast_to([128, SEQ]), sg_v[:, r],
                                 0.0, ALU.mult, ALU.add).then_inc(s_ve, 1)

    def em_sy(sy):
        for r in range(4):
            sy.wait_ge(s_ve, r + 1)
            sy.dma_start(out=ret_o[r * 128:(r + 1) * 128, :], in_=rt[:]).then_inc(s_st, 16)
        sy.wait_ge(s_st, 64)

    finals = {"ld": 32, "ve": 4, "st": 64}
    return {"g": em_g, "v": em_v, "sy": em_sy}, finals


def phase_l1rest(nc, st, d, sm, tag, TC=256, TOK=None):
    TOK = TOK_PC if TOK is None else TOK
    NCH = TOK // TC
    s_ld, s_ve, s_mm, s_ac, s_st = sm["ld"], sm["ve"], sm["mm"], sm["ac"], sm["st"]
    x2_h = d["x2_d"]
    ret_h = d["retc_d"]
    w1_h = d["W1b"]
    w2_h = d["W2b"]
    b1_h = d["b1"]
    b2_h = d["b2"]
    g1_h = d["g1"]
    be1_h = d["be1"]
    g2_h = d["g2"]
    be2_h = d["be2"]
    cst_h = d["cst_d"]
    x3_o = d["x3_d"]

    ws = st.enter_context(nc.sbuf_tensor(tag + "ws", [128, 32 * 1024], BF16))
    x2c = st.enter_context(nc.sbuf_tensor(tag + "x2c", [128, 8 * TC], F32))
    rtc = st.enter_context(nc.sbuf_tensor(tag + "rtc", [128, 8 * TC], F32))
    hb = st.enter_context(nc.sbuf_tensor(tag + "hb", [128, 8 * TC], F32))
    h16 = st.enter_context(nc.sbuf_tensor(tag + "h16", [128, 8 * TC], BF16))
    gb = st.enter_context(nc.sbuf_tensor(tag + "gb", [128, 32 * TC], BF16))
    vb = st.enter_context(nc.sbuf_tensor(tag + "vb", [128, 2 * TC], F32))
    v2b = st.enter_context(nc.sbuf_tensor(tag + "v2b", [128, TC], F32))
    thb = st.enter_context(nc.sbuf_tensor(tag + "thb", [128, TC], F32))
    sx = st.enter_context(nc.sbuf_tensor(tag + "sx", [128, 8 * TC], F32))
    sq = st.enter_context(nc.sbuf_tensor(tag + "sq", [128, 8 * TC], F32))
    stat = st.enter_context(nc.sbuf_tensor(tag + "stat", [1, 6 * TC], F32))
    mb = st.enter_context(nc.sbuf_tensor(tag + "mb", [128, TC], F32))
    rb = st.enter_context(nc.sbuf_tensor(tag + "rb", [128, TC], F32))
    ost = st.enter_context(nc.sbuf_tensor(tag + "ost", [128, 8 * TC], F32))
    b1s = st.enter_context(nc.sbuf_tensor(tag + "b1s", [128, 32], F32))
    b2s = st.enter_context(nc.sbuf_tensor(tag + "b2s", [128, 8], F32))
    g1s = st.enter_context(nc.sbuf_tensor(tag + "g1s", [128, 8], F32))
    be1s = st.enter_context(nc.sbuf_tensor(tag + "be1s", [128, 8], F32))
    g2s = st.enter_context(nc.sbuf_tensor(tag + "g2s", [128, 8], F32))
    be2s = st.enter_context(nc.sbuf_tensor(tag + "be2s", [128, 8], F32))
    cst = st.enter_context(nc.sbuf_tensor(tag + "cst", [128, 8], F32))
    ones = st.enter_context(nc.sbuf_tensor(tag + "ones", [128, 2], F32))
    ps = [st.enter_context(nc.psum_tensor(f"{tag}ps{i}", [128, TC], F32)) for i in range(4)]
    psm = st.enter_context(nc.psum_tensor(tag + "psm", [1, TC], F32))
    psq = st.enter_context(nc.psum_tensor(tag + "psq", [1, TC], F32))
    psbc = st.enter_context(nc.psum_tensor(tag + "psbc", [128, 2 * TC], F32))

    ws8 = ws[:].rearrange("p (k m) -> p k m", k=8)
    ws32 = ws[:].rearrange("p (k m) -> p k m", k=32)
    x2c_v = x2c[:].rearrange("p (r t) -> p r t", r=8)
    rtc_v = rtc[:].rearrange("p (r t) -> p r t", r=8)
    hb_v = hb[:].rearrange("p (r t) -> p r t", r=8)
    h16_v = h16[:].rearrange("p (r t) -> p r t", r=8)
    gb_v = gb[:].rearrange("p (m t) -> p m t", m=32)
    vb_v = vb[:].rearrange("p (d t) -> p d t", d=2)
    sx_v = sx[:].rearrange("p (r t) -> p r t", r=8)
    sq_v = sq[:].rearrange("p (r t) -> p r t", r=8)
    ost_v = ost[:].rearrange("p (r t) -> p r t", r=8)
    LDC = 6
    HWG = 16 * 1024
    MMC = 32 + 8 + 3 * 4
    ACC = 32 + 8 + 32 * 1 + 3 * 4 + 3 + 1
    VEC = None

    mmn = [0]; acn = [1]; ven = [0]

    MMCc = 52
    ACCc = 86
    VECc = 46

    def em_g(g):
        for w, dst in [(cst_h, cst[:]), (b1_h, b1s[:]), (b2_h, b2s[:]),
                       (g1_h, g1s[:]), (be1_h, be1s[:]), (g2_h, g2s[:]), (be2_h, be2s[:])]:
            g.dma_start(out=dst, in_=w).then_inc(s_ld, 16)
        g.wait_ge(s_ld, 112)
        for c in range(NCH):
            sl = slice(c * TC, (c + 1) * TC)
            if c >= 1:
                g.wait_ge(s_ve, VECc * (c - 1) + 43)
            g.dma_start(out=x2c_v, in_=x2_h[:, sl].rearrange("(r p) t -> p r t", p=128)
                        ).then_inc(s_ld, 16)
            if c >= 1:
                g.wait_ge(s_ve, VECc * (c - 1) + 1)
            g.dma_start(out=rtc_v, in_=ret_h[:, sl].rearrange("(r p) t -> p r t", p=128)
                        ).then_inc(s_ld, 16)
            if c >= 1:
                g.wait_ge(s_mm, MMCc * (c - 1) + 40)
            g.dma_start(out=ws[:, 0:HWG], in_=w1_h[0]).then_inc(s_ld, 16)
            if c >= 1:
                g.wait_ge(s_mm, MMCc * (c - 1) + 44)
            g.dma_start(out=ws[:, HWG:2 * HWG], in_=w1_h[1]).then_inc(s_ld, 16)
            g.wait_ge(s_mm, MMCc * c + 20)
            g.dma_start(out=ws[:, 0:HWG], in_=w2_h[0]).then_inc(s_ld, 16)
            g.wait_ge(s_mm, MMCc * c + 36)
            g.dma_start(out=ws[:, HWG:2 * HWG], in_=w2_h[1]).then_inc(s_ld, 16)

    def emit_ln_tensor(t, c, src_v, base_mm, base_ac_sq, ve_src, ve_rstd):
        t.wait_ge(s_ve, ve_src)
        for k in range(8):
            mm = t.matmul(psm[:], ones[:, 0:1], src_v[:, k], start=(k == 0), stop=(k == 7))
        mm.then_inc(s_mm, 1)
        t.wait_ge(s_ac, base_ac_sq)
        for k in range(8):
            mm = t.matmul(psq[:], ones[:, 0:1], sq_v[:, k], start=(k == 0), stop=(k == 7))
        mm.then_inc(s_mm, 1)
        t.wait_ge(s_ve, ve_rstd)
        t.matmul(psbc[:, 0:TC], ones[0:1, 0:1].broadcast_to([1, 128]), stat[0:1, 0:TC],
                 start=True, stop=True).then_inc(s_mm, 1)
        t.matmul(psbc[:, TC:2 * TC], ones[0:1, 0:1].broadcast_to([1, 128]),
                 stat[0:1, 3 * TC:4 * TC], start=True, stop=True).then_inc(s_mm, 1)

    def em_t(t):
        t.wait_ge(s_ld, 112)
        for c in range(NCH):
            MB = MMCc * c
            AB = 1 + ACCc * c
            emit_ln_tensor(t, c, sx_v, MB, AB + 1, VECc * c + 1, VECc * c + 3)
            t.wait_ge(s_ve, VECc * c + 5)
            for m in range(32):
                j = m + 1
                half = m // 16
                wsv = ws[:, half * HWG:(half + 1) * HWG].rearrange("p (k m) -> p k m", k=8)
                t.wait_ge(s_ld, 112 + 16 * (LDC * c + 3 + half))
                if j > 4:
                    t.wait_ge(s_ac, 1 + ACCc * c + 2 * j - 4)
                elif c >= 1:
                    t.wait_ge(s_ac, 1 + ACCc * (c - 1) + 73 + j)
                for k in range(8):
                    mm = t.matmul(ps[(j - 1) % 4][:],
                                  wsv[:, k, (m % 16) * 128:(m % 16 + 1) * 128],
                                  h16_v[:, k], start=(k == 0), stop=(k == 7))
                mm.then_inc(s_mm, 1)
            t.wait_ge(s_ve, VECc * c + 38)
            for m in range(8):
                j = m + 1
                half = m // 4
                wsv = ws[:, half * HWG:(half + 1) * HWG].rearrange("p (k m) -> p k m", k=32)
                t.wait_ge(s_ld, 112 + 16 * (LDC * c + 5 + half))
                AB = 1 + ACCc * c
                t.wait_ge(s_ac, AB + 66 + m if m >= 4 else AB + 62 + 2 * m)
                for k in range(32):
                    mm = t.matmul(ps[(j - 1) % 4][:],
                                  wsv[:, k, (m % 4) * 128:(m % 4 + 1) * 128],
                                  gb_v[:, k], start=(k == 0), stop=(k == 31))
                mm.then_inc(s_mm, 1)
            emit_ln_tensor(t, c, sx_v, MB + 44, AB + 78, VECc * c + 39, VECc * c + 41)
            emit_ln_tensor(t, c, sx_v, MB + 48, AB + 83, VECc * c + 43, VECc * c + 45)

    def em_sc(sc):
        sc.wait_ge(s_ld, 112)
        sc.copy(ones[:], cst[:, 3:4].broadcast_to([128, 2])).then_inc(s_ac, 1)
        for c in range(NCH):
            AB = 1 + ACCc * c
            MB = MMCc * c
            sc.wait_ge(s_ve, VECc * c + 1)
            sc.activation(sq[:], sx[:], AF.Square).then_inc(s_ac, 1)
            sc.wait_ge(s_mm, MB + 2)
            sc.activation(stat[0:1, 0:TC], psm[0:1, :], AF.Identity, scale=cst[0:1, 7:8])
            sc.activation(stat[0:1, TC:2 * TC], psq[0:1, :], AF.Identity,
                          scale=cst[0:1, 7:8]).then_inc(s_ac, 1)
            sc.wait_ge(s_ve, VECc * c + 2)
            sc.activation(stat[0:1, 5 * TC:6 * TC], stat[0:1, 4 * TC:5 * TC], AF.Sqrt,
                          bias=cst[0:1, 6:7]).then_inc(s_ac, 1)
            sc.wait_ge(s_mm, MB + 4)
            sc.copy(mb[:], psbc[:, 0:TC])
            sc.copy(rb[:], psbc[:, TC:2 * TC]).then_inc(s_ac, 1)
            sc.wait_ge(s_ve, VECc * c + 4)
            for r in range(8):
                a = sc.activation(hb_v[:, r], hb_v[:, r], AF.Identity,
                                  scale=g1s[:, r:r + 1], bias=be1s[:, r:r + 1])
            a.then_inc(s_ac, 1)
            for m in range(32):
                sc.wait_ge(s_mm, MB + 4 + m + 1)
                if m >= 2:
                    sc.wait_ge(s_ve, VECc * c + 5 + m)
                sc.activation(vb_v[:, m % 2], ps[m % 4][:], AF.Identity,
                              bias=b1s[:, m:m + 1]).then_inc(s_ac, 1)
                sc.wait_ge(s_ve, VECc * c + 6 + m)
                sc.activation(thb[:], v2b[:], AF.Tanh,
                              scale=cst[:, 5:6]).then_inc(s_ac, 1)
            for m in range(8):
                sc.wait_ge(s_mm, MB + 36 + m + 1)
                sc.activation(sx_v[:, m], ps[m % 4][:], AF.Identity,
                              bias=b2s[:, m:m + 1]).then_inc(s_ac, 1)
            sc.wait_ge(s_ve, VECc * c + 39)
            sc.activation(sq[:], sx[:], AF.Square).then_inc(s_ac, 1)
            sc.wait_ge(s_mm, MB + 46)
            sc.activation(stat[0:1, 0:TC], psm[0:1, :], AF.Identity, scale=cst[0:1, 7:8])
            sc.activation(stat[0:1, TC:2 * TC], psq[0:1, :], AF.Identity,
                          scale=cst[0:1, 7:8]).then_inc(s_ac, 1)
            sc.wait_ge(s_ve, VECc * c + 40)
            sc.activation(stat[0:1, 5 * TC:6 * TC], stat[0:1, 4 * TC:5 * TC], AF.Sqrt,
                          bias=cst[0:1, 6:7]).then_inc(s_ac, 1)
            sc.wait_ge(s_mm, MB + 48)
            sc.copy(mb[:], psbc[:, 0:TC])
            sc.copy(rb[:], psbc[:, TC:2 * TC]).then_inc(s_ac, 1)
            sc.wait_ge(s_ve, VECc * c + 42)
            for r in range(8):
                a = sc.activation(sx_v[:, r], sx_v[:, r], AF.Identity,
                                  scale=g2s[:, r:r + 1], bias=be2s[:, r:r + 1])
            a.then_inc(s_ac, 1)
            sc.wait_ge(s_ve, VECc * c + 43)
            sc.activation(sq[:], sx[:], AF.Square).then_inc(s_ac, 1)
            sc.wait_ge(s_mm, MB + 50)
            sc.activation(stat[0:1, 0:TC], psm[0:1, :], AF.Identity, scale=cst[0:1, 7:8])
            sc.activation(stat[0:1, TC:2 * TC], psq[0:1, :], AF.Identity,
                          scale=cst[0:1, 7:8]).then_inc(s_ac, 1)
            sc.wait_ge(s_ve, VECc * c + 44)
            sc.activation(stat[0:1, 5 * TC:6 * TC], stat[0:1, 4 * TC:5 * TC], AF.Sqrt,
                          bias=cst[0:1, 6:7]).then_inc(s_ac, 1)
            sc.wait_ge(s_mm, MB + 52)
            sc.copy(mb[:], psbc[:, 0:TC])
            sc.copy(rb[:], psbc[:, TC:2 * TC]).then_inc(s_ac, 1)

    def em_v(v):
        for c in range(NCH):
            AB = 1 + ACCc * c
            v.wait_ge(s_ld, 112 + 16 * (LDC * c + 2))
            if c >= 1:
                v.wait_ge(s_ac, 1 + ACCc * (c - 1) + 86)
            v.tensor_tensor(sx_v[:], x2c_v[:], rtc_v[:], ALU.add).then_inc(s_ve, 1)
            v.wait_ge(s_ac, AB + 2)
            v.tensor_tensor(stat[0:1, 4 * TC:5 * TC], stat[0:1, 0:TC], stat[0:1, 0:TC], ALU.mult)
            v.scalar_tensor_tensor(stat[0:1, 4 * TC:5 * TC], stat[0:1, TC:2 * TC], 1.0,
                                   stat[0:1, 4 * TC:5 * TC], ALU.mult, ALU.subtract
                                   ).then_inc(s_ve, 1)
            v.wait_ge(s_ac, AB + 3)
            v.reciprocal(stat[0:1, 3 * TC:4 * TC], stat[0:1, 5 * TC:6 * TC]).then_inc(s_ve, 1)
            v.wait_ge(s_ac, AB + 4)
            for k in range(8):
                v.tensor_tensor(hb_v[:, k], sx_v[:, k], mb[:], ALU.subtract)
                last = v.tensor_tensor(hb_v[:, k], hb_v[:, k], rb[:], ALU.mult)
            last.then_inc(s_ve, 1)
            v.wait_ge(s_ac, AB + 5)
            v.tensor_copy(h16[:], hb[:]).then_inc(s_ve, 1)
            for m in range(32):
                v.wait_ge(s_ac, AB + 6 + 2 * m)
                vv = vb_v[:, m % 2]
                v.tensor_tensor(v2b[:], vv, vv, ALU.mult)
                v.tensor_tensor(v2b[:], v2b[:], vv, ALU.mult)
                v.scalar_tensor_tensor(v2b[:], v2b[:], GELU_TANH_C, vv, ALU.mult,
                                       ALU.add).then_inc(s_ve, 1)
                v.wait_ge(s_ac, AB + 7 + 2 * m)
                v.tensor_scalar(thb[:], thb[:], 1.0, 0.5, ALU.add, ALU.mult)
                gtt = v.tensor_tensor(gb_v[:, m], vv, thb[:], ALU.mult)
            gtt.then_inc(s_ve, 1)
            v.wait_ge(s_ac, AB + 77)
            v.tensor_tensor(sx_v[:], hb_v[:], sx_v[:], ALU.add).then_inc(s_ve, 1)
            v.wait_ge(s_ac, AB + 79)
            v.tensor_tensor(stat[0:1, 4 * TC:5 * TC], stat[0:1, 0:TC], stat[0:1, 0:TC], ALU.mult)
            v.scalar_tensor_tensor(stat[0:1, 4 * TC:5 * TC], stat[0:1, TC:2 * TC], 1.0,
                                   stat[0:1, 4 * TC:5 * TC], ALU.mult, ALU.subtract
                                   ).then_inc(s_ve, 1)
            v.wait_ge(s_ac, AB + 80)
            v.reciprocal(stat[0:1, 3 * TC:4 * TC], stat[0:1, 5 * TC:6 * TC]).then_inc(s_ve, 1)
            v.wait_ge(s_ac, AB + 81)
            for k in range(8):
                v.tensor_tensor(sx_v[:, k], sx_v[:, k], mb[:], ALU.subtract)
                last = v.tensor_tensor(sx_v[:, k], sx_v[:, k], rb[:], ALU.mult)
            last.then_inc(s_ve, 1)
            v.wait_ge(s_ac, AB + 82)
            v.tensor_tensor(sx_v[:], x2c_v[:], sx_v[:], ALU.add).then_inc(s_ve, 1)
            v.wait_ge(s_ac, AB + 84)
            v.tensor_tensor(stat[0:1, 4 * TC:5 * TC], stat[0:1, 0:TC], stat[0:1, 0:TC], ALU.mult)
            v.scalar_tensor_tensor(stat[0:1, 4 * TC:5 * TC], stat[0:1, TC:2 * TC], 1.0,
                                   stat[0:1, 4 * TC:5 * TC], ALU.mult, ALU.subtract
                                   ).then_inc(s_ve, 1)
            v.wait_ge(s_ac, AB + 85)
            v.reciprocal(stat[0:1, 3 * TC:4 * TC], stat[0:1, 5 * TC:6 * TC]).then_inc(s_ve, 1)
            v.wait_ge(s_ac, AB + 86)
            if c >= 1:
                v.wait_ge(s_st, 16 * c)
            for k in range(8):
                v.tensor_tensor(ost_v[:, k], sx_v[:, k], mb[:], ALU.subtract)
                last = v.tensor_tensor(ost_v[:, k], ost_v[:, k], rb[:], ALU.mult)
            last.then_inc(s_ve, 1)

    def em_sy(sy):
        for c in range(NCH):
            sy.wait_ge(s_ve, VECc * c + 46)
            sy.dma_start(out=x3_o[:, c * TC:(c + 1) * TC].rearrange("(r p) t -> p r t", p=128),
                         in_=ost_v).then_inc(s_st, 16)
        sy.wait_ge(s_st, 16 * NCH)

    finals = {"ld": 16 * (7 + LDC * NCH), "mm": MMCc * NCH, "ac": 1 + ACCc * NCH,
              "ve": VECc * NCH, "st": 16 * NCH}
    return {"g": em_g, "t": em_t, "sc": em_sc, "v": em_v, "sy": em_sy}, finals


class KSched:
    def __init__(self):
        self.steps = []
        self.cnt = {}
        self.marks = {}

    def add(self, eng, fn, waits=(), mark=None, sem=None, step=1):
        if mark is not None:
            assert sem is not None
            v = self.cnt.get(sem, 0) + step
            self.cnt[sem] = v
            assert mark not in self.marks, mark
            self.marks[mark] = (sem, v)
        for w in waits:
            assert w in self.marks, f"wait on undefined mark {w}"
        self.steps.append((eng, tuple(waits), fn, mark, sem, step))

    def emit_for(self, name, engobj, sems):
        for eng, waits, fn, mark, sem, step in self.steps:
            if eng != name:
                continue
            for w in waits:
                ws, wv = self.marks[w]
                engobj.wait_ge(sems[ws], wv)
            inst = fn(engobj)
            if mark is not None:
                assert inst is not None, f"step {mark} returned no instruction"
                inst.then_inc(sems[sem], step)

    def final(self, sem):
        return self.cnt.get(sem, 0)


def phase_moe(nc, st, d, sm, tag, TC=256, TOK=None):
    TOK = TOK_PC if TOK is None else TOK
    NCH = TOK // TC
    NSB = TC // 128
    x3_h = d["x3_d"]
    gw_h = d["gate_w_d"]
    gb_h = d["gate_biasR"]
    wg_h = d["wgb"]
    wu_h = d["wub"]
    wd_h = d["wdb"]
    sg_h = d["sw_gate"]
    su_h = d["sw_up"]
    sd_h = d["sw_down"]
    ml4_h = d["ml4_d"]
    ml8_h = d["ml8_d"]
    idf_h = d["idf_d"]
    cst_h = d["cst_d"]
    x4_o = d["x4"]
    w8d = d["w8d"]

    swg = st.enter_context(nc.sbuf_tensor(tag + "swg", [128, 8 * 1024], BF16))
    swu = st.enter_context(nc.sbuf_tensor(tag + "swu", [128, 8 * 1024], BF16))
    swd = st.enter_context(nc.sbuf_tensor(tag + "swd", [128, 8 * 1024], BF16))
    gw = st.enter_context(nc.sbuf_tensor(tag + "gw", [128, 8 * 8], F32))
    gbias = st.enter_context(nc.sbuf_tensor(tag + "gbias", [128, 8], F32))
    ml4 = st.enter_context(nc.sbuf_tensor(tag + "ml4", [128, 16], F32))
    ml8 = st.enter_context(nc.sbuf_tensor(tag + "ml8", [128, 64], F32))
    idf = st.enter_context(nc.sbuf_tensor(tag + "idf", [128, 128], F32))
    cst = st.enter_context(nc.sbuf_tensor(tag + "cst", [128, 8], F32))
    ones = st.enter_context(nc.sbuf_tensor(tag + "ones", [128, 2], F32))
    x3c = st.enter_context(nc.sbuf_tensor(tag + "x3c", [128, 8 * TC], F32))
    x16 = st.enter_context(nc.sbuf_tensor(tag + "x16", [128, 8 * TC], BF16))
    web = st.enter_context(nc.sbuf_tensor(tag + "web", [128, 2 * 12 * 1024], BF16))
    gt = st.enter_context(nc.sbuf_tensor(tag + "gt", [128, 512], F32))
    w8s = st.enter_context(nc.sbuf_tensor(tag + "w8s", [128, 8 * NSB], F32))
    w8T = st.enter_context(nc.sbuf_tensor(tag + "w8T", [8, TC], F32))
    w8bc = st.enter_context(nc.sbuf_tensor(tag + "w8bc", [128, 8 * TC], F32))
    shg = st.enter_context(nc.sbuf_tensor(tag + "shg", [128, 8 * TC], F32))
    sh16 = st.enter_context(nc.sbuf_tensor(tag + "sh16", [128, 8 * TC], BF16))
    ge32 = st.enter_context(nc.sbuf_tensor(tag + "ge32", [128, 4 * TC], F32))
    ysb = st.enter_context(nc.sbuf_tensor(tag + "ysb", [128, 8 * TC], F32))
    gu16 = st.enter_context(nc.sbuf_tensor(tag + "gu16", [128, 2 * 4 * TC], BF16))
    sx = st.enter_context(nc.sbuf_tensor(tag + "sx", [128, 8 * TC], F32))
    sq = st.enter_context(nc.sbuf_tensor(tag + "sq", [128, 8 * TC], F32))
    stat = st.enter_context(nc.sbuf_tensor(tag + "stat", [1, 6 * TC], F32))
    mbx = st.enter_context(nc.sbuf_tensor(tag + "mbx", [128, TC], F32))
    rbx = st.enter_context(nc.sbuf_tensor(tag + "rbx", [128, TC], F32))
    ost = st.enter_context(nc.sbuf_tensor(tag + "ost", [128, 8 * TC], F32))
    psA = st.enter_context(nc.psum_tensor(tag + "psA", [128, 512], F32))
    psB = st.enter_context(nc.psum_tensor(tag + "psB", [128, 512], F32))
    psbc = st.enter_context(nc.psum_tensor(tag + "psbc", [128, 2 * TC], F32))
    psgT_t = st.enter_context(nc.psum_tensor(tag + "psgT", [128, 136], F32))
    psg = psgT_t[:, 0:8]
    psT = psgT_t[0:8, 8:136]
    psmq_t = st.enter_context(nc.psum_tensor(tag + "psmq", [1, 2 * TC], F32))
    psm = psmq_t[0:1, 0:TC]
    psq = psmq_t[0:1, TC:2 * TC]
    sems = {"s_ld": sm["ld"], "s_ve": sm["ve"], "s_mm": sm["mm"],
            "s_ac": sm["ac"], "s_st": sm["st"]}

    swg_v = swg[:].rearrange("p (k m) -> p k m", k=8)
    swu_v = swu[:].rearrange("p (k m) -> p k m", k=8)
    swd_v = swd[:].rearrange("p (k m) -> p k m", k=8)
    gw_v = gw[:].rearrange("p (k e) -> p k e", k=8)
    x3c_v = x3c[:].rearrange("p (r t) -> p r t", r=8)
    x16_v = x16[:].rearrange("p (r t) -> p r t", r=8)
    web_v = web[:].rearrange("p (d c) -> p d c", d=2)
    shg_v = shg[:].rearrange("p (m t) -> p m t", m=8)
    sh16_v = sh16[:].rearrange("p (m t) -> p m t", m=8)
    ge_v = ge32[:].rearrange("p (m t) -> p m t", m=4)
    gu16_v = gu16[:].rearrange("p (d m t) -> p d m t", d=2, m=4)
    w8bc_v = w8bc[:].rearrange("p (e t) -> p e t", e=8)
    w8s_v = w8s[:].rearrange("p (s e) -> p s e", s=NSB)
    ysb_v = ysb[:].rearrange("p (m t) -> p m t", m=8)
    sx_v = sx[:].rearrange("p (r t) -> p r t", r=8)
    sq_v = sq[:].rearrange("p (r t) -> p r t", r=8)
    ost_v = ost[:].rearrange("p (r t) -> p r t", r=8)

    S = KSched()
    A = S.add
    for i, (w, dst) in enumerate([(cst_h, cst), (gb_h, gbias), (ml4_h, ml4),
                                  (ml8_h, ml8), (idf_h, idf)]):
        A("g", (lambda w=w, dst=dst: (lambda g: g.dma_start(out=dst[:], in_=w)))(),
          mark=f"up{i}", sem="s_ld", step=16)
    A("g", lambda g: g.dma_start(out=gw[:], in_=gw_h),
      mark="up5", sem="s_ld", step=16)
    A("g", lambda g: g.dma_start(out=swg[:], in_=sg_h),
      mark="up6", sem="s_ld", step=16)
    A("g", lambda g: g.dma_start(out=swu[:], in_=su_h),
      mark="up7", sem="s_ld", step=16)
    A("g", lambda g: g.dma_start(out=swd[:], in_=sd_h),
      mark="up8", sem="s_ld", step=16)
    A("sc", lambda sc: sc.copy(ones[:], cst[:, 3:4].broadcast_to([128, 2])),
      waits=["up8"], mark="ones", sem="s_ac")
    A("g", lambda g: g.wait_ge(sems["s_ld"], 9 * 16))

    for c in range(NCH):
        P = f"c{c}_"
        sl = slice(c * TC, (c + 1) * TC)
        A("g", (lambda sl=sl: (lambda g: g.dma_start(
            out=x3c_v, in_=x3_h[:, sl].rearrange("(r p) t -> p r t", p=128))))(),
          waits=([f"c{c-1}_xhat"] if c else []), mark=P + "ldx", sem="s_ld", step=16)
        A("v", lambda v: v.tensor_copy(x16[:], x3c[:]),
          waits=[P + "ldx"] + ([f"c{c-1}_resid"] if c else []),
          mark=P + "cast", sem="s_ve")
        for s in range(NSB):
            Q = P + f"g{s}_"
            def gmm(s=s):
                def f(t):
                    for k in range(8):
                        mm = t.matmul(psg, x3c_v[:, k, s * 128:(s + 1) * 128], gw_v[:, k],
                                      start=(k == 0), stop=(k == 7))
                    return mm
                return f
            A("t", gmm(), waits=[P + "ldx", "up5"] +
              ([P + f"g{s-1}_sig"] if s else ([f"c{c-1}_g{NSB-1}_w8T"] if c else [])),
              mark=Q + "mm", sem="s_mm")
            A("sc", (lambda s=s: (lambda sc: sc.activation(
                w8s_v[:, s], psg, AF.Sigmoid)))(),
              waits=[Q + "mm"], mark=Q + "sig", sem="s_ac")
        def gchain2():
            def f(v):
                G = [gt[:, 256 * s:256 * (s + 1)] for s in range(NSB)]
                last = None
                def both(fn):
                    nonlocal last
                    for s in range(NSB):
                        last = fn(G[s], s)
                both(lambda g, s: v.tensor_tensor(g[:, 0:8], w8s_v[:, s], gbias[:], ALU.add))
                both(lambda g, s: v.tensor_tensor(
                    g[:, 8:12], g[:, 0:8].rearrange("p (j i) -> p j i", j=4)[:, :, 0],
                    g[:, 0:8].rearrange("p (j i) -> p j i", j=4)[:, :, 1], ALU.add))
                def cmp4(g, s):
                    gs = g[:, 8:12]
                    v.tensor_tensor(g[:, 16:32].rearrange("p (j jp) -> p j jp", j=4),
                                    gs.unsqueeze(1).broadcast_to([128, 4, 4]),
                                    gs.unsqueeze(2).broadcast_to([128, 4, 4]), ALU.is_gt)
                    return v.tensor_tensor(
                        g[:, 32:48].rearrange("p (j jp) -> p j jp", j=4),
                        gs.unsqueeze(1).broadcast_to([128, 4, 4]),
                        gs.unsqueeze(2).broadcast_to([128, 4, 4]), ALU.is_equal)
                both(cmp4)
                both(lambda g, s: v.tensor_tensor(
                    g[:, 32:48], g[:, 32:48], ml4[:], ALU.mult))
                both(lambda g, s: v.tensor_tensor(g[:, 16:32], g[:, 16:32], g[:, 32:48],
                                                  ALU.add))
                both(lambda g, s: v.tensor_reduce(
                    g[:, 48:52], g[:, 16:32].rearrange("p (j jp) -> p j jp", j=4),
                    AXL.X, ALU.add))
                both(lambda g, s: v.tensor_scalar(g[:, 52:56], g[:, 48:52], 2.0, None,
                                                  ALU.is_lt))
                both(lambda g, s: v.tensor_tensor(
                    g[:, 56:64].rearrange("p (j i) -> p j i", j=4),
                    g[:, 0:8].rearrange("p (j i) -> p j i", j=4),
                    g[:, 52:56].unsqueeze(2).broadcast_to([128, 4, 2]), ALU.mult))
                def cmp8(g, s):
                    tm = g[:, 56:64]
                    v.tensor_tensor(g[:, 64:128].rearrange("p (e ep) -> p e ep", e=8),
                                    tm.unsqueeze(1).broadcast_to([128, 8, 8]),
                                    tm.unsqueeze(2).broadcast_to([128, 8, 8]), ALU.is_gt)
                    return v.tensor_tensor(
                        g[:, 128:192].rearrange("p (e ep) -> p e ep", e=8),
                        tm.unsqueeze(1).broadcast_to([128, 8, 8]),
                        tm.unsqueeze(2).broadcast_to([128, 8, 8]), ALU.is_equal)
                both(cmp8)
                both(lambda g, s: v.tensor_tensor(g[:, 128:192], g[:, 128:192], ml8[:],
                                                  ALU.mult))
                both(lambda g, s: v.tensor_tensor(g[:, 64:128], g[:, 64:128],
                                                  g[:, 128:192], ALU.add))
                both(lambda g, s: v.tensor_reduce(
                    g[:, 192:200], g[:, 64:128].rearrange("p (e ep) -> p e ep", e=8),
                    AXL.X, ALU.add))
                both(lambda g, s: v.tensor_scalar(g[:, 200:208], g[:, 192:200], 2.0, None,
                                                  ALU.is_lt))
                both(lambda g, s: v.tensor_tensor(g[:, 208:216], g[:, 200:208],
                                                  g[:, 0:8], ALU.mult))
                both(lambda g, s: v.tensor_reduce(g[:, 216:217],
                                                  g[:, 208:216].unsqueeze(1), AXL.X,
                                                  ALU.add))
                both(lambda g, s: v.tensor_scalar(g[:, 216:217], g[:, 216:217], 1e-20,
                                                  None, ALU.add))
                both(lambda g, s: v.reciprocal(g[:, 217:218], g[:, 216:217]))
                both(lambda g, s: v.tensor_scalar(w8s_v[:, s], g[:, 208:216],
                                                  g[:, 217:218], 2.5, ALU.mult, ALU.mult))
                return last
            return f
        A("v", gchain2(), waits=[P + f"g{NSB-1}_sig", "up8"], mark=P + "w8all", sem="s_ve")
        for s in range(NSB):
            Q = P + f"g{s}_"
            A("t", (lambda s=s: (lambda t: t.transpose(psT, w8s_v[:, s], idf[:])))(),
              waits=[P + "w8all"] + ([P + f"g{s-1}_w8T"] if s else ([f"c{c-1}_g{NSB-1}_w8T"] if c else [])),
              mark=Q + "tr", sem="s_mm")
            A("sc", (lambda s=s: (lambda sc: sc.copy(w8T[:, s * 128:(s + 1) * 128], psT)))(),
              waits=[Q + "tr"], mark=Q + "w8T", sem="s_ac")
        A("sy", lambda sy: sy.dma_start(out=w8d, in_=w8T[:]),
          waits=[P + f"g{NSB-1}_w8T"] + ([f"c{c-1}_ldw8"] if c else []),
          mark=P + "stw8", sem="s_st", step=16)
        A("g", lambda g: g.dma_start(out=w8bc_v, in_=w8d.unsqueeze(0).broadcast_to([128, 8, TC])),
          waits=[P + "stw8"] + ([f"c{c-1}_e7_guTT"] if c else []),
          mark=P + "ldw8", sem="s_ld", step=16)
        bankq = [None, None]

        def bwait(par, extra=()):
            w = list(extra)
            if bankq[par] is not None:
                w.append(bankq[par])
            return w

        def kmm(bank, wv, kk, rhs_v, mslice):
            def f(t):
                for k in range(kk):
                    mm = t.matmul(bank[:, 0:TC], wv[:, k, mslice], rhs_v[:, k],
                                  start=(k == 0), stop=(k == kk - 1))
                return mm
            return f
        ti = 0
        for m in range(8):
            Q = P + f"sh{m}_"
            pg, pu = ti % 2, (ti + 1) % 2
            A("t", kmm([psA, psB][pg], swg_v, 8, x16_v, slice(m * 128, (m + 1) * 128)),
              waits=bwait(pg, [P + "cast", "up8"]), mark=Q + "mmg", sem="s_mm")
            A("sc", (lambda m=m, pg=pg: (lambda sc: sc.activation(
                shg_v[:, m], [psA, psB][pg][:, 0:TC], AF.Silu)))(),
              waits=[Q + "mmg"], mark=Q + "evg", sem="s_ac")
            bankq[pg] = Q + "evg"
            A("t", kmm([psA, psB][pu], swu_v, 8, x16_v, slice(m * 128, (m + 1) * 128)),
              waits=bwait(pu), mark=Q + "mmu", sem="s_mm")
            A("v", (lambda m=m, pu=pu: (lambda v: v.tensor_tensor(
                sh16_v[:, m], shg_v[:, m], [psA, psB][pu][:, 0:TC], ALU.mult)))(),
              waits=[Q + "mmu", Q + "evg"], mark=Q + "sh16", sem="s_ve")
            bankq[pu] = Q + "sh16"
            ti += 2
        for m in range(8):
            Q = P + f"sd{m}_"
            pp = ti % 2
            A("t", kmm([psA, psB][pp], swd_v, 8, sh16_v, slice(m * 128, (m + 1) * 128)),
              waits=bwait(pp, [P + "sh7_sh16"]), mark=Q + "mm", sem="s_mm")
            A("sc", (lambda m=m, pp=pp: (lambda sc: sc.copy(
                ysb_v[:, m], [psA, psB][pp][:, 0:TC])))(),
              waits=[Q + "mm"] + ([f"c{c-1}_resid"] if c and m == 0 else []),
              mark=Q + "ev", sem="s_ac")
            bankq[pp] = Q + "ev"
            ti += 1
        for e in range(8):
            Q = P + f"e{e}_"
            d = e % 2
            ei = c * 8 + e
            wts = []
            if ei >= 2:
                wts.append(f"c{(ei-2)//8}_e{(ei-2)%8}_dnlast")
            A("g", (lambda e=e, d=d: (lambda g: g.dma_start(
                out=web[:, d * 12288:d * 12288 + 4096], in_=wg_h[e])))(),
              waits=wts, mark=Q + "lg", sem="s_ld", step=16)
            A("g", (lambda e=e, d=d: (lambda g: g.dma_start(
                out=web[:, d * 12288 + 4096:d * 12288 + 8192], in_=wu_h[e])))(),
              mark=Q + "lu", sem="s_ld", step=16)
            A("g", (lambda e=e, d=d: (lambda g: g.dma_start(
                out=web[:, d * 12288 + 8192:d * 12288 + 12288], in_=wd_h[e])))(),
              mark=Q + "ld", sem="s_ld", step=16)
            wgv = web_v[:, d, 0:8 * 512].rearrange("p (k m) -> p k m", k=8)
            wuv = web_v[:, d, 8 * 512:16 * 512].rearrange("p (k m) -> p k m", k=8)
            wdv = web_v[:, d, 16 * 512:16 * 512 + 4 * 1024].rearrange("p (k m) -> p k m", k=4)
            for m in range(4):
                R = Q + f"m{m}_"
                pg, pu = ti % 2, (ti + 1) % 2
                A("t", kmm([psA, psB][pg], wgv, 8, x16_v, slice(m * 128, (m + 1) * 128)),
                  waits=bwait(pg, [Q + "lg"]), mark=R + "mmg", sem="s_mm")
                gw_wts = [R + "mmg"]
                if m == 0 and e >= 1:
                    gw_wts.append(P + f"e{e-1}_guTT")
                elif m == 0 and c >= 1:
                    gw_wts.append(f"c{c-1}_e7_guTT")
                A("sc", (lambda m=m, pg=pg: (lambda sc: sc.activation(
                    ge_v[:, m], [psA, psB][pg][:, 0:TC], AF.Silu)))(),
                  waits=gw_wts, mark=R + "evg", sem="s_ac")
                bankq[pg] = R + "evg"
                A("t", kmm([psA, psB][pu], wuv, 8, x16_v, slice(m * 128, (m + 1) * 128)),
                  waits=bwait(pu, [Q + "lu"]), mark=R + "mmu", sem="s_mm")
                A("v", (lambda m=m, pu=pu: (lambda v: v.tensor_tensor(
                    ge_v[:, m], ge_v[:, m], [psA, psB][pu][:, 0:TC], ALU.mult)))(),
                  waits=[R + "mmu", R + "evg"], mark=R + "gu", sem="s_ve")
                bankq[pu] = R + "gu"
                ti += 2
            gu_wts = [Q + "m3_gu", P + "ldw8"]
            ei2 = c * 8 + e - 2
            if ei2 >= 0:
                gu_wts.append(f"c{ei2//8}_e{ei2%8}_dnlast")
            A("v", (lambda e=e, d=d: (lambda v: v.tensor_tensor(
                gu16_v[:, d], ge_v,
                w8bc_v[:, e].unsqueeze(1).broadcast_to([128, 4, TC]), ALU.mult)))(),
              waits=gu_wts, mark=Q + "guTT", sem="s_ve")
            for m in range(8):
                R = Q + f"d{m}_"
                pp = ti % 2
                A("t", kmm([psA, psB][pp], wdv, 4, gu16_v[:, d], slice(m * 128, (m + 1) * 128)),
                  waits=bwait(pp, [Q + "guTT", Q + "ld"] +
                              ([P + f"sd{m}_ev"] if e == 0 else
                               [P + f"e{e-1}_d{m}_add"])),
                  mark=R + "mm", sem="s_mm")
                A("v", (lambda m=m, pp=pp: (lambda v: v.tensor_tensor(
                    ysb_v[:, m], ysb_v[:, m], [psA, psB][pp][:, 0:TC], ALU.add)))(),
                  waits=[R + "mm"], mark=R + "add", sem="s_ve")
                bankq[pp] = R + "add"
                ti += 1
            A("v", lambda v: v.nop(), waits=[Q + "d7_add"],
              mark=Q + "dnlast", sem="s_ve")
        A("v", lambda v: v.tensor_tensor(sx_v[:], ysb_v[:], x3c_v[:], ALU.add),
          waits=[P + "e7_dnlast"] + ([f"c{c-1}_sqr", f"c{c-1}_psmm"] if c else []),
          mark=P + "resid", sem="s_ve")
        A("sc", lambda sc: sc.activation(sq[:], sx[:], AF.Square),
          waits=[P + "resid"], mark=P + "sqr", sem="s_ac")
        def stats_mm(which, src_v):
            def f(t):
                tgt = psm if which == 'm' else psq
                for k in range(8):
                    mm = t.matmul(tgt, ones[:, 0:1], src_v[:, k], start=(k == 0),
                                  stop=(k == 7))
                return mm
            return f
        A("t", stats_mm("m", sx_v), waits=[P + "resid", "ones"] +
          ([f"c{c-1}_stats"] if c else []), mark=P + "psmm", sem="s_mm")
        A("t", stats_mm("q", sq_v), waits=[P + "sqr"], mark=P + "psmq", sem="s_mm")
        A("sc", lambda sc: (sc.activation(stat[0:1, 0:TC], psm, AF.Identity,
                                          scale=cst[0:1, 7:8]),
                            sc.activation(stat[0:1, TC:2 * TC], psq, AF.Identity,
                                          scale=cst[0:1, 7:8]))[-1],
          waits=[P + "psmm", P + "psmq"], mark=P + "stats", sem="s_ac")
        A("v", lambda v: (v.tensor_tensor(stat[0:1, 4 * TC:5 * TC], stat[0:1, 0:TC],
                                          stat[0:1, 0:TC], ALU.mult),
                          v.scalar_tensor_tensor(stat[0:1, 4 * TC:5 * TC],
                                                 stat[0:1, TC:2 * TC], 1.0,
                                                 stat[0:1, 4 * TC:5 * TC], ALU.mult,
                                                 ALU.subtract))[-1],
          waits=[P + "stats"], mark=P + "var", sem="s_ve")
        A("sc", lambda sc: sc.activation(stat[0:1, 5 * TC:6 * TC], stat[0:1, 4 * TC:5 * TC],
                                         AF.Sqrt, bias=cst[0:1, 6:7]),
          waits=[P + "var"], mark=P + "sqrt", sem="s_ac")
        A("v", lambda v: v.reciprocal(stat[0:1, 3 * TC:4 * TC], stat[0:1, 5 * TC:6 * TC]),
          waits=[P + "sqrt"], mark=P + "rstd", sem="s_ve")
        def bcast_mm(t):
            t.matmul(psbc[:, 0:TC], ones[0:1, 0:1].broadcast_to([1, 128]), stat[0:1, 0:TC],
                     start=True, stop=True)
            return t.matmul(psbc[:, TC:2 * TC], ones[0:1, 0:1].broadcast_to([1, 128]),
                            stat[0:1, 3 * TC:4 * TC], start=True, stop=True)
        A("t", bcast_mm, waits=[P + "rstd"] + ([f"c{c-1}_bcev"] if c else []),
          mark=P + "bc", sem="s_mm")
        A("sc", lambda sc: (sc.copy(mbx[:], psbc[:, 0:TC]),
                            sc.copy(rbx[:], psbc[:, TC:2 * TC]))[-1],
          waits=[P + "bc"], mark=P + "bcev", sem="s_ac")
        def xhat(v):
            for k in range(8):
                v.tensor_tensor(ost_v[:, k], sx_v[:, k], mbx[:], ALU.subtract)
                last = v.tensor_tensor(ost_v[:, k], ost_v[:, k], rbx[:], ALU.mult)
            return last
        A("v", xhat, waits=[P + "bcev"] + ([f"c{c-1}_st"] if c else []),
          mark=P + "xhat", sem="s_ve")
        A("sy", (lambda sl=sl: (lambda sy: sy.dma_start(
            out=x4_o[:, sl].rearrange("(r p) t -> p r t", p=128), in_=ost_v)))(),
          waits=[P + "xhat"], mark=P + "st", sem="s_st", step=16)

    def em_g(g):
        S.emit_for("g", g, sems)

    def em_t(t):
        S.emit_for("t", t, sems)

    def em_sc(sc):
        S.emit_for("sc", sc, sems)

    def em_v(v):
        S.emit_for("v", v, sems)

    def em_sy(sy):
        S.emit_for("sy", sy, sems)
        sy.wait_ge(sems["s_st"], S.final("s_st"))

    finals = {"ld": S.final("s_ld"), "mm": S.final("s_mm"), "ac": S.final("s_ac"),
              "ve": S.final("s_ve"), "st": S.final("s_st")}
    return {"g": em_g, "t": em_t, "sc": em_sc, "v": em_v, "sy": em_sy}, finals


def phase_compact(nc, st, d, sm, tag, triples=()):
    s_ld, s_ve, s_ac, s_st = sm["ld"], sm["ve"], sm["ac"], sm["st"]
    CT = 2048
    msk = st.enter_context(nc.sbuf_tensor(tag + "msk", [128, 2], F32))
    sbA = st.enter_context(nc.sbuf_tensor(tag + "sbA", [128, 2 * CT], F32))
    sbB = st.enter_context(nc.sbuf_tensor(tag + "sbB", [128, 2 * CT], F32))
    sbO = st.enter_context(nc.sbuf_tensor(tag + "sbO", [128, 2 * CT], F32))
    msk_h = d["msk_d"]
    tiles = []
    for (Aap, Bap, Oap) in triples:
        R, C = Oap.shape
        ct = min(CT, C)
        assert R % 128 == 0 and C % ct == 0, (R, C)
        for r in range(R // 128):
            for cc in range(C // ct):
                rs = slice(r * 128, (r + 1) * 128)
                csl = slice(cc * ct, cc * ct + ct)
                tiles.append((Aap[rs, csl], Bap[rs, csl], Oap[rs, csl], ct))
    N = len(tiles)

    def em_g(g):
        g.dma_start(out=msk[:], in_=msk_h).then_inc(s_ld, 16)
        g.wait_ge(s_ld, 16)
        for i, (a, b, o, w) in enumerate(tiles):
            db = i % 2
            if i >= 2:
                g.wait_ge(s_ve, i - 1)
            g.dma_start(out=sbA[:, db * CT:db * CT + w], in_=a).then_inc(s_ld, 16)
            g.dma_start(out=sbB[:, db * CT:db * CT + w], in_=b).then_inc(s_ld, 16)

    def em_sc(sc):
        for i, (a, b, o, w) in enumerate(tiles):
            db = i % 2
            sc.wait_ge(s_ld, 16 * (2 + 2 * i))
            if i >= 2:
                sc.wait_ge(s_st, 16 * (i - 1))
            sc.activation(sbO[:, db * CT:db * CT + w], sbA[:, db * CT:db * CT + w],
                          AF.Identity, scale=msk[:, 0:1]).then_inc(s_ac, 1)

    def em_v(v):
        for i, (a, b, o, w) in enumerate(tiles):
            db = i % 2
            v.wait_ge(s_ac, i + 1)
            v.wait_ge(s_ld, 16 * (3 + 2 * i))
            v.scalar_tensor_tensor(sbO[:, db * CT:db * CT + w],
                                   sbB[:, db * CT:db * CT + w], msk[:, 1:2],
                                   sbO[:, db * CT:db * CT + w],
                                   ALU.mult, ALU.add).then_inc(s_ve, 1)

    def em_sy(sy):
        for i, (a, b, o, w) in enumerate(tiles):
            sy.wait_ge(s_ve, i + 1)
            sy.dma_start(out=o, in_=sbO[:, (i % 2) * CT:(i % 2) * CT + w]
                         ).then_inc(s_st, 16)
        sy.wait_ge(s_st, 16 * N)

    finals = {"ld": 16 * (1 + 2 * N), "ac": N, "ve": N, "st": 16 * N}
    return {"g": em_g, "sc": em_sc, "v": em_v, "sy": em_sy}, finals


PAIRS = [[0, 1], [2, 3], [4, 5], [6, 7]]


def build_fused():
    nc = bass.Bass("TRN2", target_bir_lowering=False)
    dh = {}
    d = {}

    def decl(name, shape, kind=None, dt=F32):
        h = (nc.dram_tensor(name, shape, dt, kind=kind) if kind
             else nc.dram_tensor(name, shape, dt))
        dh[name] = h
        d[name] = h.ap()

    for name, shape in [
        ("xT", [SEQ // 256, 128, 8 * 256]), ("w_in_x", [128, 8 * D_INNER]),
        ("w_in_z", [128, 8 * D_HALF]), ("wcv_d", [128, 64]), ("bcv_d", [128, 16]),
        ("wxp_d", [128, 16 * 96]), ("wdt_d", [DT_RANK, D_HALF]), ("bdt_d", [128, 8]),
        ("cst_d", [128, 8]), ("msk_d", [128, 2]),
        ("alog", [128, 128]), ("dd", [128, 8]),
        ("w_out", [128, 16 * 1024]), ("xT_c", [D_MODEL, TOK_PC]),
        ("w1T", [2, 128, 8 * 2048]), ("w2T", [2, 128, 8 * 2048]),
        ("h1T", [8, 128, 32 * 512]), ("h2T", [8, 128, 32 * 512]),
        ("w3T", [2, 128, 32 * 512]),
        ("s1", [128, 32]), ("s2", [128, 32]), ("sh1", [128, 32]),
        ("sh2", [128, 32]), ("s3", [128, 8]),
        ("w_ret", [128, 8 * 1024]), ("b_ret", [128, 8]),
        ("W1", [2, 128, 8 * 2048]), ("W2", [2, 128, 32 * 512]), ("b1", [128, 32]),
        ("b2", [128, 8]), ("g1", [128, 8]), ("be1", [128, 8]),
        ("g2", [128, 8]), ("be2", [128, 8]),
        ("gate_w_d", [128, 64]), ("gate_biasR", [128, 8]),
        ("w_gate", [8, 128, 8 * 512]), ("w_up", [8, 128, 8 * 512]),
        ("w_down", [8, 128, 4 * 1024]),
        ("sw_gate", [128, 8 * 1024]), ("sw_up", [128, 8 * 1024]),
        ("sw_down", [128, 8 * 1024]),
        ("ml4_d", [128, 16]), ("ml8_d", [128, 64]), ("idf_d", [128, 128]),
    ]:
        decl(name, shape, "ExternalInput")

    for name, shape in [
        ("xs_d", [D_HALF, SEQ]), ("dt_d", [D_HALF, SEQ]), ("bc_d", [96, SEQ]),
        ("zs_d", [D_HALF, SEQ]), ("yz_d", [D_HALF, SEQ]),
        ("yzag_d", [D_INNER, SEQ]), ("yzT_d", [D_INNER, TOK_PC]),
        ("x1_d", [D_MODEL, TOK_PC]), ("x2_d", [D_MODEL, TOK_PC]),
        ("sig_d", [D_MODEL, TOK_PC]), ("sigag_d", [2 * D_MODEL, TOK_PC]),
        ("sigb_d", [512, SEQ]), ("ret_d", [512, SEQ]),
        ("retag_d", [D_MODEL, SEQ]), ("retc_d", [D_MODEL, TOK_PC]),
        ("x3_d", [D_MODEL, TOK_PC]), ("w8d", [8, 256]),
    ]:
        decl(name, shape)

    for name, shape in [
        ("w1b", [2, 128, 8 * 2048]), ("w2b", [2, 128, 8 * 2048]),
        ("h1b", [8, 128, 32 * 512]), ("h2b", [8, 128, 32 * 512]),
        ("w3b", [2, 128, 32 * 512]),
        ("W1b", [2, 128, 8 * 2048]), ("W2b", [2, 128, 32 * 512]),
        ("wgb", [8, 128, 8 * 512]), ("wub", [8, 128, 8 * 512]),
        ("wdb", [8, 128, 4 * 1024]),
    ]:
        decl(name, shape, dt=BF16)

    decl("x4", [D_MODEL, TOK_PC], "ExternalOutput")

    conv = []
    for s, t in [("W1", "W1b"), ("W2", "W2b"),
                 ("w1T", "w1b"), ("w2T", "w2b"), ("w3T", "w3b"),
                 ("h1T", "h1b"), ("h2T", "h2b")]:
        ns = d[s].shape[0]
        n = d[s].shape[-1] // 4096
        for q in range(ns):
            for i in range(n):
                conv.append((d[s][q][:, i * 4096:(i + 1) * 4096],
                             d[t][q][:, i * 4096:(i + 1) * 4096]))
    for s, t in [("w_gate", "wgb"), ("w_up", "wub"), ("w_down", "wdb")]:
        for e in range(8):
            conv.append((d[s][e], d[t][e]))

    st = contextlib.ExitStack()
    s_cc = st.enter_context(nc.semaphore("s_cc"))
    recs = []

    def add_phase(tag, fn, **kw):
        sm = {n: st.enter_context(nc.semaphore(f"{tag}_{n}"))
              for n in ("ld", "ve", "mm", "ac", "st", "xa", "xb")}
        pst = contextlib.ExitStack()
        em, fin = fn(nc, pst, d, sm, tag + "_", **kw)
        pst.close()
        recs.append((em, {sm[k]: v for k, v in fin.items()}))

    ccn = [0]

    def add_cc(in_name, out_name):
        ccn[0] += 1
        n = ccn[0]
        in_h, out_h = dh[in_name], dh[out_name]

        def em_g(g, in_h=in_h, out_h=out_h, n=n):
            g.collective_compute(
                "AllGather", mybir.AluOpType.bypass, replica_groups=PAIRS,
                ins=[in_h.ap().opt()], outs=[out_h.ap().opt()]).then_inc(s_cc, 1)
            g.wait_ge(s_cc, n)
        recs.append(({"g": em_g}, {s_cc: n}))

    add_phase("A", phase_pre)
    add_phase("B", phase_scan, conv=conv)
    add_cc("yz_d", "yzag_d")
    add_phase("CPy", phase_compact, triples=[
        (d["yzag_d"][:, 0:TOK_PC], d["yzag_d"][:, TOK_PC:SEQ], d["yzT_d"])])
    add_phase("C", phase_outln)
    add_phase("D", phase_ffn0)
    add_phase("E", phase_sig)
    add_cc("sig_d", "sigag_d")
    add_phase("CPs", phase_compact, triples=[
        (d["sigag_d"][0:512, :], d["sigag_d"][512:1024, :], d["sigb_d"][:, 0:TOK_PC]),
        (d["sigag_d"][1024:1536, :], d["sigag_d"][1536:2048, :],
         d["sigb_d"][:, TOK_PC:SEQ])])
    add_phase("F", phase_ema)
    add_cc("ret_d", "retag_d")
    add_phase("CPr", phase_compact, triples=[
        (d["retag_d"][:, 0:TOK_PC], d["retag_d"][:, TOK_PC:SEQ], d["retc_d"])])
    add_phase("G", phase_l1rest)
    add_phase("H", phase_moe)

    block = st.enter_context(nc.Block())

    def mk(name):
        def run(eng):
            prev = None
            for em, fin in recs:
                if prev:
                    for sem, v in prev.items():
                        if v:
                            eng.wait_ge(sem, v)
                f = em.get(name)
                if f:
                    f(eng)
                prev = fin
        return run

    block.tensor(mk("t"))
    block.scalar(mk("sc"))
    block.vector(mk("v"))
    block.gpsimd(mk("g"))
    block.sync(mk("sy"))
    st.close()
    return nc


def _arr(s, m):
    return np.ascontiguousarray(np.asarray(s, np.float32).reshape(m, 128).T)


def host_maps(inputs):
    inputs = {k: np.asarray(v) for k, v in inputs.items()}
    consts = _consts_host()
    cores = list(range(NCORES))

    preM = host_pre_inputs(inputs, consts)
    wD = host_ffn0_weights(inputs, consts)
    scanM = host_scan_inputs(inputs)
    x = inputs["L0_attn_x"]
    ml4 = np.zeros((4, 4), np.float32)
    for j in range(4):
        ml4[j, :j] = 1.0
    ml8 = np.zeros((8, 8), np.float32)
    for e in range(8):
        ml8[e, :e] = 1.0
    shared = {
        "w_out": pack_kpm(inputs["L0_attn_W_out"], 16),
        "w_ret": pack_kpm(inputs["L1_attn_W_ret"], 8),
        "b_ret": _arr(inputs["L1_attn_b_ret"], 8),
        "W1": np.stack([pack_kpm(np.asarray(inputs["L1_attn_W1"])[:, i * 2048:(i + 1) * 2048], 8)
                        for i in range(2)]),
        "W2": np.stack([pack_kpm(np.asarray(inputs["L1_attn_W2"])[:, i * 512:(i + 1) * 512], 32)
                        for i in range(2)]),
        "b1": _arr(inputs["L1_attn_b1"], 32), "b2": _arr(inputs["L1_attn_b2"], 8),
        "g1": _arr(inputs["L1_attn_ln1_scale"], 8),
        "be1": _arr(inputs["L1_attn_ln1_bias"], 8),
        "g2": _arr(inputs["L1_attn_ln2_scale"], 8),
        "be2": _arr(inputs["L1_attn_ln2_bias"], 8),
        "gate_w_d": pack_kpm(inputs["L1_ffn_gate_w"], 8),
        "gate_biasR": np.ascontiguousarray(
            np.broadcast_to(np.asarray(inputs["L1_ffn_gate_bias"], np.float32), (128, 8))),
        "w_gate": np.stack([pack_kpm(w, 8) for w in np.asarray(inputs["L1_ffn_w_gate"])]),
        "w_up": np.stack([pack_kpm(w, 8) for w in np.asarray(inputs["L1_ffn_w_up"])]),
        "w_down": np.stack([pack_kpm(w, 4) for w in np.asarray(inputs["L1_ffn_w_down"])]),
        "sw_gate": pack_kpm(inputs["L1_ffn_sw_gate"], 8),
        "sw_up": pack_kpm(inputs["L1_ffn_sw_up"], 8),
        "sw_down": pack_kpm(inputs["L1_ffn_sw_down"], 8),
        "ml4_d": np.ascontiguousarray(np.broadcast_to(ml4.reshape(-1), (128, 16))),
        "ml8_d": np.ascontiguousarray(np.broadcast_to(ml8.reshape(-1), (128, 64))),
        "idf_d": np.eye(128, dtype=np.float32),
    }
    maps = []
    for core in cores:
        b, h = divmod(core, 2)
        sl = slice(h * TOK_PC, (h + 1) * TOK_PC)
        msk = np.zeros((128, 2), np.float32)
        msk[:, 0] = 1.0 - h
        msk[:, 1] = float(h)
        m = dict(preM[core])
        m.update(wD)
        m.update(shared)
        m.update({
            "alog": scanM[core][0], "dd": scanM[core][1],
            "xT_c": np.ascontiguousarray(x[b].T[:, sl]),
            "msk_d": msk,
        })
        maps.append(m)
    return maps


def kernel(**inputs):
    inputs = {k: np.asarray(v) for k, v in inputs.items()}
    cores = list(range(NCORES))
    nc = build_fused()
    maps = host_maps(inputs)
    res = run_bass_kernel_spmd(nc, maps, core_ids=cores).results

    out = np.empty((BATCH, SEQ, D_MODEL), np.float32)
    for core in cores:
        b, h = divmod(core, 2)
        out[b, h * TOK_PC:(h + 1) * TOK_PC, :] = res[core]["x4"].T
    return out
